# version 07
# baseline (speedup 1.0000x reference)
import contextlib
import numpy as np
import ml_dtypes
import concourse.bass as bass
import concourse.mybir as mybir
from concourse.bass_utils import run_bass_kernel_spmd
from concourse.bass_core_rust import IndirectOffsetOnAxis

F32 = mybir.dt.float32
BF16 = mybir.dt.bfloat16
I32 = mybir.dt.int32
U32 = mybir.dt.uint32
OP = mybir.AluOpType
AX = mybir.AxisListType.X
AF = mybir.ActivationFunctionType

MAGIC = float(np.float32(1.5 * 2**23))
NCORES = 8
T = 16384
TC = 2048
NT = TC // 128
H = 1024
HC = H // 128
I2 = 4096
II = 2048
IIC = II // 128
E = 8
CAP = 2048
EF = 16
FCAP = 512

MILESTONE = 4
M1CUT = 99
M2CUT = 99
SKIP_WR = False
SKIP_WT = False
DEBUG_TAPS = ()
DELAY_AG = 0


class Prog:
    ENGINES = ("sync", "vector", "scalar", "tensor", "gpsimd")

    def __init__(self):
        self.ops = []
        self.by_engine = {e: [] for e in self.ENGINES}

    def add(self, engine, fn, deps=(), dma=False, tag="", coll=False, dsem=None):
        deps = [d for d in deps if d is not None]
        op = {"e": engine, "fn": fn, "deps": deps, "dma": dma, "coll": coll,
              "id": len(self.ops), "need_inc": False, "tag": tag,
              "dsem": (dsem or "d") if dma else None}
        self.ops.append(op)
        self.by_engine[engine].append(op)
        return op["id"]

    def check_schedule(self):
        bykey = {}
        for op in self.ops:
            if op["dma"] and op["dsem"].endswith("!"):
                bykey.setdefault((op["e"], op["dsem"]), []).append(op["id"])
        exdeps = []
        for op in self.ops:
            dd = set()
            for d in op["deps"]:
                dop = self.ops[d]
                if dop["dma"] and dop["dsem"].endswith("!"):
                    dd.update(bykey[(dop["e"], dop["dsem"])])
                else:
                    dd.add(d)
            exdeps.append(dd)
        self._exdeps = exdeps
        pos = {e: 0 for e in self.ENGINES}
        done = [False] * len(self.ops)
        progress = True
        while progress:
            progress = False
            for e in self.ENGINES:
                while pos[e] < len(self.by_engine[e]):
                    op = self.by_engine[e][pos[e]]
                    if all(done[d] for d in self._exdeps[op["id"]]):
                        done[op["id"]] = True
                        pos[e] += 1
                        progress = True
                    else:
                        break
        stuck = {e: pos[e] for e in self.ENGINES if pos[e] < len(self.by_engine[e])}
        if stuck:
            msg = []
            for e, p in stuck.items():
                op = self.by_engine[e][p]
                blocking = [d for d in self._exdeps[op["id"]] if not done[d]]
                bl = [(self.ops[d]["e"], self.ops[d]["tag"], d) for d in blocking[:4]]
                msg.append(f"engine {e} stuck at op id={op['id']} tag={op['tag']!r} waiting {bl}")
            raise RuntimeError("SCHEDULE DEADLOCK:\n" + "\n".join(msg))

    def emit(self, nc, block, final_ids):
        self.check_schedule()
        for op in self.ops:
            if op["dma"] or op["coll"]:
                op["need_inc"] = True
            for d in op["deps"]:
                self.ops[d]["need_inc"] = True
        for d in final_ids:
            self.ops[d]["need_inc"] = True
        ncoll = 0
        dkey_count = {}
        for e in self.ENGINES:
            cexec = 0
            for op in self.by_engine[e]:
                if op["coll"]:
                    op["semval"] = 1
                    op["collidx"] = ncoll
                    ncoll += 1
                elif op["dma"]:
                    key = (e, op["dsem"])
                    dkey_count[key] = dkey_count.get(key, 0) + 16
                    op["semval"] = dkey_count[key]
                elif op["need_inc"]:
                    cexec += 1
                    op["semval"] = cexec
                else:
                    op["semval"] = None
        self.ncoll = ncoll
        self.dkey_total = dict(dkey_count)
        es = self._es
        sems = {}
        for e in self.ENGINES:
            sems[(e, "x")] = es.enter_context(nc.semaphore(f"sx_{e}"))
        for key in dkey_count:
            sems[key] = es.enter_context(nc.semaphore(f"sd_{key[0]}_{key[1].rstrip(chr(33))}"))
        for i in range(self.ncoll):
            sems[("coll", i)] = es.enter_context(nc.semaphore(f"sc_{i}"))
        self.sems = sems

        prog = self

        def dep_key_val(dop):
            if dop["coll"]:
                return ("coll", dop["collidx"]), 1
            if dop["dma"]:
                key = (dop["e"], dop["dsem"])
                if dop["dsem"].endswith("!"):
                    return key, prog.dkey_total[key]
                return key, dop["semval"]
            return (dop["e"], "x"), dop["semval"]

        def run_engine(eng_obj, ename):
            known = {}
            for op in prog.by_engine[ename]:
                needs = {}
                for d in op["deps"]:
                    key, v = dep_key_val(prog.ops[d])
                    if needs.get(key, 0) < v:
                        needs[key] = v
                for key, v in sorted(needs.items(), key=lambda kv: str(kv[0])):
                    if known.get(key, 0) >= v:
                        continue
                    eng_obj.wait_ge(sems[key], v)
                    known[key] = v
                ins = op["fn"](eng_obj)
                if op["coll"]:
                    ins.then_inc(sems[("coll", op["collidx"])], 1)
                elif op["dma"]:
                    ins.then_inc(sems[(op["e"], op["dsem"])], 16)
                elif op["need_inc"]:
                    ins.then_inc(sems[(op["e"], "x")], 1)

        with nc.Block() as block2:
            block2.sync(lambda s_: run_engine(s_, "sync"))
            block2.vector(lambda v_: run_engine(v_, "vector"))
            block2.scalar(lambda s_: run_engine(s_, "scalar"))
            block2.tensor(lambda t_: run_engine(t_, "tensor"))

            def gps(g_):
                run_engine(g_, "gpsimd")
                known = {}
                for d in final_ids:
                    key, v = dep_key_val(prog.ops[d])
                    if known.get(key, 0) < v:
                        known[key] = v
                for key, v in sorted(known.items(), key=lambda kv: str(kv[0])):
                    g_.wait_ge(self.sems[key], v)

            block2.gpsimd(gps)


def build_program(milestone=MILESTONE, debug_taps=DEBUG_TAPS):
    nc = bass.Bass("TRN2", target_bir_lowering=False)
    P = Prog()
    es = contextlib.ExitStack()
    P._es = es

    def sb(name, shape, dtype):
        return es.enter_context(nc.sbuf_tensor(name, shape, dtype))

    def psum(name, shape, dtype=F32):
        return es.enter_context(nc.psum_tensor(name, shape, dtype))

    x_sl = nc.dram_tensor("x_sl", [TC, H], F32, kind="ExternalInput").ap()
    gnw_d = nc.dram_tensor("gnw", [128, H], F32, kind="ExternalInput").ap()
    ginw_d = nc.dram_tensor("ginw", [128, H], F32, kind="ExternalInput").ap()
    gwT_d = nc.dram_tensor("gwT", [H, E], F32, kind="ExternalInput").ap()
    eyebf_d = nc.dram_tensor("eyebf", [128, 128], BF16, kind="ExternalInput").ap()
    eyef_d = nc.dram_tensor("eyef", [128, 128], F32, kind="ExternalInput").ap()
    iota8_d = nc.dram_tensor("iota8", [128, EF], F32, kind="ExternalInput").ap()
    iota128_d = nc.dram_tensor("iota128", [128, 1], F32, kind="ExternalInput").ap()
    myhot_d = nc.dram_tensor("myhot", [8, 1], F32, kind="ExternalInput").ap()
    myhotr_d = nc.dram_tensor("myhotr", [1, 8], F32, kind="ExternalInput").ap()
    gpn_d = nc.dram_tensor("gpn", [128, H], F32, kind="ExternalInput").ap()
    dpn_d = nc.dram_tensor("dpn", [128, II], F32, kind="ExternalInput").ap()
    gpwT_d = nc.dram_tensor("gpwT", [H, I2], F32, kind="ExternalInput").ap()
    dpwT_d = nc.dram_tensor("dpwT", [II, H], F32, kind="ExternalInput").ap()

    out_d = nc.dram_tensor("out", [TC, H], F32, kind="ExternalOutput").ap()
    w1s_d = nc.dram_tensor("w1s", [EF, 128, 512], F32, kind="ExternalInput").ap()
    w2s_d = nc.dram_tensor("w2s", [EF, 65, H], F32, kind="ExternalInput").ap()
    b1f_d = nc.dram_tensor("b1f", [EF, 64], F32, kind="ExternalInput").ap()
    wr_d = nc.dram_tensor("wrs", [128, 8, EF], F32, kind="ExternalInput").ap()
    brr_d = nc.dram_tensor("brr", [128, EF], F32, kind="ExternalInput").ap()
    iotac16_d = nc.dram_tensor("iotac16", [EF, 1], F32, kind="ExternalInput").ap()
    rb128_d = nc.dram_tensor("rb128", [EF, 48], F32, kind="ExternalInput").ap()
    rbp_d = nc.dram_tensor("rbp", [128, 48], F32, kind="ExternalInput").ap()
    y_d = nc.dram_tensor("y_d", [2049, H], F32).ap()
    s_d = nc.dram_tensor("s_d", [TC, H], F32).ap()
    tb16_d = nc.dram_tensor("tb16_d", [2048, 17], F32).ap()
    kofT_d = nc.dram_tensor("kofT_d", [EF, 128], F32).ap()
    BLKB = TC * E + E * TC
    ag1a_in = nc.dram_tensor("ag1a_in", [TC, H], F32).ap()
    ag1a_out = nc.dram_tensor("ag1a_out", [T, H], F32).ap()
    ag1b_in_f = nc.dram_tensor("ag1b_in", [BLKB], F32).ap()
    ag1b_out_f = nc.dram_tensor("ag1b_out", [NCORES, BLKB], F32).ap()
    ag1w_in = ag1b_in_f[0:TC * E].rearrange("(r c) -> r c", c=E)
    ag1b_in = ag1b_in_f[TC * E:BLKB].rearrange("(e t) -> e t", t=TC)
    tbl_d = nc.dram_tensor("tbl_d", [128, 132], F32).ap()
    q1s_d = nc.dram_tensor("q1s_d", [CAP, H], BF16).ap()
    q2s_d = nc.dram_tensor("q2s_d", [CAP, II], BF16).ap()
    ag2_in = nc.dram_tensor("ag2_in", [CAP, H], F32).ap()
    ag2_out = nc.dram_tensor("ag2_out", [T, H], F32).ap()
    gidx_d = nc.dram_tensor("gidx_d", [4, T], F32).ap()
    tapd = {}
    for name, shape in [("tap_ln", [TC, H]), ("tap_logits", [TC, E]),
                        ("tap_wsel", [TC, E]), ("tap_top", [TC, 4]),
                        ("tap_tok", [128, 16]), ("tap_gidx", [4, T]),
                        ("tap_s", [TC, H]), ("tap_ag2", [CAP, H]),
                        ("tap_xin", [CAP, H]), ("tap_tok2", [128, 16]),
                        ("tap_q1", [CAP, H]), ("tap_q2", [CAP, II]),
                        ("tap_ag1", [T, H]), ("tap_gw", [128, 64]),
                        ("tap_r1", [TC, H]), ("tap_r2", [TC, H]),
                        ("tap_ag2o", [T, H]), ("tap_wtk", [128, 16]),
                        ("tap_csc", [128, 16]), ("tap_fl2", [TC, EF]),
                        ("tap_fslot", [128, 192]), ("tap_y", [2049, H])]:
        if name in debug_taps:
            tapd[name] = nc.dram_tensor(name, shape, F32, kind="ExternalOutput").ap()

    ARENA = 153 * 1024
    _ar0, _ar1 = nc.bump_sbuf(ARENA)

    def A(name, shape, dtype, off):
        return nc.alloc_sbuf_tensor_at(name, shape, dtype, offset=_ar0 + off)

    ln = A("ln", [128, NT, H], F32, 0)
    zz = A("zz", [128, NT, H], F32, 64 * 1024)
    xrot = A("xrot", [128, 2, H], F32, 128 * 1024)
    qbrot = A("qbrot", [128, 2, H], BF16, 136 * 1024)
    qtrot = A("qtrot", [128, 2, HC, 128], BF16, 140 * 1024)
    glog8 = A("glog8", [8, TC], F32, 144 * 1024)
    wselT = A("wselT", [E, TC], F32, 144 * 1024)
    gp_wq = A("gp_wq", [128, HC, I2], BF16, 0)
    dp_wq = A("dp_wq", [128, IIC, H], BF16, 64 * 1024)
    wrot = A("wrot", [128, 2, 4096], F32, 96 * 1024)
    wcol8 = A("wcol8", [128, E, 128], F32, 64 * 1024)
    R8 = A("R8", [128, E, 128], F32, 68 * 1024)
    keep8 = A("keep8", [128, E, 128], F32, 72 * 1024)
    pctmp = A("pctmp", [128, 2, 128], F32, 76 * 1024)
    natmp = A("natmp", [128, 128], F32, 77 * 1024)
    getmp = A("getmp", [128, 128], F32, 77 * 1024 + 512)
    gAB = A("gAB", [128, 4, 128], F32, 78 * 1024)
    koffB = A("koffB", [128, 128], F32, 80 * 1024)
    Rmy = A("Rmy", [128, 128], F32, 80 * 1024 + 512)
    g2buf = A("g2buf", [128, 16, 132], F32, 81 * 1024)
    xinr = A("xinr", [128, 2, H], F32, 96 * 1024)
    p1r = A("p1r", [128, 1, H], F32, 104 * 1024)
    q1r = A("q1r", [128, 4, H], BF16, 108 * 1024)
    q1T = A("q1T", [128, HC, 512], BF16, 116 * 1024)
    ubuf = A("ubuf", [128, II], F32, 124 * 1024)
    sgb = A("sgb", [128, 512], F32, 132 * 1024)
    q2r = A("q2r", [128, 2, II], BF16, 134 * 1024)
    q2T = A("q2T", [128, IIC, 256], BF16, 142 * 1024)
    ctb = ubuf
    lnrot = A("lnrot", [128, 1, H], F32, 128 * 1024)
    lnTb = A("lnTb", [128, 2, 128], F32, 132 * 1024)
    xg = A("xg", [128, 2, H], F32, 64 * 1024)
    xgT = A("xgT", [128, 2, 8, 128], F32, 72 * 1024)
    w1b = A("w1b", [128, 2, 512], F32, 80 * 1024)
    ybuf = A("ybuf", [128, 2, H], F32, 84 * 1024)
    w2b = A("w2b", [65, 2, H], F32, 128 * 1024)
    r1g = A("r1g", [128, 2, H], F32, 96 * 1024)
    r2g = A("r2g", [128, 2, H], F32, 104 * 1024)
    lnr = A("lnr", [128, 2, H], F32, 112 * 1024)
    stile = A("stile", [128, 2, H], F32, 120 * 1024)
    gidxw = A("gidxw", [128, 16, 4], F32, 150 * 1024)
    gidxi = A("gidxi", [128, 2, 16], I32, 151 * 1024)
    gnw = sb("gnw_s", [128, H], F32)
    ginw = sb("ginw_s", [128, H], F32)
    gpn = sb("gpn_s", [128, H], F32)
    dpn = sb("dpn_s", [128, II], F32)
    radj = sb("radj", [128, 16], F32)
    tok2i = sb("tok2i", [128, 16], I32)
    wtr = sb("wtr", [128, 2, E], F32)
    wsum = sb("wsum", [128, 4], F32)
    wqsc = sb("wqsc", [128, 4], F32)
    exsc = sb("exsc", [128, 64], F32)
    wtk_dbg = sb("wtk_dbg", [128, 16], F32)
    csc_dbg = sb("csc_dbg", [128, 16], F32)
    l2t = sb("l2t", [128, NT, EF], F32)
    wrouter = sb("wrouter", [128, 8, EF], F32)
    brrep = sb("brrep", [128, EF], F32)
    e1f16 = sb("e1f16", [128, NT], F32)
    e2f16 = sb("e2f16", [128, NT], F32)
    i1u16 = sb("i1u16", [128, 8], mybir.dt.uint32)
    wx16 = sb("wx16", [128, 8], F32)
    mx16 = sb("mx16", [128, EF], F32)
    msk16 = sb("msk16", [128, EF], F32)
    me16 = sb("me16", [128, EF, NT], F32)
    R16 = sb("R16", [128, EF, NT], F32)
    ktot16 = sb("ktot16", [128, EF], F32)
    koff16 = sb("koff16", [128, EF], F32)
    ktT = sb("ktT", [EF, 128], F32)
    kscT = sb("kscT", [EF, 128], F32)
    koffTx16 = sb("koffTx16", [EF, 128], F32)
    cntT16 = sb("cntT16", [EF, 1], F32)
    pcT16 = sb("pcT16", [EF, 1], F32)
    pc1r = sb("pc1r", [1, EF], F32)
    sc1r = sb("sc1r", [1, EF], F32)
    off1r = sb("off1r", [1, EF], F32)
    offT16 = sb("offT16", [EF, 1], F32)
    endT16 = sb("endT16", [EF, 1], F32)
    erbhot = sb("erbhot", [EF, 48], F32)
    eh2 = sb("eh2", [EF, 48], F32)
    erow = sb("erow", [1, 48], F32)
    cntrow = sb("cntrow", [1, 48], F32)
    offrow = sb("offrow", [1, 48], F32)
    erbB = sb("erbB", [128, 48], F32)
    cntB = sb("cntB", [128, 48], F32)
    offB = sb("offB", [128, 48], F32)
    rloc48 = sb("rloc48", [128, 48], F32)
    validf = sb("validf", [128, 48], F32)
    rlT = sb("rlT", [48, 128], F32)
    pzT48 = sb("pzT48", [48, 128], F32)
    pz48 = sb("pz48", [128, 48], F32)
    fv48 = sb("fv48", [128, 48], F32)
    j48 = sb("j48", [128, 48], F32)
    tok48 = sb("tok48", [128, 48], F32)
    sct48 = sb("sct48", [128, 48], F32)
    gidx48 = sb("gidx48", [128, 48], F32)
    idx48 = sb("idx48", [128, 48], F32)
    idx48i = sb("idx48i", [128, 48], I32)
    w1i48 = sb("w1i48", [128, 48], I32)
    ebi128 = sb("ebi128", [128, 48], I32)
    b1B = sb("b1B", [128, 2, 64], F32)
    w2i48 = sb("w2i48", [65, 48], I32)
    gidx48i = sb("gidx48i", [128, 48], I32)
    scti = sb("scti", [128, 48], I32)
    er48 = sb("er48", [48, 1], F32)
    erb48i = sb("erb48i", [48, 1], I32)
    erbi1 = sb("erbi1", [1, 48], I32)
    koffg = sb("koffg", [48, 128], F32)
    dump48 = sb("dump48", [48, 128], F32)
    dump16f = sb("dump16f", [128, EF], F32)
    dumpa16 = sb("dumpa16", [128, NT], F32)
    tbs16 = sb("tbs16", [128, 2, 17], F32)
    g6b = sb("g6b", [128, 2, 17], F32)
    rb128s = sb("rb128s", [EF, 48], F32)
    rbps = sb("rbps", [128, 48], F32)
    iotac16 = sb("iotac16_s", [EF, 1], F32)
    hr16 = sb("hr16", [128, 2, 64], F32)
    hrT = sb("hrT", [65, 2, 128], F32)
    gwT = sb("gwT_s", [128, HC * E], F32)
    gwq = sb("gwq_s", [128, HC, E], BF16)
    eyebf = sb("eyebf_s", [128, 128], BF16)
    eyef = sb("eyef_s", [128, 128], F32)
    iota8 = sb("iota8_s", [128, EF], F32)
    ones8 = sb("ones8_s", [128, EF], F32)

    epsc = sb("epsc", [128, 4], F32)
    iota128 = sb("iota128_s", [128, 1], F32)
    myhot = sb("myhot_s", [8, 1], F32)
    myhotr = sb("myhotr_s", [1, 8], F32)
    mflag = sb("mflag_s", [128, 8], F32)
    ktot8 = sb("ktot8", [128, E], F32)
    koffT = sb("koffT", [8, 128], F32)
    koffTx = sb("koffTx", [8, 128], F32)
    koff8 = sb("koff8", [128, E], F32)
    koffmy = sb("koffmy", [128, 1], F32)
    koffrow = sb("koffrow", [1, 128], F32)
    tbl = sb("tbl", [128, 132], F32)
    sval = sb("sval", [128, 16], F32)
    cnts = sb("cnts", [128, 16], F32)
    pz = sb("pz", [128, 16], F32)
    pzi = sb("pzi", [128, 16], I32)
    jv = sb("jv", [128, 16], F32)
    fv = sb("fv", [128, 16], F32)
    tokf = sb("tokf", [128, 16], F32)
    toki = sb("toki", [128, 16], I32)
    dump8 = sb("dump8", [128, 128], F32)
    st_sum = sb("st_sum", [128, NT], F32)
    st_ssq = sb("st_ssq", [128, NT], F32)
    st_mean = sb("st_mean", [128, NT], F32)
    st_rstd = sb("st_rstd", [128, NT], F32)
    st_rs1 = sb("st_rs1", [128, NT], F32)
    st_rs2 = sb("st_rs2", [128, NT], F32)
    st_mg = sb("st_mg", [128, NT], F32)
    st_sg = sb("st_sg", [128, NT], F32)
    st_dqg = sb("st_dqg", [128, NT], F32)
    gsc = sb("gsc", [1, 4], F32)
    gsc_bc = sb("gsc_bc", [128, 2], F32)
    grow = sb("grow", [1, 128], F32)
    logit = sb("logit", [128, NT, E], F32)
    exs = sb("exs", [128, NT, E], F32)
    probs = sb("probs", [128, NT, E], F32)
    masked = sb("masked", [128, NT, E], F32)
    negmx = sb("negmx", [128, NT], F32)
    sumex = sb("sumex", [128, NT], F32)
    rsum = sb("rsum", [128, NT], F32)
    w1t = sb("w1t", [128, NT], F32)
    w2t = sb("w2t", [128, NT], F32)
    e1t = sb("e1t", [128, NT], F32)
    e2t = sb("e2t", [128, NT], F32)
    i1u = sb("i1u", [128, E], U32)
    i2u = sb("i2u", [128, E], U32)
    wx8 = sb("wx8", [128, E], F32)
    wsel = sb("wsel", [128, NT, E], F32)
    dump = sb("dump", [128, H], F32)

    pbk = [psum(f"pbk{i}", [128, 512], F32) for i in range(6)]
    ptrb = psum("ptrb", [128, 1024], BF16)
    ptrx = psum("ptrx", [128, 512], F32)
    ptrf = ptrx[:, 0:128]
    ptr2 = ptrx[:, 128:256]
    ptr3 = ptrx[:, 256:384]
    ptr1 = ptrf

    def D(engine, fn, deps=(), tag="", dsem=None):
        return P.add(engine, fn, deps, dma=True, tag=tag, dsem=dsem)

    def C(engine, fn, deps=(), tag=""):
        return P.add(engine, fn, deps, dma=False, tag=tag)

    ld_gnw = D("sync", lambda e: e.dma_start(out=gnw[:], in_=gnw_d), dsem="ld0!")
    ld_ginw = D("sync", lambda e: e.dma_start(out=ginw[:], in_=ginw_d), dsem="ld0!")
    ld_gw = D("sync", lambda e: e.dma_start(out=gwT[:].rearrange("p (c n) -> p c n", c=HC), in_=gwT_d.rearrange("(c p) n -> p c n", p=128)), dsem="ld0!")
    ld_eb = D("sync", lambda e: e.dma_start(out=eyebf[:], in_=eyebf_d), dsem="ld0!")
    ld_ef = D("sync", lambda e: e.dma_start(out=eyef[:], in_=eyef_d), dsem="ld0!")
    ld_io = D("sync", lambda e: e.dma_start(out=iota8[:], in_=iota8_d), dsem="ld0!")
    ones_op = C("vector", lambda e: e.memset(ones8[:], 1.0))
    eps_op0 = C("vector", lambda e: e.memset(epsc[:, 0:1], 1e-5))
    eps_op1 = C("vector", lambda e: e.memset(epsc[:, 1:2], 1e-6))
    eps_op2 = C("vector", lambda e: e.memset(epsc[:, 2:3], 1e-8))
    eps_ops = [eps_op0, eps_op1, eps_op2]
    onesrow = sb("onesrow", [1, 128], F32)
    onesrow_op = C("vector", lambda e: e.memset(onesrow[:], 1.0))

    x_tiled = x_sl.rearrange("(n p) h -> p n h", p=128)
    ld_x = []
    sub_ops = []
    ln_ops = []
    ssq_ops = []
    for t in range(NT if M1CUT >= 1 else 0):
        pace = [sub_ops[t - 2]] if t >= 2 else []
        ld = D("sync", (lambda tt: lambda e: e.dma_start(out=xrot[:, tt % 2, :], in_=x_tiled[:, tt, :]))(t), pace, dsem=f"x{t % 2}")
        ld_x.append(ld)
        r = C("vector", (lambda tt: lambda e: e.tensor_reduce(st_sum[:, tt:tt + 1], xrot[:, tt % 2, :], axis=AX, op=OP.add))(t), [ld])
        mn = C("vector", (lambda tt: lambda e: e.tensor_scalar(st_mean[:, tt:tt + 1], st_sum[:, tt:tt + 1], 1.0 / 1024.0, None, OP.mult, OP.bypass))(t), [r])
        s = C("vector", (lambda tt: lambda e: e.tensor_scalar(ln[:, tt, :], xrot[:, tt % 2, :], st_mean[:, tt:tt + 1], None, OP.subtract, OP.bypass))(t),
              [mn, ld])
        sub_ops.append(s)
    for t in range(NT if M1CUT >= 1 else 0):
        q = C("scalar", (lambda tt: lambda e: e.activation(dump[:], ln[:, tt, :], AF.Square, accum_out=st_ssq[:, tt:tt + 1]))(t), [sub_ops[t]])
        ssq_ops.append(q)
    sqrt_op = C("scalar", lambda e: e.activation(st_rstd[:], st_ssq[:], AF.Sqrt, bias=epsc[:, 0:1], scale=1.0 / 1024.0), ssq_ops + eps_ops) if M1CUT >= 1 else None
    rcp_op = C("vector", lambda e: e.reciprocal(st_rstd[:], st_rstd[:]), [sqrt_op]) if M1CUT >= 1 else None
    for t in range(NT if M1CUT >= 1 else 0):
        o = C("vector", (lambda tt: lambda e: e.tensor_scalar(ln[:, tt, :], ln[:, tt, :], st_rstd[:, tt:tt + 1], None, OP.mult, OP.bypass))(t),
              [rcp_op, sub_ops[t]])
        ln_ops.append(o)

    ssq1 = []
    wsel_ops = []
    lg_ops = []
    if M1CUT >= 2:
      gq0 = C("vector", lambda e: e.tensor_reduce(gsc_bc[:, 0:1], gwT[:], axis=AX, op=OP.add, apply_absolute_value=True), [ld_gw])
      gq1 = C("tensor", lambda e: e.transpose(ptr1[0:1, 0:128], gsc_bc[:, 0:1], eyef[:]), [gq0, ld_ef])
      gq2 = C("scalar", lambda e: e.copy(grow[:], ptr1[0:1, 0:128]), [gq1])
      gq3 = C("vector", lambda e: e.tensor_reduce(gsc[:, 0:1], grow[:], axis=AX, op=OP.add), [gq2])
      gq4 = C("vector", lambda e: e.tensor_scalar(gsc[:, 3:4], gsc[:, 0:1], 1.0 / float(H * E), 1e-5, OP.mult, OP.max), [gq3])
      gq5 = C("vector", lambda e: e.reciprocal(gsc[:, 2:3], gsc[:, 3:4]), [gq4])
      gq6a = C("tensor", lambda e: e.matmul(ptr1[:, 0:2], onesrow[:], gsc[0:1, 2:4], start=True, stop=True), [gq5, onesrow_op, gq2])
      gq6 = C("scalar", lambda e: e.copy(gsc_bc[:, 0:2], ptr1[:, 0:2]), [gq6a])
      gq7 = C("scalar", lambda e: e.activation(gwT[:], gwT[:], AF.Copy, bias=MAGIC, scale=gsc_bc[:, 0:1]), [gq6, gq0])
      gq8 = C("vector", lambda e: e.tensor_scalar(gwT[:], gwT[:], MAGIC, -1.0, OP.subtract, OP.max), [gq7])
      gq9 = C("vector", lambda e: e.tensor_scalar(gwq[:].rearrange("p c n -> p (c n)"), gwT[:], 1.0, None, OP.min, OP.bypass), [gq8])

      
      for t in range(NT):
          q = C("scalar", (lambda tt: lambda e: e.activation(dump[:], ln[:, tt, :], AF.Square, accum_out=st_ssq[:, tt:tt + 1]))(t), [ln_ops[t]])
          ssq1.append(q)
      sq1 = C("scalar", lambda e: e.activation(st_rs1[:], st_ssq[:], AF.Sqrt, bias=epsc[:, 1:2], scale=1.0 / 1024.0), ssq1 + eps_ops)
      rc1 = C("vector", lambda e: e.reciprocal(st_rs1[:], st_rs1[:]), [sq1])
      xn_ops = []
      for t in range(NT):
          o = C("vector", (lambda tt: lambda e: e.scalar_tensor_tensor(zz[:, tt, :], ln[:, tt, :], st_rs1[:, tt:tt + 1],
                                                                       gnw[:, :], OP.mult, OP.mult))(t),
                [rc1, ln_ops[t], ld_gnw])
          xn_ops.append(o)
      ssq2 = []
      for t in range(NT):
          q = C("scalar", (lambda tt: lambda e: e.activation(dump[:], zz[:, tt, :], AF.Square, accum_out=st_ssq[:, tt:tt + 1]))(t), [xn_ops[t]])
          ssq2.append(q)
      sq2 = C("scalar", lambda e: e.activation(st_rs2[:], st_ssq[:], AF.Sqrt, bias=epsc[:, 2:3], scale=1.0 / 1024.0), ssq2 + eps_ops)
      rc2 = C("vector", lambda e: e.reciprocal(st_rs2[:], st_rs2[:]), [sq2])
      zzg_ops = []
      for t in range(NT):
          o = C("vector", (lambda tt: lambda e: e.scalar_tensor_tensor(zz[:, tt, :], zz[:, tt, :], st_rs2[:, tt:tt + 1],
                                                                       ginw[:, :], OP.mult, OP.mult))(t),
                [rc2, xn_ops[t], ld_ginw])
          zzg_ops.append(o)
      mg_ops = []
      for t in range(NT):
          o = C("vector", (lambda tt: lambda e: e.tensor_reduce(st_mg[:, tt:tt + 1], zz[:, tt, :], axis=AX, op=OP.max, apply_absolute_value=True))(t),
                [zzg_ops[t]])
          mg_ops.append(o)
      mgc = C("vector", lambda e: e.tensor_scalar(st_mg[:], st_mg[:], 1e-5, None, OP.max, OP.bypass), mg_ops)
      sg0 = C("vector", lambda e: e.reciprocal(st_sg[:], st_mg[:]), [mgc])
      sg1 = C("vector", lambda e: e.tensor_scalar(st_sg[:], st_sg[:], 127.0, None, OP.mult, OP.bypass), [sg0])
      dqg = C("vector", lambda e: e.tensor_scalar(st_dqg[:], st_mg[:], 1.0 / 127.0, gsc_bc[:, 1:2], OP.mult, OP.mult), [mgc, gq6])

      mag_ops = []
      qb_ops = []
      gevac_ops = []
      tr_ops = [[None] * HC for _ in range(NT)]
      ev_ops = []
      mm_ops = []
      for t in range(NT):
          m = C("scalar", (lambda tt: lambda e: e.activation(zz[:, tt, :], zz[:, tt, :], AF.Copy, bias=MAGIC, scale=st_sg[:, tt:tt + 1]))(t),
                [sg1, zzg_ops[t], mg_ops[t]])
          mag_ops.append(m)
          pace = [tr_ops[t - 2][HC - 1]] if t >= 2 else []
          qb = C("vector", (lambda tt: lambda e: e.tensor_scalar(qbrot[:, tt % 2, :], zz[:, tt, :], MAGIC, None, OP.subtract, OP.bypass))(t),
                 [m] + pace)
          qb_ops.append(qb)
          pace_ev = [ev_ops[t - 1]] if t >= 1 else []
          for b in range(HC):
              tr = C("tensor", (lambda tt, bb: lambda e: e.transpose(ptrb[:, bb * 128:(bb + 1) * 128],
                                                                     qbrot[:, tt % 2, bb * 128:(bb + 1) * 128], eyebf[:]))(t, b),
                     [qb, ld_eb] + (pace_ev if b == 0 else [tr_ops[t][b - 1]]))
              tr_ops[t][b] = tr
          pace_mm = [mm_ops[t - 2]] if t >= 2 else []
          ev = C("scalar", (lambda tt: lambda e: e.copy(qtrot[:, tt % 2, :, :].rearrange("p c n -> p (c n)"), ptrb[:]))(t),
                 [tr_ops[t][HC - 1]] + pace_mm)
          ev_ops.append(ev)
          def mk_mm(tt):
              def f(e):
                  last = None
                  for c in range(HC):
                      last = e.matmul(pbk[4 + tt % 2][0:8, 0:128], gwq[:, c, :], qtrot[:, tt % 2, c, :],
                                      start=(c == 0), stop=(c == HC - 1))
                  return last
              return f
          pace_gb = [gevac_ops[t - 2]] if t >= 2 else []
          mm = C("tensor", mk_mm(t), [ev, gq9] + pace_gb)
          mm_ops.append(mm)
          gev = C("scalar", (lambda tt: lambda e: e.copy(glog8[:, tt * 128:(tt + 1) * 128], pbk[4 + tt % 2][0:8, 0:128]))(t), [mm])
          gevac_ops.append(gev)

      evg = gevac_ops[-1]

      lg_ops = []
      for t in range(NT):
          tr = C("tensor", (lambda tt: lambda e: e.transpose(ptrf[:, 0:8], glog8[0:8, tt * 128:(tt + 1) * 128], eyef[0:8, 0:8]))(t),
                 [gevac_ops[t]] + ([lg_ops[t - 1]] if t >= 1 else [gq6]))
          ev = C("scalar", (lambda tt: lambda e: e.activation(logit[:, tt, :], ptrf[:, 0:8], AF.Copy, scale=st_dqg[:, tt:tt + 1]))(t),
                 [tr, dqg])
          lg_ops.append(ev)

      nm_ops = [C("vector", (lambda tt: lambda e: e.tensor_reduce(negmx[:, tt:tt + 1], logit[:, tt, :], axis=AX, op=OP.max, negate=True))(t), [lg_ops[t]])
                for t in range(NT)]
      ex_ops = [C("scalar", (lambda tt: lambda e: e.activation(exs[:, tt, :], logit[:, tt, :], AF.Exp, bias=negmx[:, tt:tt + 1]))(t), [nm_ops[t], lg_ops[t]])
                for t in range(NT)]
      se_ops = [C("vector", (lambda tt: lambda e: e.tensor_reduce(sumex[:, tt:tt + 1], exs[:, tt, :], axis=AX, op=OP.add))(t), [ex_ops[t]])
                for t in range(NT)]
      rs_op = C("vector", lambda e: e.reciprocal(rsum[:], sumex[:]), se_ops)
      pr_ops = [C("vector", (lambda tt: lambda e: e.tensor_scalar(probs[:, tt, :], exs[:, tt, :], rsum[:, tt:tt + 1], None, OP.mult, OP.bypass))(t),
                  [rs_op, ex_ops[t]]) for t in range(NT)]
      wsel_ops = []
      for t in range(NT):
          a1 = C("vector", (lambda tt: lambda e: e.tensor_reduce(w1t[:, tt:tt + 1], probs[:, tt, :], axis=AX, op=OP.max))(t), [pr_ops[t]])
          a2 = C("vector", (lambda tt: lambda e: e.tensor_scalar(wx8[:, 0:E], ones8[:, 0:E], w1t[:, tt:tt + 1], None, OP.mult, OP.bypass))(t), [a1, ones_op])
          a3 = C("vector", (lambda tt: lambda e: e.max_index(i1u[:], wx8[:, 0:E], probs[:, tt, :]))(t), [a2])
          a4 = C("vector", (lambda tt: lambda e: e.tensor_copy(e1t[:, tt:tt + 1], i1u[:, 0:1]))(t), [a3])
          a5 = C("vector", (lambda tt: lambda e: e.tensor_scalar(masked[:, tt, :], iota8[:, 0:E], e1t[:, tt:tt + 1], None, OP.is_equal, OP.bypass))(t),
                 [a4, ld_io])
          a6 = C("vector", (lambda tt: lambda e: e.scalar_tensor_tensor(masked[:, tt, :], masked[:, tt, :], -1e30, probs[:, tt, :], OP.mult, OP.add))(t), [a5])
          a7 = C("vector", (lambda tt: lambda e: e.tensor_reduce(w2t[:, tt:tt + 1], masked[:, tt, :], axis=AX, op=OP.max))(t), [a6])
          a8 = C("vector", (lambda tt: lambda e: e.tensor_scalar(wx8[:, 0:E], ones8[:, 0:E], w2t[:, tt:tt + 1], None, OP.mult, OP.bypass))(t), [a7])
          a9 = C("vector", (lambda tt: lambda e: e.max_index(i2u[:], wx8[:, 0:E], masked[:, tt, :]))(t), [a8])
          a10 = C("vector", (lambda tt: lambda e: e.tensor_copy(e2t[:, tt:tt + 1], i2u[:, 0:1]))(t), [a9])
          b1 = C("vector", (lambda tt: lambda e: e.tensor_scalar(wsel[:, tt, :], iota8[:, 0:E], e1t[:, tt:tt + 1], None, OP.is_equal, OP.bypass))(t), [a4])
          b2 = C("vector", (lambda tt: lambda e: e.tensor_scalar(wsel[:, tt, :], wsel[:, tt, :], w1t[:, tt:tt + 1], None, OP.mult, OP.bypass))(t), [b1])
          b3 = C("vector", (lambda tt: lambda e: e.tensor_scalar(masked[:, tt, :], iota8[:, 0:E], e2t[:, tt:tt + 1], None, OP.is_equal, OP.bypass))(t), [a10, b2])
          b4 = C("vector", (lambda tt: lambda e: e.scalar_tensor_tensor(wsel[:, tt, :], masked[:, tt, :], w2t[:, tt:tt + 1], wsel[:, tt, :], OP.mult, OP.add))(t), [b3])
          wsel_ops.append(b4)

    ag_dep = None
    class _M2Stop2(Exception):
        pass
    try:
      if milestone >= 2 and M2CUT >= 1:
        ld_i128 = D("sync", lambda e: e.dma_start(out=iota128[:], in_=iota128_d), dsem="ld0!")
        ld_hot = D("sync", lambda e: e.dma_start(out=myhot[:], in_=myhot_d), dsem="ld0!")
        ld_hotr = D("sync", lambda e: e.dma_start(out=myhotr[:], in_=myhotr_d), dsem="ld0!")
        wr_ops = []
        for t in range(0 if SKIP_WR else NT):
            w1_ = D("sync", (lambda tt: lambda e: e.dma_start(out=ag1a_in[tt * 128:(tt + 1) * 128, :], in_=ln[:, tt, :]))(t), [ln_ops[t] if len(ln_ops) > t else None], dsem="agw!")
            w2_ = D("sync", (lambda tt: lambda e: e.dma_start(out=ag1w_in[tt * 128:(tt + 1) * 128, :], in_=wsel[:, tt, :]))(t), [wsel_ops[t] if len(wsel_ops) > t else None], dsem="agw!")
            wr_ops += [w1_, w2_]
        wt_ops = []
        for t in range(0 if SKIP_WT else NT):
            trw = C("tensor", (lambda tt: lambda e: e.transpose(ptrf[0:8, 0:128], wsel[:, tt, :], eyef[:]))(t),
                    [wsel_ops[t] if len(wsel_ops) > t else None] + ([wt_ops[-1]] if wt_ops else (lg_ops[-1:] if lg_ops else [])))
            evw = C("scalar", (lambda tt: lambda e: e.copy(wselT[:, tt * 128:(tt + 1) * 128], ptrf[0:8, 0:128]))(t), [trw])
            wt_ops.append(evw)
        wrb = D("sync", lambda e: e.dma_start(out=ag1b_in, in_=wselT[:]), wt_ops, dsem="agw!") if not SKIP_WT else None
        if M2CUT < 1.25:
            ag_dep = None
            raise _M2Stop2
        ag1b_op = P.add("gpsimd", lambda e: e.collective_compute(
            "AllGather", OP.bypass, replica_groups=[list(range(NCORES))],
            ins=[ag1b_in_f], outs=[ag1b_out_f.rearrange("r c -> (r c)")]), wr_ops + [wrb], coll=True)
        ag1a_op = P.add("gpsimd", lambda e: e.collective_compute(
            "AllGather", OP.bypass, replica_groups=[list(range(NCORES))],
            ins=[ag1a_in.rearrange("r c -> (r c)")], outs=[ag1a_out.rearrange("r c -> (r c)")]), wr_ops, coll=True)
        ag_dep = [ag1a_op, ag1b_op]
    except _M2Stop2:
        pass

    toki_ops = []
    gidx_wr = []
    routing_done = []
    class _M2Stop(Exception):
        pass
    try:
        _m2go = milestone >= 2 and M2CUT >= 2 and ag_dep is not None
        if not _m2go:
            raise _M2Stop
        ld_wcol = []
        for e_ in range(E):
            for r in range(NCORES):
                op = D("sync", (lambda ee, rr: lambda e: e.dma_start(
                    out=wcol8[16 * rr:16 * (rr + 1), ee, :],
                    in_=ag1b_out_f[rr, TC * E + ee * TC:TC * E + (ee + 1) * TC].rearrange("(p f) -> p f", p=16)))(e_, r), [ag1b_op], dsem="wcol!")
                ld_wcol.append(op)
        if M2CUT < 3:
            routing_done = ld_wcol
            raise _M2Stop
        ke_ops = []
        r_ops = []
        for e_ in range(E):
            ke = C("vector", (lambda ee: lambda e: e.tensor_scalar(keep8[:, ee, :], wcol8[:, ee, :], 0.0, None, OP.is_gt, OP.bypass))(e_), ld_wcol)
            ke_ops.append(ke)
        for e_ in range(E):
            r = C("vector", (lambda ee: lambda e: e.tensor_tensor_scan(R8[:, ee, :], keep8[:, ee, :], keep8[:, ee, :], 0.0, OP.add, OP.bypass))(e_),
                  [ke_ops[e_]])
            r_ops.append(r)
        for e_ in range(E):
            C("vector", (lambda ee: lambda e: e.tensor_copy(ktot8[:, ee:ee + 1], R8[:, ee, 127:128]))(e_), [r_ops[e_]])
        kt_done = C("vector", lambda e: e.tensor_copy(dump8[:, 0:E], ktot8[:]), r_ops + [])
        trk = C("tensor", lambda e: e.transpose(ptrf[0:E, 0:128], ktot8[:], eyef[:]), [kt_done] + wt_ops[-1:])
        evk = C("scalar", lambda e: e.copy(koffT[:], ptrf[0:E, 0:128]), [trk])
        sck = C("vector", lambda e: e.tensor_tensor_scan(koffTx[:], koffT[:], koffT[:], 0.0, OP.add, OP.bypass), [evk])
        exk = C("vector", lambda e: e.tensor_tensor(koffTx[:], koffTx[:], koffT[:], OP.subtract), [sck])
        trk2 = C("tensor", lambda e: e.transpose(ptrf[:, 0:E], koffTx[:], eyef[0:E, 0:E]), [exk])
        evk2 = C("scalar", lambda e: e.copy(koff8[:], ptrf[:, 0:E]), [trk2])
        rg_ops = []
        kp_ops = []
        for e_ in range(E):
            rg = C("vector", (lambda ee: lambda e: e.tensor_scalar(R8[:, ee, :], R8[:, ee, :], koff8[:, ee:ee + 1], None, OP.add, OP.bypass))(e_),
                   [evk2, r_ops[e_]])
            rg_ops.append(rg)
            kp = C("vector", (lambda ee: lambda e: e.scalar_tensor_tensor(keep8[:, ee, :], R8[:, ee, :], float(CAP), keep8[:, ee, :], OP.is_le, OP.mult))(e_),
                   [rg, ke_ops[e_]])
            kp_ops.append(kp)
        if M2CUT < 4:
            routing_done = kp_ops
            raise _M2Stop
        zg = C("vector", lambda e: e.memset(gAB[:], 0.0))
        zp = C("vector", lambda e: e.memset(pctmp[:, 0, :], 0.0))
        prev_pc = zp
        acc_ops = [zg]
        for e_ in range(E):
            cur, nxt = e_ % 2, (e_ + 1) % 2
            isA = C("vector", (lambda c: lambda e: e.tensor_scalar(natmp[:], pctmp[:, c, :], 0.0, None, OP.is_equal, OP.bypass))(cur), [prev_pc] + ld_wcol)
            isA2 = C("vector", (lambda ee: lambda e: e.scalar_tensor_tensor(natmp[:], wcol8[:, ee, :], 0.0, natmp[:], OP.is_gt, OP.mult))(e_), [isA])
            ge1 = C("vector", (lambda ee: lambda e: e.tensor_scalar(getmp[:], R8[:, ee, :], float(ee * CAP - 1), None, OP.add, OP.bypass))(e_), [rg_ops[e_]])
            ge2 = C("vector", (lambda ee: lambda e: e.tensor_tensor(getmp[:], getmp[:], keep8[:, ee, :], OP.mult))(e_), [ge1, kp_ops[e_]])
            a1 = C("vector", (lambda: lambda e: e.tensor_tensor(dump8[:], natmp[:], getmp[:], OP.mult))(), [isA2, ge2])
            a2 = C("vector", (lambda: lambda e: e.tensor_tensor(gAB[:, 0, :], gAB[:, 0, :], dump8[:], OP.add))(), [a1] + acc_ops[-1:])
            a3 = C("vector", (lambda ee: lambda e: e.tensor_tensor(dump8[:], keep8[:, ee, :], wcol8[:, ee, :], OP.mult))(e_), [kp_ops[e_], a2])
            a4 = C("vector", (lambda: lambda e: e.tensor_tensor(dump8[:], dump8[:], natmp[:], OP.mult))(), [a3])
            a5 = C("vector", (lambda: lambda e: e.tensor_tensor(gAB[:, 2, :], gAB[:, 2, :], dump8[:], OP.add))(), [a4])
            b0 = C("vector", (lambda c: lambda e: e.tensor_scalar(natmp[:], pctmp[:, c, :], 1.0, None, OP.is_equal, OP.bypass))(cur), [a5, prev_pc])
            b1 = C("vector", (lambda ee: lambda e: e.scalar_tensor_tensor(natmp[:], wcol8[:, ee, :], 0.0, natmp[:], OP.is_gt, OP.mult))(e_), [b0])
            b2 = C("vector", (lambda: lambda e: e.tensor_tensor(dump8[:], natmp[:], getmp[:], OP.mult))(), [b1, ge2])
            b3 = C("vector", (lambda: lambda e: e.tensor_tensor(gAB[:, 1, :], gAB[:, 1, :], dump8[:], OP.add))(), [b2])
            b4 = C("vector", (lambda ee: lambda e: e.tensor_tensor(dump8[:], keep8[:, ee, :], wcol8[:, ee, :], OP.mult))(e_), [b3, kp_ops[e_]])
            b5 = C("vector", (lambda: lambda e: e.tensor_tensor(dump8[:], dump8[:], natmp[:], OP.mult))(), [b4])
            b6 = C("vector", (lambda: lambda e: e.tensor_tensor(gAB[:, 3, :], gAB[:, 3, :], dump8[:], OP.add))(), [b5])
            pcn = C("vector", (lambda ee, c, n: lambda e: e.scalar_tensor_tensor(pctmp[:, n, :], wcol8[:, ee, :], 0.0, pctmp[:, c, :], OP.is_gt, OP.add))(e_, cur, nxt),
                    [b6, prev_pc])
            prev_pc = pcn
            acc_ops.append(b6)
        for j in range(4):
            op = D("sync", (lambda jj: lambda e: e.dma_start(out=gidx_d[jj, :].rearrange("(p f) -> p f", p=128), in_=gAB[:, jj, :]))(j), [acc_ops[-1], prev_pc], dsem="gidx!")
            gidx_wr.append(op)

        if M2CUT < 5:
            routing_done = gidx_wr
            raise _M2Stop
        zR = C("vector", lambda e: e.memset(Rmy[:], 0.0))
        prevR = [zR]
        mko = C("tensor", lambda e: e.matmul(ptr1[0:1, 0:128], myhot[:], koffTx[:], start=True, stop=True), [exk, ld_hot, gq6, evk2])
        evko = C("scalar", lambda e: e.copy(koffrow[:], ptr1[0:1, 0:128]), [mko])
        tkm = C("tensor", lambda e: e.transpose(ptrf[:, 0:1], koffrow[:], eyef[0:1, 0:1]), [evko, evk2])
        evkm = C("scalar", lambda e: e.copy(koffmy[:], ptrf[:, 0:1]), [tkm])
        mkb = C("tensor", lambda e: e.matmul(ptr1[:, 0:128], onesrow[:], koffrow[:], start=True, stop=True), [evko, onesrow_op, evkm])
        evkb = C("scalar", lambda e: e.copy(koffB[:], ptr1[:, 0:128]), [mkb])
        for e_ in range(E):
            pass
        mfl = C("tensor", lambda e: e.matmul(ptr1[:, 0:8], onesrow[:], myhotr[:], start=True, stop=True), [ld_hotr, onesrow_op, evkb])
        evmf = C("scalar", lambda e: e.copy(mflag[:, 0:8], ptr1[:, 0:8]), [mfl])
        mflag_hold = evmf
        mfl_evmf = evmf
        rmy_ops = [zR]
        for e_ in range(E):
            op = C("vector", (lambda ee: lambda e: e.scalar_tensor_tensor(Rmy[:], R8[:, ee, :], mflag[:, ee:ee + 1], Rmy[:], OP.mult, OP.add))(e_),
                   [mflag_hold, rg_ops[e_], rmy_ops[-1]])
            rmy_ops.append(op)
        rmy_done = rmy_ops[-1]
        tb0 = C("vector", lambda e: e.memset(tbl[:], 0.0))
        tb1 = C("scalar", lambda e: e.copy(tbl[:, 0:1], koffmy[:]), [evkm, tb0])
        tb2 = C("vector", lambda e: e.tensor_scalar(tbl[:, 1:129], Rmy[:], koffmy[:, 0:1], None, OP.subtract, OP.bypass), [rmy_done, evkm, tb0])
        tb3 = D("sync", lambda e: e.dma_start(out=tbl_d, in_=tbl[:]), [tb1, tb2], dsem="tbl!")
        sv_ops = []
        for i in range(16):
            op = C("vector", (lambda ii: lambda e: e.tensor_scalar(sval[:, ii:ii + 1], iota128[:], float(128 * ii), None, OP.add, OP.bypass))(i),
                   [ld_i128])
            sv_ops.append(op)
        l1_ops = []
        for i in range(16):
            op = C("vector", (lambda ii: lambda e: e.tensor_scalar(dump8[:], koffB[:], sval[:, ii:ii + 1], None, OP.is_le, OP.add, accum_out=cnts[:, ii:ii + 1]))(i),
                   [evkb, sv_ops[i]] + acc_ops[-1:])
            l1_ops.append(op)
        pz_op = C("vector", lambda e: e.tensor_scalar(pz[:], cnts[:], 1.0, None, OP.subtract, OP.bypass), l1_ops)
        pzi_op = C("vector", lambda e: e.tensor_copy(pzi[:], pz[:]), [pz_op])
        g2_ops = []
        for i in range(16):
            op = P.add("gpsimd", (lambda ii: lambda e: e.indirect_dma_start(
                g2buf[:, ii, :], None, tbl_d, IndirectOffsetOnAxis(pzi[:, ii:ii + 1], 0)))(i), [pzi_op, tb3], dma=True, dsem="g2!")
            g2_ops.append(op)
        jv_ops = []
        fv_ops = []
        for i in range(16):
            j1 = C("vector", (lambda ii: lambda e: e.scalar_tensor_tensor(jv[:, ii:ii + 1], g2buf[:, ii, 0:1], -1.0, sval[:, ii:ii + 1], OP.mult, OP.add))(i),
                   [g2_ops[i], sv_ops[i]])
            jv_ops.append(j1)
        for i in range(16):
            f1 = C("vector", (lambda ii: lambda e: e.tensor_scalar(dump8[:], g2buf[:, ii, 1:129], jv[:, ii:ii + 1], None, OP.is_le, OP.add, accum_out=fv[:, ii:ii + 1]))(i),
                   [jv_ops[i], g2_ops[i]])
            fv_ops.append(f1)
        tk_op = C("vector", lambda e: e.scalar_tensor_tensor(tokf[:], pz[:], 128.0, fv[:], OP.mult, OP.add), fv_ops + [pz_op])
        toki_op = C("vector", lambda e: e.tensor_copy(toki[:], tokf[:]), [tk_op])
        toki_ops = [toki_op]
        routing_done = [toki_op, pzi_op] + gidx_wr

    
    except _M2Stop:
        pass

    s_ops = []
    finals = []
    SPENG = mybir.EngineType.SP
    if milestone >= 3:
        alias_w = wr_ops + [qb_ops[-1], lg_ops[-1], wsel_ops[-1], mm_ops[-1], ev_ops[-1], gevac_ops[-1]]
        routing_alias = [toki_op, pzi_op] + gidx_wr + fv_ops + jv_ops


        ld_gpn = D("sync", lambda e: e.dma_start(out=gpn[:], in_=gpn_d), dsem="ldw!")
        ld_dpn = D("sync", lambda e: e.dma_start(out=dpn[:], in_=dpn_d), dsem="ldw!")

        def wquant(src_view, nchunks, chunk_free, wq_dst_fn, acc_col, sc_col, denom, deps0, prefix, csub):
            def w3(cc):
                return wrot[:, cc % 2, 0:chunk_free].rearrange("p (c o) -> p c o", c=csub)
            z0 = C("vector", (lambda: lambda e: e.memset(wsum[:, acc_col:acc_col + 1], 0.0))(), deps0)
            red_prev = z0
            lds = []
            for ch in range(nchunks):
                pace = [lds[ch - 2][1]] if ch >= 2 else []
                ldo = D("sync", (lambda cc: lambda e: e.dma_start(out=w3(cc), in_=src_view(cc)))(ch),
                        deps0 + [p for p in pace], dsem=f"{prefix}{ch % 2}")
                r1_ = C("vector", (lambda cc: lambda e: e.tensor_reduce(wsum[:, 2:3], wrot[:, cc % 2, 0:chunk_free], axis=AX, op=OP.add, apply_absolute_value=True))(ch),
                        [ldo, red_prev])
                a1_ = C("vector", (lambda: lambda e: e.tensor_tensor(wsum[:, acc_col:acc_col + 1], wsum[:, acc_col:acc_col + 1], wsum[:, 2:3], OP.add))(),
                        [r1_])
                red_prev = a1_
                lds.append((ldo, a1_))
            t1 = C("tensor", (lambda: lambda e: e.transpose(ptrf[0:1, 0:128], wsum[:, acc_col:acc_col + 1], eyef[:]))(), [red_prev] + self_ptrf[-1:])
            e1_ = C("scalar", (lambda: lambda e: e.copy(grow[:], ptrf[0:1, 0:128]))(), [t1])
            r2_ = C("vector", (lambda: lambda e: e.tensor_reduce(gsc[:, 0:1], grow[:], axis=AX, op=OP.add))(), [e1_])
            m1_ = C("vector", (lambda: lambda e: e.tensor_scalar(gsc[:, 3:4], gsc[:, 0:1], 1.0 / denom, 1e-5, OP.mult, OP.max))(), [r2_])
            w1_q = C("vector", (lambda: lambda e: e.reciprocal(gsc[:, 2:3], gsc[:, 3:4]))(), [m1_])
            b1_ = C("tensor", (lambda: lambda e: e.matmul(ptrf[:, 0:2], onesrow[:], gsc[0:1, 2:4], start=True, stop=True))(), [w1_q, e1_, onesrow_op])
            b2_ = C("scalar", (lambda: lambda e: e.copy(wqsc[:, sc_col:sc_col + 2], ptrf[:, 0:2]))(), [b1_])
            self_ptrf.append(b2_)
            qops = []
            prevq = None
            for ch in range(nchunks):
                pace = [qops[ch - 2][2]] if ch >= 2 else [lds[-1][1]]
                ldo = D("sync", (lambda cc: lambda e: e.dma_start(out=w3(cc), in_=src_view(cc)))(ch),
                        pace, dsem=f"{prefix}{ch % 2}")
                mg_ = C("scalar", (lambda cc: lambda e: e.activation(wrot[:, cc % 2, 0:chunk_free], wrot[:, cc % 2, 0:chunk_free], AF.Copy, bias=MAGIC, scale=wqsc[:, sc_col:sc_col + 1]))(ch),
                        [ldo, b2_])
                c1_ = C("vector", (lambda cc: lambda e: e.tensor_scalar(wrot[:, cc % 2, 0:chunk_free], wrot[:, cc % 2, 0:chunk_free], MAGIC, -1.0, OP.subtract, OP.max))(ch), [mg_])
                c2_ = C("vector", (lambda cc: lambda e: e.tensor_scalar(wq_dst_fn(cc), w3(cc), 1.0, None, OP.min, OP.bypass))(ch), [c1_])
                qops.append((ldo, c1_, c2_))
                prevq = c2_
            return [q[2] for q in qops]

        self_ptrf = [mfl_evmf] if milestone >= 2 else []
        gpw_t = gpwT_d.rearrange("(c p) o -> p c o", p=128)
        dpw_t = dpwT_d.rearrange("(c p) o -> p c o", p=128)

        gpq_ops = wquant(lambda cc: gpw_t[:, :, 512 * cc:512 * (cc + 1)], 8, 4096,
                         lambda cc: gp_wq[:, :, 512 * cc:512 * (cc + 1)],
                         0, 0, float(H * I2), alias_w, "wsg", HC)
        dpq_ops = wquant(lambda cc: dpw_t[:, 4 * cc:4 * (cc + 1), :], 4, 4096,
                         lambda cc: dp_wq[:, 4 * cc:4 * (cc + 1), :],
                         1, 2, float(II * H), routing_alias + [gpq_ops[-1]], "wsd", 4)

        rt_done = []
        rtr_evs = []
        rtr_lev = []
        if milestone >= 4:
            ldf_ops = [
                D("sync", lambda e: e.dma_start(out=wrouter[:], in_=wr_d), dsem="ldf!"),
                D("sync", lambda e: e.dma_start(out=brrep[:], in_=brr_d), dsem="ldf!"),
                D("sync", lambda e: e.dma_start(out=iotac16[:], in_=iotac16_d), dsem="ldf!"),
                D("sync", lambda e: e.dma_start(out=rb128s[:], in_=rb128_d), dsem="ldf!"),
                D("sync", lambda e: e.dma_start(out=rbps[:], in_=rbp_d), dsem="ldf!"),
            ]
            rtr_mms = []
            badd_ops = []
            for t in range(NT):
                lr = D("sync", (lambda tt: lambda e: e.dma_start(
                    out=lnrot[:, 0, :], in_=ag1a_in[tt * 128:(tt + 1) * 128, :]))(t),
                    [wr_ops[2 * t]] + ([rtr_evs[-1]] if rtr_evs else [wsel_ops[-1]]), dsem=f"lr{t % 2}")
                for hc in range(8):
                    k = t * 8 + hc
                    bnk = pbk[0] if k % 2 == 0 else pbk[2]
                    tr = C("tensor", (lambda cc, bb: lambda e: e.transpose(
                        bb[:, 0:128], lnrot[:, 0, cc * 128:(cc + 1) * 128], eyef[:]))(hc, bnk),
                        [lr] + ([rtr_evs[k - 2]] if k >= 2 else []))
                    ev = C("scalar", (lambda cc, bb: lambda e: e.copy(lnTb[:, cc % 2, :], bb[:, 0:128]))(hc, bnk),
                           [tr] + ([rtr_mms[k - 2]] if k >= 2 else []))
                    mm = C("tensor", (lambda cc: lambda e: e.matmul(
                        pbk[1][:, 0:EF], lnTb[:, cc % 2, :], wrouter[:, cc, :],
                        start=(cc == 0), stop=(cc == 7)))(hc),
                        [ev, ldf_ops[0]] + ([rtr_lev[-1]] if (hc == 0 and rtr_lev) else []))
                    rtr_mms.append(mm)
                    rtr_evs.append(ev)
                lev = C("scalar", (lambda tt: lambda e: e.copy(l2t[:, tt, :], pbk[1][:, 0:EF]))(t), [rtr_mms[-1]])
                rtr_lev.append(lev)
                badd = C("vector", (lambda tt: lambda e: e.tensor_tensor(l2t[:, tt, :], l2t[:, tt, :], brrep[:], OP.add))(t),
                         [lev, ldf_ops[1]])
                badd_ops.append(badd)
            rt_done = badd_ops

        wselrows = ag1b_out_f.rearrange("r b -> (r b)").rearrange("(n k) -> n k", k=E)
        gath_ops = [None] * 16
        wg_ops = [None] * 16
        q1w_ops = []
        q1t_ops = {}
        gpmm_ops = {}
        umul_ops = {}
        q2w_ops = {}
        q2t_ops = {}
        dpmm_ops = {}
        ctev_ops = [None] * 16
        ag2w_ops = []
        quant_last = [None] * 16
        dq1b_ops = [None] * 16
        wt2_ops = [None] * 16
        qc1_ops = [None] * 16
        xin_readers = {}
        csc_ops = [None] * 16
        qc2_ops = [None] * 16
        EC = lambda rb, k: exsc[:, (rb % 4) * 16 + k:(rb % 4) * 16 + k + 1]

        if DELAY_AG:
            dlast = None
            for _i in range(DELAY_AG):
                dlast = C("vector", lambda e: e.memset(getmp[:, 0:16], 0.0), ([dlast] if dlast else []) + ag_dep)
            ag_dep = ag_dep + [dlast]
        txin_ops = [None] * 16
        for rb in range(16):
            ch = rb // 4
            grp = rb // 2
            par = rb % 2
            pace_g = xin_readers[rb - 2] if rb >= 2 else [dpq_ops[-1]]
            if rb >= 2 and txin_ops[rb - 2] is not None:
                pace_g = pace_g + [txin_ops[rb - 2]]
            g1 = P.add("gpsimd", (lambda r: lambda e: e.indirect_dma_start(
                xinr[:, r % 2, :], None, ag1a_out, IndirectOffsetOnAxis(toki[:, r:r + 1], 0)))(rb),
                [toki_op] + ag_dep + pace_g, dma=True, dsem=f"gx{par}")
            gath_ops[rb] = g1
            if "tap_xin" in tapd:
                tx = D("gpsimd", (lambda r: lambda e: e.dma_start(
                    out=tapd["tap_xin"][r * 128:(r + 1) * 128, :], in_=xinr[:, r % 2, :]))(rb), [g1], dsem=f"tx{par}")
                txin_ops[rb] = tx
                finals.append(tx)
            ss1 = C("scalar", (lambda r: lambda e: e.activation(dump[:], xinr[:, r % 2, :], AF.Square, accum_out=EC(r, 0)))(rb), [g1])
            xin_readers[rb] = [ss1]
            sq1_ = C("scalar", (lambda r: lambda e: e.activation(EC(r, 0), EC(r, 0), AF.Sqrt, bias=epsc[:, 2:3], scale=1.0 / float(H)))(rb), [ss1])
            rc1_ = C("vector", (lambda r: lambda e: e.reciprocal(EC(r, 0), EC(r, 0)))(rb), [sq1_])
            pp1 = C("vector", (lambda r: lambda e: e.tensor_tensor(p1r[:, 0, :], xinr[:, r % 2, :], gpn[:], OP.mult))(rb), [g1, ld_gpn] + ([qc1_ops[rb - 1]] if rb >= 1 else []))
            xin_readers[rb].append(pp1)
            mm1 = C("vector", (lambda r: lambda e: e.tensor_reduce(EC(r, 1), p1r[:, 0, :], axis=AX, op=OP.max, apply_absolute_value=True))(rb), [pp1])
            mm1b = C("vector", (lambda r: lambda e: e.tensor_scalar(EC(r, 1), EC(r, 1), 1e-5, None, OP.max, OP.bypass))(rb), [mm1])
            sc1 = C("vector", (lambda r: lambda e: e.reciprocal(EC(r, 2), EC(r, 1)))(rb), [mm1b])
            sc1b = C("vector", (lambda r: lambda e: e.tensor_scalar(EC(r, 2), EC(r, 2), 127.0, None, OP.mult, OP.bypass))(rb), [sc1])
            dq1 = C("vector", (lambda r: lambda e: e.tensor_tensor(EC(r, 3), EC(r, 0), EC(r, 1), OP.mult))(rb), [rc1_, mm1b])
            dq1b = C("vector", (lambda r: lambda e: e.tensor_scalar(EC(r, 3), EC(r, 3), 1.0 / 127.0, wqsc[:, 1:2], OP.mult, OP.mult))(rb), [dq1, gpq_ops[-1]])
            dq1b_ops[rb] = dq1b
            mgq1 = C("scalar", (lambda r: lambda e: e.activation(p1r[:, 0, :], p1r[:, 0, :], AF.Copy, bias=MAGIC, scale=EC(r, 2)))(rb), [pp1, sc1b, mm1])
            qc1 = C("vector", (lambda r: lambda e: e.tensor_scalar(q1r[:, r % 4, :], p1r[:, 0, :], MAGIC, None, OP.subtract, OP.bypass))(rb), [mgq1] + ([q1t_ops[(rb - 4) // 4][0]] if rb >= 4 else []))
            q1w = D("sync", (lambda r: lambda e: e.dma_start(out=q1s_d[r * 128:(r + 1) * 128, :], in_=q1r[:, r % 4, :]))(rb), [qc1], dsem=f"q1w{ch}!")
            q1w_ops.append(q1w)
            quant_last[rb] = qc1
            qc1_ops[rb] = qc1

            if rb % 4 == 3:
                deps_t = [q1w_ops[ch * 4 + k] for k in range(4)]
                war = [gpmm_ops[(ch - 1, 3, 1)]] if ch >= 1 else []
                tts = []
                for kf in range(HC):
                    tt_ = D("sync", (lambda cc, kk: lambda e: e.dma_start(
                        out=q1T[:, kk, :], in_=q1s_d[512 * cc:512 * (cc + 1), 128 * kk:128 * (kk + 1)], transpose=True))(ch, kf),
                        deps_t + war, dsem=f"q1t{ch}!")
                    tts.append(tt_)
                q1t_ops[ch] = tts

                for r2_ in range(ch * 4, ch * 4 + 4):
                    rloc = r2_ % 4
                    for j in range(4):
                        bA = pbk[(j % 2) * 2]
                        bB = pbk[(j % 2) * 2 + 1]
                        warA = [umul_ops[(r2_, j - 2)]] if j >= 2 else ([umul_ops[(r2_ - 1, 2)], umul_ops[(r2_ - 1, 3)]] if r2_ > 0 else [])
                        def mk_gp(rr, jj, bank, ocbase):
                            def f(e):
                                last = None
                                for kf in range(HC):
                                    last = e.matmul(bank[:], q1T[:, kf, (rr % 4) * 128:(rr % 4 + 1) * 128],
                                                    gp_wq[:, kf, ocbase * 512:(ocbase + 1) * 512],
                                                    start=(kf == 0), stop=(kf == HC - 1))
                                return last
                            return f
                        rtr_war = (rtr_evs[-1:] + rtr_lev[-1:]) if (milestone >= 4 and r2_ == 0 and j == 0) else []
                        mA = C("tensor", mk_gp(r2_, j, bA, j), q1t_ops[ch] + [gpq_ops[-1]] + warA + rtr_war)
                        mB = C("tensor", mk_gp(r2_, j, bB, j + 4), q1t_ops[ch] + warA + rtr_war)
                        gpmm_ops[(r2_, j, 0)] = mA
                        gpmm_ops[(r2_, j, 1)] = mB
                        sl_ = C("scalar", (lambda rr, jj, bank: lambda e: e.activation(sgb[:], bank[:], AF.Silu, scale=EC(rr, 3)))(r2_, j, bA),
                                [mA, dq1b_ops[r2_]] + ([umul_ops[(r2_, j - 1)]] if j >= 1 else ([umul_ops[(r2_ - 1, 3)]] if r2_ > 0 else
                                 (rtr_evs[-1:] if (milestone >= 4 and r2_ == 0) else []))))
                        um_ = C("vector", (lambda rr, jj, bank: lambda e: e.scalar_tensor_tensor(
                            ubuf[:, jj * 512:(jj + 1) * 512], bank[:], EC(rr, 3), sgb[:], OP.mult, OP.mult))(r2_, j, bB),
                            [mB, sl_, dq1b_ops[r2_]] + (([qc2_ops[r2_ - 1]] + ag2w_ops[r2_ - 1:r2_]) if (j == 0 and r2_ > 0) else []) +
                            (rtr_evs[-1:] if (milestone >= 4 and r2_ == 0 and j == 2) else []))
                        umul_ops[(r2_, j)] = um_
                    udeps = [umul_ops[(r2_, j)] for j in range(4)]
                    ss2 = C("scalar", (lambda rr: lambda e: e.activation(q2r[:, rr % 2, :], ubuf[:], AF.Square, accum_out=EC(rr, 5)))(r2_),
                            udeps + (rtr_evs[-1:] if (milestone >= 4 and r2_ == 0) else []))
                    sq2_ = C("scalar", (lambda rr: lambda e: e.activation(EC(rr, 5), EC(rr, 5), AF.Sqrt, bias=epsc[:, 2:3], scale=1.0 / float(II)))(r2_), [ss2])
                    rc2_ = C("vector", (lambda rr: lambda e: e.reciprocal(EC(rr, 5), EC(rr, 5)))(r2_), [sq2_])
                    pp2 = C("vector", (lambda rr: lambda e: e.tensor_tensor(ubuf[:], ubuf[:], dpn[:], OP.mult))(r2_), udeps + [ld_dpn, ss2])
                    mm2 = C("vector", (lambda rr: lambda e: e.tensor_reduce(EC(rr, 6), ubuf[:], axis=AX, op=OP.max, apply_absolute_value=True))(r2_), [pp2])
                    mm2b = C("vector", (lambda rr: lambda e: e.tensor_scalar(EC(rr, 6), EC(rr, 6), 1e-5, None, OP.max, OP.bypass))(r2_), [mm2])
                    sc2 = C("vector", (lambda rr: lambda e: e.reciprocal(EC(rr, 7), EC(rr, 6)))(r2_), [mm2b])
                    sc2b = C("vector", (lambda rr: lambda e: e.tensor_scalar(EC(rr, 7), EC(rr, 7), 127.0, None, OP.mult, OP.bypass))(r2_), [sc2])
                    dq2 = C("vector", (lambda rr: lambda e: e.tensor_tensor(EC(rr, 8), EC(rr, 5), EC(rr, 6), OP.mult))(r2_), [rc2_, mm2b])
                    dq2b = C("vector", (lambda rr: lambda e: e.tensor_scalar(EC(rr, 8), EC(rr, 8), 1.0 / 127.0, wqsc[:, 3:4], OP.mult, OP.mult))(r2_), [dq2, dpq_ops[-1]])
                    csc = C("vector", (lambda rr: lambda e: e.tensor_copy(EC(rr, 9), EC(rr, 8)))(r2_), [dq2b])
                    csc_ops[r2_] = csc
                    if "tap_csc" in tapd:
                        C("vector", (lambda rr: lambda e: e.tensor_copy(csc_dbg[:, rr:rr + 1], EC(rr, 9)))(r2_), [csc])
                    mgq2 = C("scalar", (lambda rr: lambda e: e.activation(ubuf[:], ubuf[:], AF.Copy, bias=MAGIC, scale=EC(rr, 7)))(r2_), [pp2, sc2b, mm2])
                    qc2 = C("vector", (lambda rr: lambda e: e.tensor_scalar(q2r[:, rr % 2, :], ubuf[:], MAGIC, None, OP.subtract, OP.bypass))(r2_),
                            [mgq2] + ([q2t_ops[(r2_ - 2) // 2][0]] if r2_ >= 2 else []))
                    qc2_ops[r2_] = qc2
                    q2w = D("sync", (lambda rr: lambda e: e.dma_start(out=q2s_d[rr * 128:(rr + 1) * 128, :], in_=q2r[:, rr % 2, :]))(r2_), [qc2], dsem=f"q2w{r2_ // 2}!")
                    q2w_ops[r2_] = q2w
                    quant_last[r2_] = qc2

                    if r2_ % 2 == 1:
                        gg_ = r2_ // 2
                        depsq = [q2w_ops[gg_ * 2], q2w_ops[gg_ * 2 + 1]]
                        warq = [dpmm_ops[(gg_ - 1, 1, 1)]] if gg_ >= 1 else []
                        tts2 = []
                        for kf in range(IIC):
                            tt_ = D("sync", (lambda gx, kk: lambda e: e.dma_start(
                                out=q2T[:, kk, :], in_=q2s_d[256 * gx:256 * (gx + 1), 128 * kk:128 * (kk + 1)], transpose=True))(gg_, kf),
                                depsq + warq, dsem=f"q2t{gg_}!")
                            tts2.append(tt_)
                        q2t_ops[gg_] = tts2
                        for rr2 in (gg_ * 2, gg_ * 2 + 1):
                            for oc2 in range(2):
                                bC = pbk[4 + oc2]
                                warc = [ctev_ops[rr2 - 1]] if rr2 >= 1 else []
                                def mk_dp(rrx, ocx, bank):
                                    def f(e):
                                        last = None
                                        for kf in range(IIC):
                                            last = e.matmul(bank[:], q2T[:, kf, (rrx % 2) * 128:(rrx % 2 + 1) * 128],
                                                            dp_wq[:, kf, ocx * 512:(ocx + 1) * 512],
                                                            start=(kf == 0), stop=(kf == IIC - 1))
                                        return last
                                    return f
                                mD = C("tensor", mk_dp(rr2, oc2, bC), tts2 + [dpq_ops[-1]] + warc + [gevac_ops[-1]])
                                dpmm_ops[(gg_, rr2 % 2, oc2)] = mD
                            ct_ = C("scalar", (lambda rrx: lambda e: (
                                e.activation(ctb[:, 0:512], pbk[4][:], AF.Copy, scale=EC(rrx, 9)),
                                e.activation(ctb[:, 512:1024], pbk[5][:], AF.Copy, scale=EC(rrx, 9)))[-1])(rr2),
                                [dpmm_ops[(gg_, rr2 % 2, 0)], dpmm_ops[(gg_, rr2 % 2, 1)], csc_ops[rr2], qc2_ops[rr2]] +
                                ([ag2w_ops[rr2 - 1]] if rr2 >= 1 else []))
                            ctev_ops[rr2] = ct_
                            aw = D("sync", (lambda rrx: lambda e: e.dma_start(out=ag2_in[rrx * 128:(rrx + 1) * 128, :], in_=ctb[:, 0:H]))(rr2), [ct_], dsem=f"a2w{rr2 % 2}")
                            ag2w_ops.append(aw)

        ag2_op = P.add("gpsimd", lambda e: e.collective_compute(
            "AllGather", OP.bypass, replica_groups=[list(range(NCORES))],
            ins=[ag2_in.rearrange("r c -> (r c)")], outs=[ag2_out.rearrange("r c -> (r c)")]), ag2w_ops, coll=True)

        fy_sc_ops = []
        so_ops_sd = []
        if milestone >= 4:
            adrain = [ag2w_ops[-1], ag2w_ops[-2]]
            t2_ops = []
            for n in range(NT):
                c1 = C("vector", (lambda nn: lambda e: e.tensor_reduce(mx16[:, 0:1], l2t[:, nn, :], axis=AX, op=OP.max))(n),
                       [rt_done[n]] + t2_ops[-1:])
                c2 = C("vector", (lambda nn: lambda e: e.tensor_scalar(wx16[:], ones8[:, 0:8], mx16[:, 0:1], None, OP.mult, OP.bypass))(n),
                       [c1, ones_op])
                c3 = C("vector", (lambda nn: lambda e: e.max_index(i1u16[:], wx16[:], l2t[:, nn, :]))(n), [c2])
                c4 = C("vector", (lambda nn: lambda e: e.tensor_copy(e1f16[:, nn:nn + 1], i1u16[:, 0:1]))(n), [c3])
                c5 = C("vector", (lambda nn: lambda e: e.tensor_scalar(msk16[:], iota8[:, 0:EF], e1f16[:, nn:nn + 1], None, OP.is_equal, OP.bypass))(n),
                       [c4, ld_io])
                c6 = C("vector", (lambda nn: lambda e: e.scalar_tensor_tensor(msk16[:], msk16[:], -1e30, l2t[:, nn, :], OP.mult, OP.add))(n), [c5])
                c7 = C("vector", (lambda nn: lambda e: e.tensor_reduce(mx16[:, 0:1], msk16[:], axis=AX, op=OP.max))(n), [c6])
                c8 = C("vector", (lambda nn: lambda e: e.tensor_scalar(wx16[:], ones8[:, 0:8], mx16[:, 0:1], None, OP.mult, OP.bypass))(n), [c7])
                c9 = C("vector", (lambda nn: lambda e: e.max_index(i1u16[:], wx16[:], msk16[:]))(n), [c8])
                c10 = C("vector", (lambda nn: lambda e: e.tensor_copy(e2f16[:, nn:nn + 1], i1u16[:, 0:1]))(n), [c9])
                t2_ops = [c10]
            top2_done = t2_ops[0]
            me_ops = []
            for e_ in range(EF):
                m1 = C("vector", (lambda ee: lambda e: e.tensor_scalar(me16[:, ee, :], e1f16[:], float(ee), None, OP.is_equal, OP.bypass))(e_),
                       [top2_done])
                m2 = C("vector", (lambda ee: lambda e: e.tensor_scalar(dumpa16[:], e2f16[:], float(ee), None, OP.is_equal, OP.bypass))(e_),
                       [top2_done] + me_ops[-1:])
                m3 = C("vector", (lambda ee: lambda e: e.tensor_tensor(me16[:, ee, :], me16[:, ee, :], dumpa16[:], OP.add))(e_), [m1, m2])
                me_ops.append(m3)
            r16_ops = []
            for e_ in range(EF):
                r = C("vector", (lambda ee: lambda e: e.tensor_tensor_scan(R16[:, ee, :], me16[:, ee, :], me16[:, ee, :], 0.0, OP.add, OP.bypass))(e_),
                      [me_ops[e_]])
                r16_ops.append(r)
            kt16 = C("vector", lambda e: e.tensor_copy(ktot16[:], R16[:, :, NT - 1]), r16_ops)
            ft1 = C("tensor", lambda e: e.transpose(ptr2[0:EF, 0:128], ktot16[:], eyef[:]), [kt16] + rtr_evs[-1:])
            fe1 = C("scalar", lambda e: e.copy(ktT[:], ptr2[0:EF, 0:128]), [ft1])
            fs1 = C("vector", lambda e: e.tensor_tensor_scan(kscT[:], ktT[:], ktT[:], 0.0, OP.add, OP.bypass), [fe1])
            fx1 = C("vector", lambda e: e.tensor_tensor(koffTx16[:], kscT[:], ktT[:], OP.subtract), [fs1])
            ktw = D("sync", lambda e: e.dma_start(out=kofT_d, in_=koffTx16[:]), [fx1], dsem="ktw!")
            ft2 = C("tensor", lambda e: e.transpose(ptr3[:, 0:EF], koffTx16[:], eyef[0:EF, 0:EF]), [fx1] + rtr_lev[-1:])
            fe2 = C("scalar", lambda e: e.copy(koff16[:], ptr3[:, 0:EF]), [ft2])
            cnt1 = C("vector", lambda e: e.tensor_copy(cntT16[:], kscT[:, 127:128]), [fs1])
            pca = C("vector", lambda e: e.tensor_scalar(pcT16[:], cntT16[:], 1.0 / 128.0, 0.49609375, OP.mult, OP.add), [cnt1])
            pcb = C("vector", lambda e: e.tensor_scalar(pcT16[:], pcT16[:], MAGIC, MAGIC, OP.add, OP.subtract), [pca])
            pcc = C("vector", lambda e: e.tensor_scalar(pcT16[:], pcT16[:], 128.0, None, OP.mult, OP.bypass), [pcb])
            ft3 = C("tensor", lambda e: e.transpose(ptr2[0:1, 0:EF], pcT16[:], eyef[0:EF, 0:EF]), [pcc, fe1])
            fe3 = C("scalar", lambda e: e.copy(pc1r[:], ptr2[0:1, 0:EF]), [ft3])
            fs3 = C("vector", lambda e: e.tensor_tensor_scan(sc1r[:], pc1r[:], pc1r[:], 0.0, OP.add, OP.bypass), [fe3])
            fo3 = C("vector", lambda e: e.tensor_tensor(off1r[:], sc1r[:], pc1r[:], OP.subtract), [fs3])
            ft4 = C("tensor", lambda e: e.transpose(ptr2[0:EF, 0:1], off1r[:], eyef[0:1, 0:1]), [fo3, fe3])
            fe4 = C("scalar", lambda e: e.copy(offT16[:], ptr2[0:EF, 0:1]), [ft4])
            ft5 = C("tensor", lambda e: e.transpose(ptr2[0:EF, 1:2], sc1r[:], eyef[0:1, 0:1]), [fs3, fe4])
            fe5 = C("scalar", lambda e: e.copy(endT16[:], ptr2[0:EF, 1:2]), [ft5])
            eh1 = C("vector", lambda e: e.tensor_scalar(erbhot[:], rb128s[:], offT16[0:EF, 0:1], None, OP.is_ge, OP.bypass), [fe4, ldf_ops[3]])
            eh2_ = C("vector", lambda e: e.tensor_scalar(eh2[:], rb128s[:], endT16[0:EF, 0:1], None, OP.is_lt, OP.bypass), [fe5, ldf_ops[3]])
            eh3 = C("vector", lambda e: e.tensor_tensor(erbhot[:], erbhot[:], eh2[:], OP.mult), [eh1, eh2_])
            mr1 = C("tensor", lambda e: e.matmul(ptr3[0:1, 0:48], iotac16[:], erbhot[:], start=True, stop=True), [eh3, ldf_ops[2], fe2])
            er1 = C("scalar", lambda e: e.copy(erow[:], ptr3[0:1, 0:48]), [mr1])
            mr2 = C("tensor", lambda e: e.matmul(ptr3[0:1, 64:112], cntT16[:], erbhot[:], start=True, stop=True), [eh3, cnt1, er1])
            er2 = C("scalar", lambda e: e.copy(cntrow[:], ptr3[0:1, 64:112]), [mr2])
            mr3 = C("tensor", lambda e: e.matmul(ptr2[0:1, 0:48], offT16[:], erbhot[:], start=True, stop=True), [eh3, fe4, fe5])
            er3 = C("scalar", lambda e: e.copy(offrow[:], ptr2[0:1, 0:48]), [mr3])
            mb1 = C("tensor", lambda e: e.matmul(ptr3[:, 0:48], onesrow[:], erow[:], start=True, stop=True), [er1, onesrow_op, er2])
            eb1 = C("scalar", lambda e: e.copy(erbB[:], ptr3[:, 0:48]), [mb1])
            mb2 = C("tensor", lambda e: e.matmul(ptr3[:, 64:112], onesrow[:], cntrow[:], start=True, stop=True), [er2, eb1])
            eb2 = C("scalar", lambda e: e.copy(cntB[:], ptr3[:, 64:112]), [mb2])
            mb3 = C("tensor", lambda e: e.matmul(ptr2[:, 0:48], onesrow[:], offrow[:], start=True, stop=True), [er3, eb2])
            eb3 = C("scalar", lambda e: e.copy(offB[:], ptr2[:, 0:48]), [mb3])
            rl1 = C("vector", lambda e: e.tensor_tensor(rloc48[:], rbps[:], offB[:], OP.subtract), [eb3, ldf_ops[4]])
            vl1 = C("vector", lambda e: e.tensor_tensor(validf[:], rloc48[:], cntB[:], OP.is_lt), [rl1, eb2])
            ft6 = C("tensor", lambda e: e.transpose(ptr2[0:48, 0:1], erow[:], eyef[0:1, 0:1]), [er1, eb3])
            fe6 = C("scalar", lambda e: e.copy(er48[:], ptr2[0:48, 0:1]), [ft6])
            ei1 = C("vector", lambda e: e.tensor_copy(erb48i[:], er48[:]), [fe6])
            ei2 = C("vector", lambda e: e.tensor_copy(erbi1[:], erow[:]), [er1])
            kg = P.add("gpsimd", lambda e: e.indirect_dma_start(
                koffg[:], None, kofT_d, IndirectOffsetOnAxis(erb48i[:], 0)), [ei1, ktw], dma=True, dsem="kg0")
            ft7 = C("tensor", lambda e: e.transpose(ptr2[0:48, 0:128], rloc48[:], eyef[:]), [rl1, fe6])
            fe7 = C("scalar", lambda e: e.copy(rlT[:], ptr2[0:48, 0:128]), [ft7])
            l1ops = []
            for p in range(128):
                op = C("vector", (lambda pp: lambda e: e.tensor_scalar(
                    dump48[:], koffg[:], rlT[:, pp:pp + 1], None, OP.is_le, OP.add,
                    accum_out=pzT48[:, pp:pp + 1]))(p), [kg, fe7] + l1ops[-1:])
                l1ops.append(op)
            ft8 = C("tensor", lambda e: e.transpose(ptr2[0:128, 0:48], pzT48[:], eyef[0:48, 0:48]), l1ops[-1:] + [fe7])
            fe8 = C("scalar", lambda e: e.copy(pz48[:], ptr2[0:128, 0:48]), [ft8])
            pzm = C("vector", lambda e: e.tensor_scalar(pz48[:], pz48[:], 1.0, None, OP.subtract, OP.bypass), [fe8])
            tb_ops = []
            for e_ in range(EF):
                ta = C("scalar", (lambda ee: lambda e: e.copy(tbs16[:, ee % 2, 0:1], koff16[:, ee:ee + 1]))(e_),
                       [fe2] + ([tb_ops[-2]] if len(tb_ops) >= 2 else []))
                tb = C("vector", (lambda ee: lambda e: e.tensor_copy(tbs16[:, ee % 2, 1:17], R16[:, ee, :]))(e_),
                       [r16_ops[e_], ta])
                tw = D("sync", (lambda ee: lambda e: e.dma_start(
                    out=tb16_d[ee * 128:(ee + 1) * 128, :], in_=tbs16[:, ee % 2, :]))(e_), [ta, tb], dsem=f"tb{e_ % 2}")
                tb_ops.append(tw)
            ix1 = C("vector", lambda e: e.scalar_tensor_tensor(idx48[:], erbB[:], 128.0, pz48[:], OP.mult, OP.add), [pzm, eb1])
            ix2 = C("vector", lambda e: e.tensor_copy(idx48i[:], idx48[:]), [ix1])
            fvops = [None] * 48
            g6ops = [None] * 48
            for rb in range(48):
                g6 = P.add("gpsimd", (lambda r: lambda e: e.indirect_dma_start(
                    g6b[:, r % 2, :], None, tb16_d, IndirectOffsetOnAxis(idx48i[:, r:r + 1], 0)))(rb),
                    [ix2, tb_ops[-1], tb_ops[-2]] + ([fvops[rb - 2]] if rb >= 2 else []), dma=True, dsem=f"g7{rb % 2}")
                g6ops[rb] = g6
                j1 = C("vector", (lambda r: lambda e: e.scalar_tensor_tensor(
                    j48[:, r:r + 1], g6b[:, r % 2, 0:1], -1.0, rloc48[:, r:r + 1], OP.mult, OP.add))(rb), [g6, rl1])
                f1 = C("vector", (lambda r: lambda e: e.tensor_scalar(
                    dump16f[:], g6b[:, r % 2, 1:17], j48[:, r:r + 1], None, OP.is_le, OP.add,
                    accum_out=fv48[:, r:r + 1]))(rb), [j1])
                fvops[rb] = f1
            tk1 = C("vector", lambda e: e.scalar_tensor_tensor(tok48[:], fv48[:], 128.0, pz48[:], OP.mult, OP.add), fvops[-2:] + [pzm] + fvops[:46])
            sc1_ = C("vector", lambda e: e.tensor_scalar(sct48[:], tok48[:], 2048.0, None, OP.subtract, OP.bypass), [tk1])
            sc2_ = C("vector", lambda e: e.tensor_tensor(sct48[:], sct48[:], validf[:], OP.mult), [sc1_, vl1])
            sc3_ = C("vector", lambda e: e.tensor_scalar(sct48[:], sct48[:], 2048.0, None, OP.add, OP.bypass), [sc2_])
            gi1_ = C("vector", lambda e: e.tensor_tensor(gidx48[:], tok48[:], validf[:], OP.mult), [tk1, vl1])
            gii = C("vector", lambda e: e.tensor_copy(gidx48i[:], gidx48[:]), [gi1_])
            sti = C("vector", lambda e: e.tensor_copy(scti[:], sct48[:]), [sc3_])
            wia = C("vector", lambda e: e.tensor_scalar(idx48[:], erbB[:], 128.0, None, OP.mult, OP.bypass), [eb1, ix2])
            wib = C("vector", lambda e: e.tensor_scalar(idx48[:], idx48[:], iota128[:, 0:1], None, OP.add, OP.bypass), [wia, ld_i128])
            wic = C("vector", lambda e: e.tensor_copy(w1i48[:], idx48[:]), [wib])
            w2a = C("vector", lambda e: e.tensor_scalar(idx48[0:65, :], erbB[0:65, :], 65.0, None, OP.mult, OP.bypass), [wic, eb1])
            w2b_ = C("vector", lambda e: e.tensor_scalar(idx48[0:65, :], idx48[0:65, :], iota128[0:65, 0:1], None, OP.add, OP.bypass), [w2a])
            w2c = C("vector", lambda e: e.tensor_copy(w2i48[:], idx48[0:65, :]), [w2b_])
            ebc = C("vector", lambda e: e.tensor_copy(ebi128[:], erbB[:]), [eb1])

            hro = C("vector", lambda e: e.memset(hrT[64:65, :, :], 1.0))
            zy = C("vector", lambda e: e.memset(ybuf[:, 0, :], 0.0), adrain)
            yz_ops = []
            for r in range(16):
                yz_ops.append(D("gpsimd", (lambda rr: lambda e: e.dma_start(
                    out=y_d[rr * 128:(rr + 1) * 128, :], in_=ybuf[:, 0, :]))(r), [zy], dsem="yz!"))
            yz_ops.append(D("gpsimd", lambda e: e.dma_start(out=y_d[2048:2049, :], in_=ybuf[0:1, 0, :]), [zy], dsem="yz!"))
            yzl = yz_ops[-1]

            fgx_ops = [None] * 48
            ftrx_ops = {}
            fevx_ops = {}
            fhmm_ops = {}
            fhmml = [None] * 48
            frl_ops = [None] * 48
            fba_ops = [None] * 48
            ftrh_ops = [None] * 48
            fevh_ops = [None] * 48
            fymm_ops = {}
            fyev_ops = {}
            fsc_ops = [None] * 48
            fwl_ops = [None] * 48
            fvl_ops = [None] * 48
            fbl_ops = [None] * 48
            for rb in range(48):
                par = rb % 2
                gx = P.add("gpsimd", (lambda r: lambda e: e.indirect_dma_start(
                    xg[:, r % 2, :], None, ag1a_in, IndirectOffsetOnAxis(gidx48i[:, r:r + 1], 0)))(rb),
                    [gii] + (adrain if rb < 2 else [ftrx_ops[(rb - 2, 7)]]), dma=True, dsem=f"fx{par}")
                fgx_ops[rb] = gx
                wl = P.add("gpsimd", (lambda r: lambda e: e.indirect_dma_start(
                    w1b[:, r % 2, :], None, w1s_d.rearrange("e p n -> (e p) n"),
                    IndirectOffsetOnAxis(w1i48[:, r:r + 1], 0)))(rb),
                    [wic] + (adrain if rb < 2 else [fhmml[rb - 2]]), dma=True, dsem=f"fw{par}")
                fwl_ops[rb] = wl
                vl = P.add("gpsimd", (lambda r: lambda e: e.indirect_dma_start(
                    w2b[0:65, r % 2, :], None, w2s_d.rearrange("e p n -> (e p) n"),
                    IndirectOffsetOnAxis(w2i48[:, r:r + 1], 0)))(rb),
                    [w2c] + (adrain if rb < 2 else [fymm_ops[(rb - 2, 1)]]), dma=True, dsem=f"fv{par}")
                fvl_ops[rb] = vl
                gb1 = P.add("gpsimd", (lambda r: lambda e: e.indirect_dma_start(
                    b1B[:, r % 2, :], None, b1f_d, IndirectOffsetOnAxis(ebi128[:, r:r + 1], 0)))(rb),
                    [ebc] + ([] if rb < 2 else [fba_ops[rb - 2]]), dma=True, dsem=f"fb{par}")
                for hc in range(8):
                    bnk = pbk[2] if hc % 2 == 0 else pbk[3]
                    tr = C("tensor", (lambda r, cc, bb: lambda e: e.transpose(
                        bb[:, 0:128], xg[:, r % 2, cc * 128:(cc + 1) * 128], eyef[:]))(rb, hc, bnk),
                        [gx] + ([fevx_ops[(rb, hc - 2)]] if hc >= 2 else
                                ([fevx_ops[(rb - 1, hc + 6)]] if rb >= 1 else [fe8, ctev_ops[15]])))
                    ev = C("scalar", (lambda r, cc, bb: lambda e: e.copy(xgT[:, r % 2, cc, :], bb[:, 0:128]))(rb, hc, bnk),
                           [tr] + ([fhmm_ops[(rb - 2, hc)]] if rb >= 2 else []))
                    ftrx_ops[(rb, hc)] = tr
                    fevx_ops[(rb, hc)] = ev
                for hc in range(8):
                    mm = C("tensor", (lambda r, cc: lambda e: e.matmul(
                        pbk[0][:, 0:64], xgT[:, r % 2, cc, :], w1b[:, r % 2, cc * 64:(cc + 1) * 64],
                        start=(cc == 0), stop=(cc == 7)))(rb, hc),
                        [fevx_ops[(rb, hc)], wl] + ([fba_ops[rb - 1]] if (hc == 0 and rb >= 1) else []))
                    fhmm_ops[(rb, hc)] = mm
                fhmml[rb] = fhmm_ops[(rb, 7)]
                ba = C("vector", (lambda r: lambda e: e.tensor_tensor(hr16[:, r % 2, :], pbk[0][:, 0:64], b1B[:, r % 2, :], OP.add))(rb),
                       [fhmml[rb], gb1] + ([ftrh_ops[rb - 2]] if rb >= 2 else []))
                fba_ops[rb] = ba
                rl = C("scalar", (lambda r: lambda e: e.activation(hr16[:, r % 2, :], hr16[:, r % 2, :], AF.Relu))(rb), [ba])
                frl_ops[rb] = rl
                ftrh = C("tensor", (lambda r: lambda e: e.transpose(
                    ptr3[0:64, 0:128], hr16[:, r % 2, :], eyef[:]))(rb),
                    [rl] + ([fevh_ops[rb - 1]] if rb >= 1 else []))
                ftrh_ops[rb] = ftrh
                evh = C("scalar", (lambda r: lambda e: e.copy(hrT[0:64, r % 2, :], ptr3[0:64, 0:128]))(rb),
                       [ftrh] + ([fymm_ops[(rb - 2, 1)]] if rb >= 2 else []))
                fevh_ops[rb] = evh
                y0 = C("tensor", (lambda r: lambda e: e.matmul(
                    pbk[4][:], hrT[0:65, r % 2, :], w2b[0:65, r % 2, 0:512], start=True, stop=True))(rb),
                    [evh, vl, hro] + ([fyev_ops[(rb - 1, 0)]] if rb >= 1 else []))
                y1 = C("tensor", (lambda r: lambda e: e.matmul(
                    pbk[5][:], hrT[0:65, r % 2, :], w2b[0:65, r % 2, 512:1024], start=True, stop=True))(rb),
                    [evh, vl, hro] + ([fyev_ops[(rb - 1, 1)]] if rb >= 1 else []))
                fymm_ops[(rb, 0)] = y0
                fymm_ops[(rb, 1)] = y1
                ye0 = C("scalar", (lambda r: lambda e: e.copy(ybuf[:, r % 2, 0:512], pbk[4][:]))(rb),
                        [y0] + ([fsc_ops[rb - 2]] if rb >= 2 else [yzl]))
                ye1 = C("scalar", (lambda r: lambda e: e.copy(ybuf[:, r % 2, 512:1024], pbk[5][:]))(rb),
                        [y1] + ([fsc_ops[rb - 2]] if rb >= 2 else [yzl]))
                fyev_ops[(rb, 0)] = ye0
                fyev_ops[(rb, 1)] = ye1
                sc = P.add("gpsimd", (lambda r: lambda e: e.indirect_dma_start(
                    y_d, IndirectOffsetOnAxis(scti[:, r:r + 1], 0), ybuf[:, r % 2, :], None,
                    compute_op=OP.add))(rb),
                    [ye0, ye1, sti, yz_ops[-1]], dma=True, dsem=f"fy{par}")
                fsc_ops[rb] = sc
            fy_sc_ops = fsc_ops

        gidx2 = gidx_d.rearrange("j (r x) -> j r x", x=TC)
        gwin_ops = []
        for j in range(4):
            def mk_gw(jj):
                def f(e):
                    rv = nc.partition_id(engines=[SPENG])
                    return e.dma_start(out=gidxw[:, :, jj], in_=gidx2[jj, rv].rearrange("(i p) -> p i", p=128),
                                      allow_slow_non_contiguous=True)
                return f
            gwin_ops.append(D("sync", mk_gw(j), gidx_wr, dsem="gwin!"))
        gi0 = C("vector", lambda e: e.tensor_copy(gidxi[:, 0, :], gidxw[:, :, 0]), gwin_ops)
        gi1 = C("vector", lambda e: e.tensor_copy(gidxi[:, 1, :], gidxw[:, :, 1]), gwin_ops)
        sout_ops = []
        tapr_ops = [[] for _ in range(NT)]
        for i in range(NT):
            pace_c = ([sout_ops[i - 2]] + tapr_ops[i - 2]) if i >= 2 else []
            cg1 = P.add("gpsimd", (lambda ii: lambda e: e.indirect_dma_start(
                r1g[:, ii % 2, :], None, ag2_out, IndirectOffsetOnAxis(gidxi[:, 0, ii:ii + 1], 0)))(i),
                [ag2_op, gi0] + pace_c, dma=True, dsem=f"cg{i % 2}")
            cg2 = P.add("gpsimd", (lambda ii: lambda e: e.indirect_dma_start(
                r2g[:, ii % 2, :], None, ag2_out, IndirectOffsetOnAxis(gidxi[:, 1, ii:ii + 1], 0)))(i),
                [ag2_op, gi1] + pace_c, dma=True, dsem=f"ch{i % 2}")
            if "tap_r1" in tapd:
                tr1 = D("gpsimd", (lambda ii: lambda e: e.dma_start(out=tapd["tap_r1"][ii * 128:(ii + 1) * 128, :], in_=r1g[:, ii % 2, :]))(i), [cg1], dsem=f"tr{i % 2}")
                tr2 = D("gpsimd", (lambda ii: lambda e: e.dma_start(out=tapd["tap_r2"][ii * 128:(ii + 1) * 128, :], in_=r2g[:, ii % 2, :]))(i), [cg2], dsem=f"ts{i % 2}")
                tapr_ops[i] = [tr1, tr2]
                finals += [tr1, tr2]
            s1_ = C("vector", (lambda ii: lambda e: e.tensor_scalar(stile[:, ii % 2, :], r2g[:, ii % 2, :], gidxw[:, ii:ii + 1, 3], None, OP.mult, OP.bypass))(i),
                    [cg2, gwin_ops[3]] + pace_c)
            s2_ = C("vector", (lambda ii: lambda e: e.scalar_tensor_tensor(stile[:, ii % 2, :], r1g[:, ii % 2, :], gidxw[:, ii:ii + 1, 2], stile[:, ii % 2, :], OP.mult, OP.add))(i),
                    [cg1, s1_, gwin_ops[2]])
            s_ops.append(s2_)
            if milestone >= 3:
                s_target = out_d if milestone == 3 else s_d
                so = D("gpsimd", (lambda ii, tgt: lambda e: e.dma_start(out=tgt.rearrange("(n p) h -> p n h", p=128)[:, ii, :], in_=stile[:, ii % 2, :]))(i, s_target),
                       [s2_], dsem=f"so{i % 2}")
                if "tap_s" in tapd:
                    so2 = D("gpsimd", (lambda ii: lambda e: e.dma_start(out=tapd["tap_s"].rearrange("(n p) h -> p n h", p=128)[:, ii, :], in_=stile[:, ii % 2, :]))(i),
                            [s2_, so], dsem=f"so{i % 2}")
                    sout_ops.append(so2)
                    finals.append(so2)
                else:
                    sout_ops.append(so)
                if milestone == 3:
                    finals.append(so)
                else:
                    finals.append(so)

    if milestone >= 4:
        f4_out = []
        f4_chain = [None] * NT
        for i in range(NT):
            par = i % 2
            pace4 = [f4_chain[i - 2]] if i >= 2 else [sout_ops[-1], sout_ops[-2]]
            lx = D("sync", (lambda ii: lambda e: e.dma_start(out=r1g[:, ii % 2, :], in_=x_sl[ii * 128:(ii + 1) * 128, :]))(i),
                   pace4, dsem=f"c4x{par}")
            ll = D("sync", (lambda ii: lambda e: e.dma_start(out=lnr[:, ii % 2, :], in_=ag1a_in[ii * 128:(ii + 1) * 128, :]))(i),
                   pace4, dsem=f"c4l{par}")
            ly = D("gpsimd", (lambda ii: lambda e: e.dma_start(out=r2g[:, ii % 2, :], in_=y_d[ii * 128:(ii + 1) * 128, :]))(i),
                   [fy_sc_ops[-1], fy_sc_ops[-2]] + pace4, dsem=f"c4y{par}")
            ls = D("sync", (lambda ii: lambda e: e.dma_start(out=stile[:, ii % 2, :], in_=s_d[ii * 128:(ii + 1) * 128, :]))(i),
                   [sout_ops[ii_] for ii_ in (len(sout_ops) - 1, len(sout_ops) - 2)] + pace4, dsem=f"c4s{par}")
            v1 = C("vector", (lambda ii: lambda e: e.tensor_tensor(lnr[:, ii % 2, :], lnr[:, ii % 2, :], r2g[:, ii % 2, :], OP.add))(i),
                   [ll, ly])
            v2 = C("scalar", (lambda ii: lambda e: e.activation(lnr[:, ii % 2, :], lnr[:, ii % 2, :], AF.Silu))(i), [v1])
            v3 = C("vector", (lambda ii: lambda e: e.tensor_tensor(lnr[:, ii % 2, :], lnr[:, ii % 2, :], stile[:, ii % 2, :], OP.mult))(i),
                   [v2, ls])
            v4 = C("vector", (lambda ii: lambda e: e.tensor_tensor(lnr[:, ii % 2, :], lnr[:, ii % 2, :], r1g[:, ii % 2, :], OP.add))(i),
                   [v3, lx])
            o4 = D("gpsimd", (lambda ii: lambda e: e.dma_start(out=out_d[ii * 128:(ii + 1) * 128, :], in_=lnr[:, ii % 2, :]))(i),
                   [v4], dsem=f"o4{par}")
            f4_chain[i] = o4
            f4_out.append(o4)
            finals.append(o4)

    if "tap_ln" in tapd:
        for t in range(NT):
            finals.append(D("gpsimd", (lambda tt: lambda e: e.dma_start(out=tapd["tap_ln"].rearrange("(n p) h -> p n h", p=128)[:, tt, :], in_=ln[:, tt, :]))(t),
                            [ln_ops[t]] + ssq1, dsem="st!"))
    if "tap_logits" in tapd:
        finals.append(D("gpsimd", lambda e: e.dma_start(out=tapd["tap_logits"].rearrange("(n p) k -> p n k", p=128), in_=logit[:]), lg_ops + nm_ops + ex_ops, dsem="st!"))
    if "tap_wsel" in tapd:
        finals.append(D("gpsimd", lambda e: e.dma_start(out=tapd["tap_wsel"].rearrange("(n p) k -> p n k", p=128), in_=wsel[:]), wsel_ops, dsem="st!"))
    if "tap_top" in tapd:
        for arridx, arr in enumerate([e1t, e2t, w1t, w2t]):
            finals.append(D("gpsimd", (lambda a, j: lambda e: e.dma_start(
                out=tapd["tap_top"].rearrange("(n p) k -> p n k", p=128)[:, :, j:j + 1], in_=a[:].rearrange("p (n o) -> p n o", o=1), allow_slow_non_contiguous=True))(arr, arridx),
                wsel_ops, dsem="st!"))

    if "tap_ag1" in tapd and milestone >= 3:
        finals.append(D("gpsimd", lambda e: e.dma_start(out=tapd["tap_ag1"], in_=ag1a_out),
                        toki_ops + ag_dep, dsem="ta!"))
    if "tap_ag2" in tapd and milestone >= 3:
        finals.append(D("gpsimd", lambda e: e.dma_start(out=tapd["tap_ag2"], in_=ag2_in), ag2w_ops[-2:], dsem="st!"))
    if "tap_q1" in tapd and milestone >= 3:
        finals.append(D("gpsimd", lambda e: e.dma_start(out=tapd["tap_q1"], in_=q1s_d),
                        [q1w_ops[3], q1w_ops[7], q1w_ops[11], q1w_ops[15]], dsem="st!"))
    if "tap_q2" in tapd and milestone >= 3:
        finals.append(D("gpsimd", lambda e: e.dma_start(out=tapd["tap_q2"], in_=q2s_d),
                        [q2w_ops[g * 2 + 1] for g in range(8)], dsem="st!"))
    if "tap_ag2o" in tapd and milestone >= 3:
        finals.append(D("gpsimd", lambda e: e.dma_start(out=tapd["tap_ag2o"], in_=ag2_out), [ag2_op], dsem="to!"))
    if "tap_csc" in tapd and milestone >= 3:
        finals.append(D("gpsimd", lambda e: e.dma_start(out=tapd["tap_csc"], in_=csc_dbg[:]), [csc_ops[15]], dsem="st!"))
    if "tap_fl2" in tapd and milestone >= 4:
        finals.append(D("gpsimd", lambda e: e.dma_start(out=tapd["tap_fl2"].rearrange("(n p) k -> p n k", p=128), in_=l2t[:]), rt_done, dsem="st!"))
    if "tap_fslot" in tapd and milestone >= 4:
        for nm, buf, dep in [(0, tok48, None), (1, validf, None), (2, sct48, None), (3, pz48, None)]:
            finals.append(D("gpsimd", (lambda nn, bb: lambda e: e.dma_start(out=tapd["tap_fslot"][:, nn * 48:(nn + 1) * 48], in_=bb[:]))(nm, buf),
                            [sti, gii], dsem="st!"))
    if "tap_y" in tapd and milestone >= 4:
        finals.append(D("gpsimd", lambda e: e.dma_start(out=tapd["tap_y"], in_=y_d), [fy_sc_ops[-1], fy_sc_ops[-2]], dsem="st!"))
    if "tap_gw" in tapd and milestone >= 3:
        finals.append(D("gpsimd", lambda e: e.dma_start(out=tapd["tap_gw"], in_=gidxw[:].rearrange("p i j -> p (i j)")), gwin_ops, dsem="st!"))
    if "tap_tok" in tapd and milestone >= 2:
        finals.append(D("gpsimd", lambda e: e.dma_start(out=tapd["tap_tok"], in_=tokf[:]), toki_ops, dsem="st!"))
    if "tap_gidx" in tapd and milestone >= 2:
        finals.append(D("gpsimd", lambda e: e.dma_start(out=tapd["tap_gidx"], in_=gidx_d), gidx_wr, dsem="st!"))
    if milestone in (1, 2):
        for t in range(NT):
            finals.append(D("gpsimd", (lambda tt: lambda e: e.dma_start(out=out_d.rearrange("(n p) h -> p n h", p=128)[:, tt, :], in_=ln[:, tt, :]))(t),
                            ([ln_ops[t]] if len(ln_ops) > t else []) + ssq1 + wsel_ops + routing_done, dsem="st!"))

    P.emit(nc, None, finals)
    es.close()
    return nc


def make_inputs(inputs):
    x = np.asarray(inputs["L0_attn_x"], np.float32).reshape(T, H)
    gnw = np.asarray(inputs["L0_attn_gate_norm_w"], np.float32).reshape(1, H)
    ginw = np.asarray(inputs["L0_attn_gate_in_norm_w"], np.float32).reshape(1, H)
    gwT = np.ascontiguousarray(np.asarray(inputs["L0_attn_gate_w"], np.float32).T)
    eyebf = np.eye(128, dtype=np.float32).astype(ml_dtypes.bfloat16)
    eyef = np.eye(128, dtype=np.float32)
    iota8 = np.broadcast_to(np.arange(EF, dtype=np.float32), (128, EF)).copy()
    W1 = np.asarray(inputs["L0_ffn_W1"], np.float32)
    W2 = np.asarray(inputs["L0_ffn_W2"], np.float32)
    b1 = np.asarray(inputs["L0_ffn_b1"], np.float32)
    b2 = np.asarray(inputs["L0_ffn_b2"], np.float32)
    Wr = np.asarray(inputs["L0_ffn_Wr"], np.float32)
    br = np.asarray(inputs["L0_ffn_br"], np.float32)
    w1s = np.ascontiguousarray(W1.reshape(EF, 8, 128, 64).transpose(0, 2, 1, 3).reshape(EF, 128, 512))
    w2s = np.ascontiguousarray(np.concatenate([W2, b2[:, None, :]], axis=1))
    wrs = np.ascontiguousarray(Wr.reshape(8, 128, EF).transpose(1, 0, 2))
    brr = np.broadcast_to(br.reshape(1, EF), (128, EF)).copy()
    iotac16 = np.arange(EF, dtype=np.float32).reshape(EF, 1)
    rb128 = np.broadcast_to((128.0 * np.arange(48, dtype=np.float32)).reshape(1, 48), (EF, 48)).copy()
    rbp = (np.arange(48, dtype=np.float32) * 128.0)[None, :] + np.arange(128, dtype=np.float32)[:, None]
    rbp = np.ascontiguousarray(rbp)
    maps = []
    for c in range(NCORES):
        maps.append({
            "x_sl": np.ascontiguousarray(x[c * TC:(c + 1) * TC]),
            "gnw": np.broadcast_to(gnw, (128, K_H)).copy() if False else np.broadcast_to(gnw, (128, H)).copy(), "ginw": np.broadcast_to(ginw, (128, H)).copy(), "gwT": gwT,
            "eyebf": eyebf, "eyef": eyef, "iota8": iota8,
            "iota128": np.arange(128, dtype=np.float32).reshape(128, 1),
            "gpn": np.broadcast_to(np.asarray(inputs["L0_attn_gp_norm_w"], np.float32)[c].reshape(1, H), (128, H)).copy(),
            "dpn": np.broadcast_to(np.asarray(inputs["L0_attn_dp_norm_w"], np.float32)[c].reshape(1, II), (128, II)).copy(),
            "gpwT": np.ascontiguousarray(np.asarray(inputs["L0_attn_gp_w"], np.float32)[c].T),
            "dpwT": np.ascontiguousarray(np.asarray(inputs["L0_attn_dp_w"], np.float32)[c].T),
            "myhot": np.eye(NCORES, dtype=np.float32)[c].reshape(8, 1),
            "myhotr": np.eye(NCORES, dtype=np.float32)[c].reshape(1, 8),
            "w1s": w1s, "w2s": w2s, "b1f": np.ascontiguousarray(b1), "wrs": wrs, "brr": brr,
            "iotac16": iotac16, "rb128": rb128, "rbp": rbp,
        })
    return maps


def kernel(**inputs):
    nc = build_program()
    maps = make_inputs(inputs)
    res = run_bass_kernel_spmd(nc, maps, core_ids=list(range(NCORES)))
    out = np.concatenate([res.results[c]["out"] for c in range(NCORES)], axis=0)
    return out.reshape(4, 4096, 1024)


# version 08
# speedup vs baseline: 1.0175x; 1.0175x over previous
import contextlib
import numpy as np
import ml_dtypes
import concourse.bass as bass
import concourse.mybir as mybir
from concourse.bass_utils import run_bass_kernel_spmd
from concourse.bass_core_rust import IndirectOffsetOnAxis

F32 = mybir.dt.float32
BF16 = mybir.dt.bfloat16
I32 = mybir.dt.int32
U32 = mybir.dt.uint32
OP = mybir.AluOpType
AX = mybir.AxisListType.X
AF = mybir.ActivationFunctionType

MAGIC = float(np.float32(1.5 * 2**23))
NCORES = 8
T = 16384
TC = 2048
NT = TC // 128
H = 1024
HC = H // 128
I2 = 4096
II = 2048
IIC = II // 128
E = 8
CAP = 2048
EF = 16
FCAP = 512

MILESTONE = 4
M1CUT = 99
M2CUT = 99
SKIP_WR = False
SKIP_WT = False
DEBUG_TAPS = ()
DELAY_AG = 0


class Prog:
    ENGINES = ("sync", "vector", "scalar", "tensor", "gpsimd")

    def __init__(self):
        self.ops = []
        self.by_engine = {e: [] for e in self.ENGINES}

    def add(self, engine, fn, deps=(), dma=False, tag="", coll=False, dsem=None):
        deps = [d for d in deps if d is not None]
        op = {"e": engine, "fn": fn, "deps": deps, "dma": dma, "coll": coll,
              "id": len(self.ops), "need_inc": False, "tag": tag,
              "dsem": (dsem or "d") if dma else None}
        self.ops.append(op)
        self.by_engine[engine].append(op)
        return op["id"]

    def check_schedule(self):
        bykey = {}
        for op in self.ops:
            if op["dma"] and op["dsem"].endswith("!"):
                bykey.setdefault((op["e"], op["dsem"]), []).append(op["id"])
        exdeps = []
        for op in self.ops:
            dd = set()
            for d in op["deps"]:
                dop = self.ops[d]
                if dop["dma"] and dop["dsem"].endswith("!"):
                    dd.update(bykey[(dop["e"], dop["dsem"])])
                else:
                    dd.add(d)
            exdeps.append(dd)
        self._exdeps = exdeps
        pos = {e: 0 for e in self.ENGINES}
        done = [False] * len(self.ops)
        progress = True
        while progress:
            progress = False
            for e in self.ENGINES:
                while pos[e] < len(self.by_engine[e]):
                    op = self.by_engine[e][pos[e]]
                    if all(done[d] for d in self._exdeps[op["id"]]):
                        done[op["id"]] = True
                        pos[e] += 1
                        progress = True
                    else:
                        break
        stuck = {e: pos[e] for e in self.ENGINES if pos[e] < len(self.by_engine[e])}
        if stuck:
            msg = []
            for e, p in stuck.items():
                op = self.by_engine[e][p]
                blocking = [d for d in self._exdeps[op["id"]] if not done[d]]
                bl = [(self.ops[d]["e"], self.ops[d]["tag"], d) for d in blocking[:4]]
                msg.append(f"engine {e} stuck at op id={op['id']} tag={op['tag']!r} waiting {bl}")
            raise RuntimeError("SCHEDULE DEADLOCK:\n" + "\n".join(msg))

    def emit(self, nc, block, final_ids):
        self.check_schedule()
        for op in self.ops:
            if op["dma"] or op["coll"]:
                op["need_inc"] = True
            for d in op["deps"]:
                self.ops[d]["need_inc"] = True
        for d in final_ids:
            self.ops[d]["need_inc"] = True
        ncoll = 0
        dkey_count = {}
        for e in self.ENGINES:
            cexec = 0
            for op in self.by_engine[e]:
                if op["coll"]:
                    op["semval"] = 1
                    op["collidx"] = ncoll
                    ncoll += 1
                elif op["dma"]:
                    key = (e, op["dsem"])
                    dkey_count[key] = dkey_count.get(key, 0) + 16
                    op["semval"] = dkey_count[key]
                elif op["need_inc"]:
                    cexec += 1
                    op["semval"] = cexec
                else:
                    op["semval"] = None
        self.ncoll = ncoll
        self.dkey_total = dict(dkey_count)
        es = self._es
        sems = {}
        for e in self.ENGINES:
            sems[(e, "x")] = es.enter_context(nc.semaphore(f"sx_{e}"))
        for key in dkey_count:
            sems[key] = es.enter_context(nc.semaphore(f"sd_{key[0]}_{key[1].rstrip(chr(33))}"))
        for i in range(self.ncoll):
            sems[("coll", i)] = es.enter_context(nc.semaphore(f"sc_{i}"))
        self.sems = sems

        prog = self

        def dep_key_val(dop):
            if dop["coll"]:
                return ("coll", dop["collidx"]), 1
            if dop["dma"]:
                key = (dop["e"], dop["dsem"])
                if dop["dsem"].endswith("!"):
                    return key, prog.dkey_total[key]
                return key, dop["semval"]
            return (dop["e"], "x"), dop["semval"]

        def run_engine(eng_obj, ename):
            known = {}
            for op in prog.by_engine[ename]:
                needs = {}
                for d in op["deps"]:
                    key, v = dep_key_val(prog.ops[d])
                    if needs.get(key, 0) < v:
                        needs[key] = v
                for key, v in sorted(needs.items(), key=lambda kv: str(kv[0])):
                    if known.get(key, 0) >= v:
                        continue
                    eng_obj.wait_ge(sems[key], v)
                    known[key] = v
                ins = op["fn"](eng_obj)
                if op["coll"]:
                    ins.then_inc(sems[("coll", op["collidx"])], 1)
                elif op["dma"]:
                    ins.then_inc(sems[(op["e"], op["dsem"])], 16)
                elif op["need_inc"]:
                    ins.then_inc(sems[(op["e"], "x")], 1)

        with nc.Block() as block2:
            block2.sync(lambda s_: run_engine(s_, "sync"))
            block2.vector(lambda v_: run_engine(v_, "vector"))
            block2.scalar(lambda s_: run_engine(s_, "scalar"))
            block2.tensor(lambda t_: run_engine(t_, "tensor"))

            def gps(g_):
                run_engine(g_, "gpsimd")
                known = {}
                for d in final_ids:
                    key, v = dep_key_val(prog.ops[d])
                    if known.get(key, 0) < v:
                        known[key] = v
                for key, v in sorted(known.items(), key=lambda kv: str(kv[0])):
                    g_.wait_ge(self.sems[key], v)

            block2.gpsimd(gps)


def build_program(milestone=MILESTONE, debug_taps=DEBUG_TAPS):
    nc = bass.Bass("TRN2", target_bir_lowering=False)
    P = Prog()
    es = contextlib.ExitStack()
    P._es = es

    def sb(name, shape, dtype):
        return es.enter_context(nc.sbuf_tensor(name, shape, dtype))

    def psum(name, shape, dtype=F32):
        return es.enter_context(nc.psum_tensor(name, shape, dtype))

    x_sl = nc.dram_tensor("x_sl", [TC, H], F32, kind="ExternalInput").ap()
    gnw_d = nc.dram_tensor("gnw", [128, H], F32, kind="ExternalInput").ap()
    ginw_d = nc.dram_tensor("ginw", [128, H], F32, kind="ExternalInput").ap()
    gwT_d = nc.dram_tensor("gwT", [H, E], F32, kind="ExternalInput").ap()
    eyebf_d = nc.dram_tensor("eyebf", [128, 128], BF16, kind="ExternalInput").ap()
    eyef_d = nc.dram_tensor("eyef", [128, 128], F32, kind="ExternalInput").ap()
    iota8_d = nc.dram_tensor("iota8", [128, EF], F32, kind="ExternalInput").ap()
    iota128_d = nc.dram_tensor("iota128", [128, 1], F32, kind="ExternalInput").ap()
    myhot_d = nc.dram_tensor("myhot", [8, 1], F32, kind="ExternalInput").ap()
    myhotr_d = nc.dram_tensor("myhotr", [1, 8], F32, kind="ExternalInput").ap()
    gpn_d = nc.dram_tensor("gpn", [128, H], F32, kind="ExternalInput").ap()
    dpn_d = nc.dram_tensor("dpn", [128, II], F32, kind="ExternalInput").ap()
    gpwT_d = nc.dram_tensor("gpwT", [H, I2], F32, kind="ExternalInput").ap()
    dpwT_d = nc.dram_tensor("dpwT", [II, H], F32, kind="ExternalInput").ap()

    out_d = nc.dram_tensor("out", [TC, H], F32, kind="ExternalOutput").ap()
    w1s_d = nc.dram_tensor("w1s", [EF, 128, 512], F32, kind="ExternalInput").ap()
    w2s_d = nc.dram_tensor("w2s", [EF, 65, H], F32, kind="ExternalInput").ap()
    b1f_d = nc.dram_tensor("b1f", [EF, 64], F32, kind="ExternalInput").ap()
    wr_d = nc.dram_tensor("wrs", [128, 8, EF], F32, kind="ExternalInput").ap()
    brr_d = nc.dram_tensor("brr", [128, EF], F32, kind="ExternalInput").ap()
    iotac16_d = nc.dram_tensor("iotac16", [EF, 1], F32, kind="ExternalInput").ap()
    rb128_d = nc.dram_tensor("rb128", [EF, 48], F32, kind="ExternalInput").ap()
    rbp_d = nc.dram_tensor("rbp", [128, 48], F32, kind="ExternalInput").ap()
    y_d = nc.dram_tensor("y_d", [2049, H], F32).ap()
    s_d = nc.dram_tensor("s_d", [TC, H], F32).ap()
    tb16_d = nc.dram_tensor("tb16_d", [2048, 17], F32).ap()
    kofT_d = nc.dram_tensor("kofT_d", [EF, 128], F32).ap()
    BLKB = TC * E + E * TC
    ag1a_in = nc.dram_tensor("ag1a_in", [TC, H], F32).ap()
    ag1a_out = nc.dram_tensor("ag1a_out", [T, H], F32).ap()
    ag1b_in_f = nc.dram_tensor("ag1b_in", [BLKB], F32).ap()
    ag1b_out_f = nc.dram_tensor("ag1b_out", [NCORES, BLKB], F32).ap()
    ag1w_in = ag1b_in_f[0:TC * E].rearrange("(r c) -> r c", c=E)
    ag1b_in = ag1b_in_f[TC * E:BLKB].rearrange("(e t) -> e t", t=TC)
    tbl_d = nc.dram_tensor("tbl_d", [128, 132], F32).ap()
    q1s_d = nc.dram_tensor("q1s_d", [CAP, H], BF16).ap()
    q2s_d = nc.dram_tensor("q2s_d", [CAP, II], BF16).ap()
    ag2_in = nc.dram_tensor("ag2_in", [CAP, H], F32).ap()
    ag2_out = nc.dram_tensor("ag2_out", [T, H], F32).ap()
    gidx_d = nc.dram_tensor("gidx_d", [4, T], F32).ap()
    tapd = {}
    for name, shape in [("tap_ln", [TC, H]), ("tap_logits", [TC, E]),
                        ("tap_wsel", [TC, E]), ("tap_top", [TC, 4]),
                        ("tap_tok", [128, 16]), ("tap_gidx", [4, T]),
                        ("tap_s", [TC, H]), ("tap_ag2", [CAP, H]),
                        ("tap_xin", [CAP, H]), ("tap_tok2", [128, 16]),
                        ("tap_q1", [CAP, H]), ("tap_q2", [CAP, II]),
                        ("tap_ag1", [T, H]), ("tap_gw", [128, 64]),
                        ("tap_r1", [TC, H]), ("tap_r2", [TC, H]),
                        ("tap_ag2o", [T, H]), ("tap_wtk", [128, 16]),
                        ("tap_csc", [128, 16]), ("tap_fl2", [TC, EF]),
                        ("tap_fslot", [128, 192]), ("tap_y", [2049, H])]:
        if name in debug_taps:
            tapd[name] = nc.dram_tensor(name, shape, F32, kind="ExternalOutput").ap()

    ARENA = 153 * 1024
    _ar0, _ar1 = nc.bump_sbuf(ARENA)

    def A(name, shape, dtype, off):
        return nc.alloc_sbuf_tensor_at(name, shape, dtype, offset=_ar0 + off)

    ln = A("ln", [128, NT, H], F32, 0)
    zz = A("zz", [128, NT, H], F32, 64 * 1024)
    xrot = A("xrot", [128, 2, H], F32, 128 * 1024)
    qbrot = A("qbrot", [128, 2, H], BF16, 136 * 1024)
    qtrot = A("qtrot", [128, 2, HC, 128], BF16, 140 * 1024)
    glog8 = A("glog8", [8, TC], F32, 144 * 1024)
    wselT = A("wselT", [E, TC], F32, 144 * 1024)
    gp_wq = A("gp_wq", [128, HC, I2], BF16, 0)
    dp_wq = A("dp_wq", [128, IIC, H], BF16, 64 * 1024)
    wrot = A("wrot", [128, 2, 4096], F32, 96 * 1024)
    wcol8 = A("wcol8", [128, E, 128], F32, 64 * 1024)
    R8 = A("R8", [128, E, 128], F32, 68 * 1024)
    keep8 = A("keep8", [128, E, 128], F32, 72 * 1024)
    pctmp = A("pctmp", [128, 2, 128], F32, 76 * 1024)
    natmp = A("natmp", [128, 128], F32, 77 * 1024)
    getmp = A("getmp", [128, 128], F32, 77 * 1024 + 512)
    gAB = A("gAB", [128, 4, 128], F32, 78 * 1024)
    koffB = A("koffB", [128, 128], F32, 80 * 1024)
    Rmy = A("Rmy", [128, 128], F32, 80 * 1024 + 512)
    g2buf = A("g2buf", [128, 16, 132], F32, 81 * 1024)
    xinr = A("xinr", [128, 2, H], F32, 96 * 1024)
    p1r = A("p1r", [128, 1, H], F32, 104 * 1024)
    q1r = A("q1r", [128, 4, H], BF16, 108 * 1024)
    q1T = A("q1T", [128, HC, 512], BF16, 116 * 1024)
    ubuf = A("ubuf", [128, II], F32, 124 * 1024)
    sgb = A("sgb", [128, 512], F32, 132 * 1024)
    q2r = A("q2r", [128, 2, II], BF16, 134 * 1024)
    q2T = A("q2T", [128, IIC, 256], BF16, 142 * 1024)
    ctb = ubuf
    lnrot = A("lnrot", [128, 1, H], F32, 128 * 1024)
    lnTb = A("lnTb", [128, 2, 128], F32, 132 * 1024)
    xg = A("xg", [128, 2, H], F32, 64 * 1024)
    xgT = A("xgT", [128, 2, 8, 128], F32, 72 * 1024)
    w1b = A("w1b", [128, 2, 512], F32, 80 * 1024)
    ybuf = A("ybuf", [128, 2, H], F32, 84 * 1024)
    w2b = A("w2b", [65, 2, H], F32, 128 * 1024)
    r1g = A("r1g", [128, 2, H], F32, 96 * 1024)
    r2g = A("r2g", [128, 2, H], F32, 104 * 1024)
    lnr = A("lnr", [128, 2, H], F32, 112 * 1024)
    stile = A("stile", [128, 2, H], F32, 120 * 1024)
    gidxw = A("gidxw", [128, 16, 4], F32, 150 * 1024)
    gidxi = A("gidxi", [128, 2, 16], I32, 151 * 1024)
    gnw = sb("gnw_s", [128, H], F32)
    ginw = sb("ginw_s", [128, H], F32)
    gpn = sb("gpn_s", [128, H], F32)
    dpn = sb("dpn_s", [128, II], F32)
    radj = sb("radj", [128, 16], F32)
    tok2i = sb("tok2i", [128, 16], I32)
    wtr = sb("wtr", [128, 2, E], F32)
    wsum = sb("wsum", [128, 4], F32)
    wqsc = sb("wqsc", [128, 4], F32)
    exsc = sb("exsc", [128, 64], F32)
    wtk_dbg = sb("wtk_dbg", [128, 16], F32)
    csc_dbg = sb("csc_dbg", [128, 16], F32)
    l2t = sb("l2t", [128, NT, EF], F32)
    wrouter = sb("wrouter", [128, 8, EF], F32)
    brrep = sb("brrep", [128, EF], F32)
    e1f16 = sb("e1f16", [128, NT], F32)
    e2f16 = sb("e2f16", [128, NT], F32)
    i1u16 = sb("i1u16", [128, 8], mybir.dt.uint32)
    wx16 = sb("wx16", [128, 8], F32)
    mx16 = sb("mx16", [128, EF], F32)
    msk16 = sb("msk16", [128, EF], F32)
    me16 = sb("me16", [128, EF, NT], F32)
    R16 = sb("R16", [128, EF, NT], F32)
    ktot16 = sb("ktot16", [128, EF], F32)
    koff16 = sb("koff16", [128, EF], F32)
    ktT = sb("ktT", [EF, 128], F32)
    kscT = sb("kscT", [EF, 128], F32)
    koffTx16 = sb("koffTx16", [EF, 128], F32)
    cntT16 = sb("cntT16", [EF, 1], F32)
    pcT16 = sb("pcT16", [EF, 1], F32)
    pc1r = sb("pc1r", [1, EF], F32)
    sc1r = sb("sc1r", [1, EF], F32)
    off1r = sb("off1r", [1, EF], F32)
    offT16 = sb("offT16", [EF, 1], F32)
    endT16 = sb("endT16", [EF, 1], F32)
    erbhot = sb("erbhot", [EF, 48], F32)
    eh2 = sb("eh2", [EF, 48], F32)
    erow = sb("erow", [1, 48], F32)
    cntrow = sb("cntrow", [1, 48], F32)
    offrow = sb("offrow", [1, 48], F32)
    erbB = sb("erbB", [128, 48], F32)
    cntB = sb("cntB", [128, 48], F32)
    offB = sb("offB", [128, 48], F32)
    rloc48 = sb("rloc48", [128, 48], F32)
    validf = sb("validf", [128, 48], F32)
    rlT = sb("rlT", [48, 128], F32)
    pzT48 = sb("pzT48", [48, 128], F32)
    pz48 = sb("pz48", [128, 48], F32)
    fv48 = sb("fv48", [128, 48], F32)
    j48 = sb("j48", [128, 48], F32)
    tok48 = sb("tok48", [128, 48], F32)
    sct48 = sb("sct48", [128, 48], F32)
    gidx48 = sb("gidx48", [128, 48], F32)
    idx48 = sb("idx48", [128, 48], F32)
    idx48i = sb("idx48i", [128, 48], I32)
    w1i48 = sb("w1i48", [128, 48], I32)
    ebi128 = sb("ebi128", [128, 48], I32)
    b1B = sb("b1B", [128, 2, 64], F32)
    w2i48 = sb("w2i48", [65, 48], I32)
    gidx48i = sb("gidx48i", [128, 48], I32)
    scti = sb("scti", [128, 48], I32)
    er48 = sb("er48", [48, 1], F32)
    erb48i = sb("erb48i", [48, 1], I32)
    erbi1 = sb("erbi1", [1, 48], I32)
    koffg = sb("koffg", [48, 128], F32)
    dump48 = sb("dump48", [48, 128], F32)
    dump16f = sb("dump16f", [128, EF], F32)
    dumpa16 = sb("dumpa16", [128, NT], F32)
    tbs16 = sb("tbs16", [128, 2, 17], F32)
    g6b = sb("g6b", [128, 2, 17], F32)
    rb128s = sb("rb128s", [EF, 48], F32)
    rbps = sb("rbps", [128, 48], F32)
    iotac16 = sb("iotac16_s", [EF, 1], F32)
    hr16 = sb("hr16", [128, 2, 64], F32)
    hrT = sb("hrT", [65, 2, 128], F32)
    gwT = sb("gwT_s", [128, HC * E], F32)
    gwq = sb("gwq_s", [128, HC, E], BF16)
    eyebf = sb("eyebf_s", [128, 128], BF16)
    eyef = sb("eyef_s", [128, 128], F32)
    iota8 = sb("iota8_s", [128, EF], F32)
    ones8 = sb("ones8_s", [128, EF], F32)

    epsc = sb("epsc", [128, 4], F32)
    iota128 = sb("iota128_s", [128, 1], F32)
    myhot = sb("myhot_s", [8, 1], F32)
    myhotr = sb("myhotr_s", [1, 8], F32)
    mflag = sb("mflag_s", [128, 8], F32)
    ktot8 = sb("ktot8", [128, E], F32)
    koffT = sb("koffT", [8, 128], F32)
    koffTx = sb("koffTx", [8, 128], F32)
    koff8 = sb("koff8", [128, E], F32)
    koffmy = sb("koffmy", [128, 1], F32)
    koffrow = sb("koffrow", [1, 128], F32)
    tbl = sb("tbl", [128, 132], F32)
    sval = sb("sval", [128, 16], F32)
    cnts = sb("cnts", [128, 16], F32)
    pz = sb("pz", [128, 16], F32)
    pzi = sb("pzi", [128, 16], I32)
    jv = sb("jv", [128, 16], F32)
    fv = sb("fv", [128, 16], F32)
    tokf = sb("tokf", [128, 16], F32)
    toki = sb("toki", [128, 16], I32)
    dump8 = sb("dump8", [128, 128], F32)
    st_sum = sb("st_sum", [128, NT], F32)
    st_ssq = sb("st_ssq", [128, NT], F32)
    st_mean = sb("st_mean", [128, NT], F32)
    st_rstd = sb("st_rstd", [128, NT], F32)
    st_rs1 = sb("st_rs1", [128, NT], F32)
    st_rs2 = sb("st_rs2", [128, NT], F32)
    st_mg = sb("st_mg", [128, NT], F32)
    st_sg = sb("st_sg", [128, NT], F32)
    st_dqg = sb("st_dqg", [128, NT], F32)
    gsc = sb("gsc", [1, 4], F32)
    gsc_bc = sb("gsc_bc", [128, 2], F32)
    grow = sb("grow", [1, 128], F32)
    logit = sb("logit", [128, NT, E], F32)
    exs = sb("exs", [128, NT, E], F32)
    probs = sb("probs", [128, NT, E], F32)
    masked = sb("masked", [128, NT, E], F32)
    negmx = sb("negmx", [128, NT], F32)
    sumex = sb("sumex", [128, NT], F32)
    rsum = sb("rsum", [128, NT], F32)
    w1t = sb("w1t", [128, NT], F32)
    w2t = sb("w2t", [128, NT], F32)
    e1t = sb("e1t", [128, NT], F32)
    e2t = sb("e2t", [128, NT], F32)
    i1u = sb("i1u", [128, E], U32)
    i2u = sb("i2u", [128, E], U32)
    wx8 = sb("wx8", [128, E], F32)
    wsel = sb("wsel", [128, NT, E], F32)
    dump = sb("dump", [128, H], F32)

    pbk = [psum(f"pbk{i}", [128, 512], F32) for i in range(6)]
    ptrb = psum("ptrb", [128, 1024], BF16)
    ptrx = psum("ptrx", [128, 512], F32)
    ptrf = ptrx[:, 0:128]
    ptr2 = ptrx[:, 128:256]
    ptr3 = ptrx[:, 256:384]
    ptr1 = ptrf

    def D(engine, fn, deps=(), tag="", dsem=None):
        return P.add(engine, fn, deps, dma=True, tag=tag, dsem=dsem)

    def C(engine, fn, deps=(), tag=""):
        return P.add(engine, fn, deps, dma=False, tag=tag)

    ld_gnw = D("sync", lambda e: e.dma_start(out=gnw[:], in_=gnw_d), dsem="ld0!")
    ld_ginw = D("sync", lambda e: e.dma_start(out=ginw[:], in_=ginw_d), dsem="ld0!")
    ld_gw = D("sync", lambda e: e.dma_start(out=gwT[:].rearrange("p (c n) -> p c n", c=HC), in_=gwT_d.rearrange("(c p) n -> p c n", p=128)), dsem="ld0!")
    ld_eb = D("sync", lambda e: e.dma_start(out=eyebf[:], in_=eyebf_d), dsem="ld0!")
    ld_ef = D("sync", lambda e: e.dma_start(out=eyef[:], in_=eyef_d), dsem="ld0!")
    ld_io = D("sync", lambda e: e.dma_start(out=iota8[:], in_=iota8_d), dsem="ld0!")
    ones_op = C("vector", lambda e: e.memset(ones8[:], 1.0))
    eps_op0 = C("vector", lambda e: e.memset(epsc[:, 0:1], 1e-5))
    eps_op1 = C("vector", lambda e: e.memset(epsc[:, 1:2], 1e-6))
    eps_op2 = C("vector", lambda e: e.memset(epsc[:, 2:3], 1e-8))
    eps_ops = [eps_op0, eps_op1, eps_op2]
    onesrow = sb("onesrow", [1, 128], F32)
    onesrow_op = C("vector", lambda e: e.memset(onesrow[:], 1.0))

    x_tiled = x_sl.rearrange("(n p) h -> p n h", p=128)
    ld_x = []
    sub_ops = []
    ln_ops = []
    ssq_ops = []
    for t in range(NT if M1CUT >= 1 else 0):
        pace = [sub_ops[t - 2]] if t >= 2 else []
        ld = D("sync", (lambda tt: lambda e: e.dma_start(out=xrot[:, tt % 2, :], in_=x_tiled[:, tt, :]))(t), pace, dsem=f"x{t % 2}")
        ld_x.append(ld)
        r = C("vector", (lambda tt: lambda e: e.tensor_reduce(st_sum[:, tt:tt + 1], xrot[:, tt % 2, :], axis=AX, op=OP.add))(t), [ld])
        mn = C("vector", (lambda tt: lambda e: e.tensor_scalar(st_mean[:, tt:tt + 1], st_sum[:, tt:tt + 1], 1.0 / 1024.0, None, OP.mult, OP.bypass))(t), [r])
        s = C("vector", (lambda tt: lambda e: e.tensor_scalar(ln[:, tt, :], xrot[:, tt % 2, :], st_mean[:, tt:tt + 1], None, OP.subtract, OP.bypass))(t),
              [mn, ld])
        sub_ops.append(s)
    for t in range(NT if M1CUT >= 1 else 0):
        q = C("scalar", (lambda tt: lambda e: e.activation(dump[:], ln[:, tt, :], AF.Square, accum_out=st_ssq[:, tt:tt + 1]))(t), [sub_ops[t]])
        ssq_ops.append(q)
    sqrt_op = C("scalar", lambda e: e.activation(st_rstd[:], st_ssq[:], AF.Sqrt, bias=epsc[:, 0:1], scale=1.0 / 1024.0), ssq_ops + eps_ops) if M1CUT >= 1 else None
    rcp_op = C("vector", lambda e: e.reciprocal(st_rstd[:], st_rstd[:]), [sqrt_op]) if M1CUT >= 1 else None
    for t in range(NT if M1CUT >= 1 else 0):
        o = C("vector", (lambda tt: lambda e: e.tensor_scalar(ln[:, tt, :], ln[:, tt, :], st_rstd[:, tt:tt + 1], None, OP.mult, OP.bypass))(t),
              [rcp_op, sub_ops[t]])
        ln_ops.append(o)

    ssq1 = []
    wsel_ops = []
    lg_ops = []
    if M1CUT >= 2:
      gq0 = C("vector", lambda e: e.tensor_reduce(gsc_bc[:, 0:1], gwT[:], axis=AX, op=OP.add, apply_absolute_value=True), [ld_gw])
      gq1 = C("tensor", lambda e: e.transpose(ptr1[0:1, 0:128], gsc_bc[:, 0:1], eyef[:]), [gq0, ld_ef])
      gq2 = C("scalar", lambda e: e.copy(grow[:], ptr1[0:1, 0:128]), [gq1])
      gq3 = C("vector", lambda e: e.tensor_reduce(gsc[:, 0:1], grow[:], axis=AX, op=OP.add), [gq2])
      gq4 = C("vector", lambda e: e.tensor_scalar(gsc[:, 3:4], gsc[:, 0:1], 1.0 / float(H * E), 1e-5, OP.mult, OP.max), [gq3])
      gq5 = C("vector", lambda e: e.reciprocal(gsc[:, 2:3], gsc[:, 3:4]), [gq4])
      gq6a = C("tensor", lambda e: e.matmul(ptr1[:, 0:2], onesrow[:], gsc[0:1, 2:4], start=True, stop=True), [gq5, onesrow_op, gq2])
      gq6 = C("scalar", lambda e: e.copy(gsc_bc[:, 0:2], ptr1[:, 0:2]), [gq6a])
      gq7 = C("scalar", lambda e: e.activation(gwT[:], gwT[:], AF.Copy, bias=MAGIC, scale=gsc_bc[:, 0:1]), [gq6, gq0])
      gq8 = C("vector", lambda e: e.tensor_scalar(gwT[:], gwT[:], MAGIC, -1.0, OP.subtract, OP.max), [gq7])
      gq9 = C("vector", lambda e: e.tensor_scalar(gwq[:].rearrange("p c n -> p (c n)"), gwT[:], 1.0, None, OP.min, OP.bypass), [gq8])

      
      for t in range(NT):
          q = C("scalar", (lambda tt: lambda e: e.activation(dump[:], ln[:, tt, :], AF.Square, accum_out=st_ssq[:, tt:tt + 1]))(t), [ln_ops[t]])
          ssq1.append(q)
      sq1 = C("scalar", lambda e: e.activation(st_rs1[:], st_ssq[:], AF.Sqrt, bias=epsc[:, 1:2], scale=1.0 / 1024.0), ssq1 + eps_ops)
      rc1 = C("vector", lambda e: e.reciprocal(st_rs1[:], st_rs1[:]), [sq1])
      xn_ops = []
      for t in range(NT):
          o = C("vector", (lambda tt: lambda e: e.scalar_tensor_tensor(zz[:, tt, :], ln[:, tt, :], st_rs1[:, tt:tt + 1],
                                                                       gnw[:, :], OP.mult, OP.mult))(t),
                [rc1, ln_ops[t], ld_gnw])
          xn_ops.append(o)
      ssq2 = []
      for t in range(NT):
          q = C("scalar", (lambda tt: lambda e: e.activation(dump[:], zz[:, tt, :], AF.Square, accum_out=st_ssq[:, tt:tt + 1]))(t), [xn_ops[t]])
          ssq2.append(q)
      sq2 = C("scalar", lambda e: e.activation(st_rs2[:], st_ssq[:], AF.Sqrt, bias=epsc[:, 2:3], scale=1.0 / 1024.0), ssq2 + eps_ops)
      rc2 = C("vector", lambda e: e.reciprocal(st_rs2[:], st_rs2[:]), [sq2])
      zzg_ops = []
      for t in range(NT):
          o = C("vector", (lambda tt: lambda e: e.scalar_tensor_tensor(zz[:, tt, :], zz[:, tt, :], st_rs2[:, tt:tt + 1],
                                                                       ginw[:, :], OP.mult, OP.mult))(t),
                [rc2, xn_ops[t], ld_ginw])
          zzg_ops.append(o)
      mg_ops = []
      for t in range(NT):
          o = C("vector", (lambda tt: lambda e: e.tensor_reduce(st_mg[:, tt:tt + 1], zz[:, tt, :], axis=AX, op=OP.max, apply_absolute_value=True))(t),
                [zzg_ops[t]])
          mg_ops.append(o)
      mgc = C("vector", lambda e: e.tensor_scalar(st_mg[:], st_mg[:], 1e-5, None, OP.max, OP.bypass), mg_ops)
      sg0 = C("vector", lambda e: e.reciprocal(st_sg[:], st_mg[:]), [mgc])
      sg1 = C("vector", lambda e: e.tensor_scalar(st_sg[:], st_sg[:], 127.0, None, OP.mult, OP.bypass), [sg0])
      dqg = C("vector", lambda e: e.tensor_scalar(st_dqg[:], st_mg[:], 1.0 / 127.0, gsc_bc[:, 1:2], OP.mult, OP.mult), [mgc, gq6])

      mag_ops = []
      qb_ops = []
      gevac_ops = []
      tr_ops = [[None] * HC for _ in range(NT)]
      ev_ops = []
      mm_ops = []
      for t in range(NT):
          m = C("scalar", (lambda tt: lambda e: e.activation(zz[:, tt, :], zz[:, tt, :], AF.Copy, bias=MAGIC, scale=st_sg[:, tt:tt + 1]))(t),
                [sg1, zzg_ops[t], mg_ops[t]])
          mag_ops.append(m)
          pace = [tr_ops[t - 2][HC - 1]] if t >= 2 else []
          qb = C("vector", (lambda tt: lambda e: e.tensor_scalar(qbrot[:, tt % 2, :], zz[:, tt, :], MAGIC, None, OP.subtract, OP.bypass))(t),
                 [m] + pace)
          qb_ops.append(qb)
          pace_ev = [ev_ops[t - 1]] if t >= 1 else []
          for b in range(HC):
              tr = C("tensor", (lambda tt, bb: lambda e: e.transpose(ptrb[:, bb * 128:(bb + 1) * 128],
                                                                     qbrot[:, tt % 2, bb * 128:(bb + 1) * 128], eyebf[:]))(t, b),
                     [qb, ld_eb] + (pace_ev if b == 0 else [tr_ops[t][b - 1]]))
              tr_ops[t][b] = tr
          pace_mm = [mm_ops[t - 2]] if t >= 2 else []
          ev = C("scalar", (lambda tt: lambda e: e.copy(qtrot[:, tt % 2, :, :].rearrange("p c n -> p (c n)"), ptrb[:]))(t),
                 [tr_ops[t][HC - 1]] + pace_mm)
          ev_ops.append(ev)
          def mk_mm(tt):
              def f(e):
                  last = None
                  for c in range(HC):
                      last = e.matmul(pbk[4 + tt % 2][0:8, 0:128], gwq[:, c, :], qtrot[:, tt % 2, c, :],
                                      start=(c == 0), stop=(c == HC - 1))
                  return last
              return f
          pace_gb = [gevac_ops[t - 2]] if t >= 2 else []
          mm = C("tensor", mk_mm(t), [ev, gq9] + pace_gb)
          mm_ops.append(mm)
          gev = C("scalar", (lambda tt: lambda e: e.copy(glog8[:, tt * 128:(tt + 1) * 128], pbk[4 + tt % 2][0:8, 0:128]))(t), [mm])
          gevac_ops.append(gev)

      evg = gevac_ops[-1]

      lg_ops = []
      for t in range(NT):
          tr = C("tensor", (lambda tt: lambda e: e.transpose(ptrf[:, 0:8], glog8[0:8, tt * 128:(tt + 1) * 128], eyef[0:8, 0:8]))(t),
                 [gevac_ops[t]] + ([lg_ops[t - 1]] if t >= 1 else [gq6]))
          ev = C("scalar", (lambda tt: lambda e: e.activation(logit[:, tt, :], ptrf[:, 0:8], AF.Copy, scale=st_dqg[:, tt:tt + 1]))(t),
                 [tr, dqg])
          lg_ops.append(ev)

      nm_ops = [C("vector", (lambda tt: lambda e: e.tensor_reduce(negmx[:, tt:tt + 1], logit[:, tt, :], axis=AX, op=OP.max, negate=True))(t), [lg_ops[t]])
                for t in range(NT)]
      ex_ops = [C("scalar", (lambda tt: lambda e: e.activation(exs[:, tt, :], logit[:, tt, :], AF.Exp, bias=negmx[:, tt:tt + 1]))(t), [nm_ops[t], lg_ops[t]])
                for t in range(NT)]
      se_ops = [C("vector", (lambda tt: lambda e: e.tensor_reduce(sumex[:, tt:tt + 1], exs[:, tt, :], axis=AX, op=OP.add))(t), [ex_ops[t]])
                for t in range(NT)]
      rs_op = C("vector", lambda e: e.reciprocal(rsum[:], sumex[:]), se_ops)
      pr_ops = [C("vector", (lambda tt: lambda e: e.tensor_scalar(probs[:, tt, :], exs[:, tt, :], rsum[:, tt:tt + 1], None, OP.mult, OP.bypass))(t),
                  [rs_op, ex_ops[t]]) for t in range(NT)]
      wsel_ops = []
      for t in range(NT):
          a1 = C("vector", (lambda tt: lambda e: e.tensor_reduce(w1t[:, tt:tt + 1], probs[:, tt, :], axis=AX, op=OP.max))(t), [pr_ops[t]])
          a2 = C("vector", (lambda tt: lambda e: e.tensor_scalar(wx8[:, 0:E], ones8[:, 0:E], w1t[:, tt:tt + 1], None, OP.mult, OP.bypass))(t), [a1, ones_op])
          a3 = C("vector", (lambda tt: lambda e: e.max_index(i1u[:], wx8[:, 0:E], probs[:, tt, :]))(t), [a2])
          a4 = C("vector", (lambda tt: lambda e: e.tensor_copy(e1t[:, tt:tt + 1], i1u[:, 0:1]))(t), [a3])
          a5 = C("vector", (lambda tt: lambda e: e.tensor_scalar(masked[:, tt, :], iota8[:, 0:E], e1t[:, tt:tt + 1], None, OP.is_equal, OP.bypass))(t),
                 [a4, ld_io])
          a6 = C("vector", (lambda tt: lambda e: e.scalar_tensor_tensor(masked[:, tt, :], masked[:, tt, :], -1e30, probs[:, tt, :], OP.mult, OP.add))(t), [a5])
          a7 = C("vector", (lambda tt: lambda e: e.tensor_reduce(w2t[:, tt:tt + 1], masked[:, tt, :], axis=AX, op=OP.max))(t), [a6])
          a8 = C("vector", (lambda tt: lambda e: e.tensor_scalar(wx8[:, 0:E], ones8[:, 0:E], w2t[:, tt:tt + 1], None, OP.mult, OP.bypass))(t), [a7])
          a9 = C("vector", (lambda tt: lambda e: e.max_index(i2u[:], wx8[:, 0:E], masked[:, tt, :]))(t), [a8])
          a10 = C("vector", (lambda tt: lambda e: e.tensor_copy(e2t[:, tt:tt + 1], i2u[:, 0:1]))(t), [a9])
          b1 = C("vector", (lambda tt: lambda e: e.tensor_scalar(wsel[:, tt, :], iota8[:, 0:E], e1t[:, tt:tt + 1], None, OP.is_equal, OP.bypass))(t), [a4])
          b2 = C("vector", (lambda tt: lambda e: e.tensor_scalar(wsel[:, tt, :], wsel[:, tt, :], w1t[:, tt:tt + 1], None, OP.mult, OP.bypass))(t), [b1])
          b3 = C("vector", (lambda tt: lambda e: e.tensor_scalar(masked[:, tt, :], iota8[:, 0:E], e2t[:, tt:tt + 1], None, OP.is_equal, OP.bypass))(t), [a10, b2])
          b4 = C("vector", (lambda tt: lambda e: e.scalar_tensor_tensor(wsel[:, tt, :], masked[:, tt, :], w2t[:, tt:tt + 1], wsel[:, tt, :], OP.mult, OP.add))(t), [b3])
          wsel_ops.append(b4)

    ag_dep = None
    class _M2Stop2(Exception):
        pass
    try:
      if milestone >= 2 and M2CUT >= 1:
        ld_i128 = D("sync", lambda e: e.dma_start(out=iota128[:], in_=iota128_d), dsem="ld0!")
        ld_hot = D("sync", lambda e: e.dma_start(out=myhot[:], in_=myhot_d), dsem="ld0!")
        ld_hotr = D("sync", lambda e: e.dma_start(out=myhotr[:], in_=myhotr_d), dsem="ld0!")
        wr_ops = []
        for t in range(0 if SKIP_WR else NT):
            w1_ = D("sync", (lambda tt: lambda e: e.dma_start(out=ag1a_in[tt * 128:(tt + 1) * 128, :], in_=ln[:, tt, :]))(t), [ln_ops[t] if len(ln_ops) > t else None], dsem="agw!")
            w2_ = D("sync", (lambda tt: lambda e: e.dma_start(out=ag1w_in[tt * 128:(tt + 1) * 128, :], in_=wsel[:, tt, :]))(t), [wsel_ops[t] if len(wsel_ops) > t else None], dsem="agw!")
            wr_ops += [w1_, w2_]
        wt_ops = []
        for t in range(0 if SKIP_WT else NT):
            trw = C("tensor", (lambda tt: lambda e: e.transpose(ptrf[0:8, 0:128], wsel[:, tt, :], eyef[:]))(t),
                    [wsel_ops[t] if len(wsel_ops) > t else None] + ([wt_ops[-1]] if wt_ops else (lg_ops[-1:] if lg_ops else [])))
            evw = C("scalar", (lambda tt: lambda e: e.copy(wselT[:, tt * 128:(tt + 1) * 128], ptrf[0:8, 0:128]))(t), [trw])
            wt_ops.append(evw)
        wrb = D("sync", lambda e: e.dma_start(out=ag1b_in, in_=wselT[:]), wt_ops, dsem="agw!") if not SKIP_WT else None
        if M2CUT < 1.25:
            ag_dep = None
            raise _M2Stop2
        ag1b_op = P.add("gpsimd", lambda e: e.collective_compute(
            "AllGather", OP.bypass, replica_groups=[list(range(NCORES))],
            ins=[ag1b_in_f], outs=[ag1b_out_f.rearrange("r c -> (r c)")]), wr_ops + [wrb], coll=True)
        ag1a_op = P.add("gpsimd", lambda e: e.collective_compute(
            "AllGather", OP.bypass, replica_groups=[list(range(NCORES))],
            ins=[ag1a_in.rearrange("r c -> (r c)")], outs=[ag1a_out.rearrange("r c -> (r c)")]), wr_ops, coll=True)
        ag_dep = [ag1a_op, ag1b_op]
    except _M2Stop2:
        pass

    toki_ops = []
    gidx_wr = []
    routing_done = []
    class _M2Stop(Exception):
        pass
    try:
        _m2go = milestone >= 2 and M2CUT >= 2 and ag_dep is not None
        if not _m2go:
            raise _M2Stop
        ld_wcol = []
        for e_ in range(E):
            for r in range(NCORES):
                op = D("sync", (lambda ee, rr: lambda e: e.dma_start(
                    out=wcol8[16 * rr:16 * (rr + 1), ee, :],
                    in_=ag1b_out_f[rr, TC * E + ee * TC:TC * E + (ee + 1) * TC].rearrange("(p f) -> p f", p=16)))(e_, r), [ag1b_op], dsem="wcol!")
                ld_wcol.append(op)
        if M2CUT < 3:
            routing_done = ld_wcol
            raise _M2Stop
        ke_ops = []
        r_ops = []
        for e_ in range(E):
            ke = C("vector", (lambda ee: lambda e: e.tensor_scalar(keep8[:, ee, :], wcol8[:, ee, :], 0.0, None, OP.is_gt, OP.bypass))(e_), ld_wcol)
            ke_ops.append(ke)
        for e_ in range(E):
            r = C("vector", (lambda ee: lambda e: e.tensor_tensor_scan(R8[:, ee, :], keep8[:, ee, :], keep8[:, ee, :], 0.0, OP.add, OP.bypass))(e_),
                  [ke_ops[e_]])
            r_ops.append(r)
        for e_ in range(E):
            C("vector", (lambda ee: lambda e: e.tensor_copy(ktot8[:, ee:ee + 1], R8[:, ee, 127:128]))(e_), [r_ops[e_]])
        kt_done = C("vector", lambda e: e.tensor_copy(dump8[:, 0:E], ktot8[:]), r_ops + [])
        trk = C("tensor", lambda e: e.transpose(ptrf[0:E, 0:128], ktot8[:], eyef[:]), [kt_done] + wt_ops[-1:])
        evk = C("scalar", lambda e: e.copy(koffT[:], ptrf[0:E, 0:128]), [trk])
        sck = C("vector", lambda e: e.tensor_tensor_scan(koffTx[:], koffT[:], koffT[:], 0.0, OP.add, OP.bypass), [evk])
        exk = C("vector", lambda e: e.tensor_tensor(koffTx[:], koffTx[:], koffT[:], OP.subtract), [sck])
        trk2 = C("tensor", lambda e: e.transpose(ptrf[:, 0:E], koffTx[:], eyef[0:E, 0:E]), [exk])
        evk2 = C("scalar", lambda e: e.copy(koff8[:], ptrf[:, 0:E]), [trk2])
        rg_ops = []
        kp_ops = []
        for e_ in range(E):
            rg = C("vector", (lambda ee: lambda e: e.tensor_scalar(R8[:, ee, :], R8[:, ee, :], koff8[:, ee:ee + 1], None, OP.add, OP.bypass))(e_),
                   [evk2, r_ops[e_]])
            rg_ops.append(rg)
            kp = C("vector", (lambda ee: lambda e: e.scalar_tensor_tensor(keep8[:, ee, :], R8[:, ee, :], float(CAP), keep8[:, ee, :], OP.is_le, OP.mult))(e_),
                   [rg, ke_ops[e_]])
            kp_ops.append(kp)
        if M2CUT < 4:
            routing_done = kp_ops
            raise _M2Stop
        zg = C("vector", lambda e: e.memset(gAB[:], 0.0))
        zp = C("vector", lambda e: e.memset(pctmp[:, 0, :], 0.0))
        prev_pc = zp
        acc_ops = [zg]
        for e_ in range(E):
            cur, nxt = e_ % 2, (e_ + 1) % 2
            isA = C("vector", (lambda c: lambda e: e.tensor_scalar(natmp[:], pctmp[:, c, :], 0.0, None, OP.is_equal, OP.bypass))(cur), [prev_pc] + ld_wcol)
            isA2 = C("vector", (lambda ee: lambda e: e.scalar_tensor_tensor(natmp[:], wcol8[:, ee, :], 0.0, natmp[:], OP.is_gt, OP.mult))(e_), [isA])
            ge1 = C("vector", (lambda ee: lambda e: e.tensor_scalar(getmp[:], R8[:, ee, :], float(ee * CAP - 1), None, OP.add, OP.bypass))(e_), [rg_ops[e_]])
            ge2 = C("vector", (lambda ee: lambda e: e.tensor_tensor(getmp[:], getmp[:], keep8[:, ee, :], OP.mult))(e_), [ge1, kp_ops[e_]])
            a1 = C("vector", (lambda: lambda e: e.tensor_tensor(dump8[:], natmp[:], getmp[:], OP.mult))(), [isA2, ge2])
            a2 = C("vector", (lambda: lambda e: e.tensor_tensor(gAB[:, 0, :], gAB[:, 0, :], dump8[:], OP.add))(), [a1] + acc_ops[-1:])
            a3 = C("vector", (lambda ee: lambda e: e.tensor_tensor(dump8[:], keep8[:, ee, :], wcol8[:, ee, :], OP.mult))(e_), [kp_ops[e_], a2])
            a4 = C("vector", (lambda: lambda e: e.tensor_tensor(dump8[:], dump8[:], natmp[:], OP.mult))(), [a3])
            a5 = C("vector", (lambda: lambda e: e.tensor_tensor(gAB[:, 2, :], gAB[:, 2, :], dump8[:], OP.add))(), [a4])
            b0 = C("vector", (lambda c: lambda e: e.tensor_scalar(natmp[:], pctmp[:, c, :], 1.0, None, OP.is_equal, OP.bypass))(cur), [a5, prev_pc])
            b1 = C("vector", (lambda ee: lambda e: e.scalar_tensor_tensor(natmp[:], wcol8[:, ee, :], 0.0, natmp[:], OP.is_gt, OP.mult))(e_), [b0])
            b2 = C("vector", (lambda: lambda e: e.tensor_tensor(dump8[:], natmp[:], getmp[:], OP.mult))(), [b1, ge2])
            b3 = C("vector", (lambda: lambda e: e.tensor_tensor(gAB[:, 1, :], gAB[:, 1, :], dump8[:], OP.add))(), [b2])
            b4 = C("vector", (lambda ee: lambda e: e.tensor_tensor(dump8[:], keep8[:, ee, :], wcol8[:, ee, :], OP.mult))(e_), [b3, kp_ops[e_]])
            b5 = C("vector", (lambda: lambda e: e.tensor_tensor(dump8[:], dump8[:], natmp[:], OP.mult))(), [b4])
            b6 = C("vector", (lambda: lambda e: e.tensor_tensor(gAB[:, 3, :], gAB[:, 3, :], dump8[:], OP.add))(), [b5])
            pcn = C("vector", (lambda ee, c, n: lambda e: e.scalar_tensor_tensor(pctmp[:, n, :], wcol8[:, ee, :], 0.0, pctmp[:, c, :], OP.is_gt, OP.add))(e_, cur, nxt),
                    [b6, prev_pc])
            prev_pc = pcn
            acc_ops.append(b6)
        for j in range(4):
            op = D("sync", (lambda jj: lambda e: e.dma_start(out=gidx_d[jj, :].rearrange("(p f) -> p f", p=128), in_=gAB[:, jj, :]))(j), [acc_ops[-1], prev_pc], dsem="gidx!")
            gidx_wr.append(op)

        if M2CUT < 5:
            routing_done = gidx_wr
            raise _M2Stop
        zR = C("vector", lambda e: e.memset(Rmy[:], 0.0))
        prevR = [zR]
        mko = C("tensor", lambda e: e.matmul(ptr1[0:1, 0:128], myhot[:], koffTx[:], start=True, stop=True), [exk, ld_hot, gq6, evk2])
        evko = C("scalar", lambda e: e.copy(koffrow[:], ptr1[0:1, 0:128]), [mko])
        tkm = C("tensor", lambda e: e.transpose(ptrf[:, 0:1], koffrow[:], eyef[0:1, 0:1]), [evko, evk2])
        evkm = C("scalar", lambda e: e.copy(koffmy[:], ptrf[:, 0:1]), [tkm])
        mkb = C("tensor", lambda e: e.matmul(ptr1[:, 0:128], onesrow[:], koffrow[:], start=True, stop=True), [evko, onesrow_op, evkm])
        evkb = C("scalar", lambda e: e.copy(koffB[:], ptr1[:, 0:128]), [mkb])
        for e_ in range(E):
            pass
        mfl = C("tensor", lambda e: e.matmul(ptr1[:, 0:8], onesrow[:], myhotr[:], start=True, stop=True), [ld_hotr, onesrow_op, evkb])
        evmf = C("scalar", lambda e: e.copy(mflag[:, 0:8], ptr1[:, 0:8]), [mfl])
        mflag_hold = evmf
        mfl_evmf = evmf
        rmy_ops = [zR]
        for e_ in range(E):
            op = C("vector", (lambda ee: lambda e: e.scalar_tensor_tensor(Rmy[:], R8[:, ee, :], mflag[:, ee:ee + 1], Rmy[:], OP.mult, OP.add))(e_),
                   [mflag_hold, rg_ops[e_], rmy_ops[-1]])
            rmy_ops.append(op)
        rmy_done = rmy_ops[-1]
        tb0 = C("vector", lambda e: e.memset(tbl[:], 0.0))
        tb1 = C("scalar", lambda e: e.copy(tbl[:, 0:1], koffmy[:]), [evkm, tb0])
        tb2 = C("vector", lambda e: e.tensor_scalar(tbl[:, 1:129], Rmy[:], koffmy[:, 0:1], None, OP.subtract, OP.bypass), [rmy_done, evkm, tb0])
        tb3 = D("sync", lambda e: e.dma_start(out=tbl_d, in_=tbl[:]), [tb1, tb2], dsem="tbl!")
        sv_ops = []
        for i in range(16):
            op = C("vector", (lambda ii: lambda e: e.tensor_scalar(sval[:, ii:ii + 1], iota128[:], float(128 * ii), None, OP.add, OP.bypass))(i),
                   [ld_i128])
            sv_ops.append(op)
        l1_ops = []
        for i in range(16):
            op = C("vector", (lambda ii: lambda e: e.tensor_scalar(dump8[:], koffB[:], sval[:, ii:ii + 1], None, OP.is_le, OP.add, accum_out=cnts[:, ii:ii + 1]))(i),
                   [evkb, sv_ops[i]] + acc_ops[-1:])
            l1_ops.append(op)
        pz_op = C("vector", lambda e: e.tensor_scalar(pz[:], cnts[:], 1.0, None, OP.subtract, OP.bypass), l1_ops)
        pzi_op = C("vector", lambda e: e.tensor_copy(pzi[:], pz[:]), [pz_op])
        g2_ops = []
        for i in range(16):
            op = P.add("gpsimd", (lambda ii: lambda e: e.indirect_dma_start(
                g2buf[:, ii, :], None, tbl_d, IndirectOffsetOnAxis(pzi[:, ii:ii + 1], 0)))(i), [pzi_op, tb3], dma=True, dsem="g2!")
            g2_ops.append(op)
        jv_ops = []
        fv_ops = []
        for i in range(16):
            j1 = C("vector", (lambda ii: lambda e: e.scalar_tensor_tensor(jv[:, ii:ii + 1], g2buf[:, ii, 0:1], -1.0, sval[:, ii:ii + 1], OP.mult, OP.add))(i),
                   [g2_ops[i], sv_ops[i]])
            jv_ops.append(j1)
        for i in range(16):
            f1 = C("vector", (lambda ii: lambda e: e.tensor_scalar(dump8[:], g2buf[:, ii, 1:129], jv[:, ii:ii + 1], None, OP.is_le, OP.add, accum_out=fv[:, ii:ii + 1]))(i),
                   [jv_ops[i], g2_ops[i]])
            fv_ops.append(f1)
        tk_op = C("vector", lambda e: e.scalar_tensor_tensor(tokf[:], pz[:], 128.0, fv[:], OP.mult, OP.add), fv_ops + [pz_op])
        toki_op = C("vector", lambda e: e.tensor_copy(toki[:], tokf[:]), [tk_op])
        toki_ops = [toki_op]
        routing_done = [toki_op, pzi_op] + gidx_wr

    
    except _M2Stop:
        pass

    s_ops = []
    finals = []
    SPENG = mybir.EngineType.SP
    if milestone >= 3:
        alias_w = wr_ops + [qb_ops[-1], lg_ops[-1], wsel_ops[-1], mm_ops[-1], ev_ops[-1], gevac_ops[-1]]
        routing_alias = [toki_op, pzi_op] + gidx_wr + fv_ops + jv_ops


        ld_gpn = D("sync", lambda e: e.dma_start(out=gpn[:], in_=gpn_d), dsem="ldw!")
        ld_dpn = D("sync", lambda e: e.dma_start(out=dpn[:], in_=dpn_d), dsem="ldw!")

        def wquant(src_view, nchunks, chunk_free, wq_dst_fn, acc_col, sc_col, denom, deps0, prefix, csub):
            def w3(cc):
                return wrot[:, cc % 2, 0:chunk_free].rearrange("p (c o) -> p c o", c=csub)
            z0 = C("vector", (lambda: lambda e: e.memset(wsum[:, acc_col:acc_col + 1], 0.0))(), deps0)
            red_prev = z0
            lds = []
            for ch in range(nchunks):
                pace = [lds[ch - 2][1]] if ch >= 2 else []
                ldo = D("sync", (lambda cc: lambda e: e.dma_start(out=w3(cc), in_=src_view(cc)))(ch),
                        deps0 + [p for p in pace], dsem=f"{prefix}{ch % 2}")
                r1_ = C("vector", (lambda cc: lambda e: e.tensor_reduce(wsum[:, 2:3], wrot[:, cc % 2, 0:chunk_free], axis=AX, op=OP.add, apply_absolute_value=True))(ch),
                        [ldo, red_prev])
                a1_ = C("vector", (lambda: lambda e: e.tensor_tensor(wsum[:, acc_col:acc_col + 1], wsum[:, acc_col:acc_col + 1], wsum[:, 2:3], OP.add))(),
                        [r1_])
                red_prev = a1_
                lds.append((ldo, a1_))
            t1 = C("tensor", (lambda: lambda e: e.transpose(ptrf[0:1, 0:128], wsum[:, acc_col:acc_col + 1], eyef[:]))(), [red_prev] + self_ptrf[-1:])
            e1_ = C("scalar", (lambda: lambda e: e.copy(grow[:], ptrf[0:1, 0:128]))(), [t1])
            r2_ = C("vector", (lambda: lambda e: e.tensor_reduce(gsc[:, 0:1], grow[:], axis=AX, op=OP.add))(), [e1_])
            m1_ = C("vector", (lambda: lambda e: e.tensor_scalar(gsc[:, 3:4], gsc[:, 0:1], 1.0 / denom, 1e-5, OP.mult, OP.max))(), [r2_])
            w1_q = C("vector", (lambda: lambda e: e.reciprocal(gsc[:, 2:3], gsc[:, 3:4]))(), [m1_])
            b1_ = C("tensor", (lambda: lambda e: e.matmul(ptrf[:, 0:2], onesrow[:], gsc[0:1, 2:4], start=True, stop=True))(), [w1_q, e1_, onesrow_op])
            b2_ = C("scalar", (lambda: lambda e: e.copy(wqsc[:, sc_col:sc_col + 2], ptrf[:, 0:2]))(), [b1_])
            self_ptrf.append(b2_)
            qops = []
            prevq = None
            for ch in range(nchunks):
                pace = [qops[ch - 2][2]] if ch >= 2 else [lds[-1][1]]
                ldo = D("sync", (lambda cc: lambda e: e.dma_start(out=w3(cc), in_=src_view(cc)))(ch),
                        pace, dsem=f"{prefix}{ch % 2}")
                mg_ = C("scalar", (lambda cc: lambda e: e.activation(wrot[:, cc % 2, 0:chunk_free], wrot[:, cc % 2, 0:chunk_free], AF.Copy, bias=MAGIC, scale=wqsc[:, sc_col:sc_col + 1]))(ch),
                        [ldo, b2_])
                c1_ = C("vector", (lambda cc: lambda e: e.tensor_scalar(wrot[:, cc % 2, 0:chunk_free], wrot[:, cc % 2, 0:chunk_free], MAGIC, -1.0, OP.subtract, OP.max))(ch), [mg_])
                c2_ = C("vector", (lambda cc: lambda e: e.tensor_scalar(wq_dst_fn(cc), w3(cc), 1.0, None, OP.min, OP.bypass))(ch), [c1_])
                qops.append((ldo, c1_, c2_))
                prevq = c2_
            return [q[2] for q in qops]

        self_ptrf = [mfl_evmf] if milestone >= 2 else []
        gpw_t = gpwT_d.rearrange("(c p) o -> p c o", p=128)
        dpw_t = dpwT_d.rearrange("(c p) o -> p c o", p=128)

        gpq_ops = wquant(lambda cc: gpw_t[:, :, 512 * cc:512 * (cc + 1)], 8, 4096,
                         lambda cc: gp_wq[:, :, 512 * cc:512 * (cc + 1)],
                         0, 0, float(H * I2), alias_w, "wsg", HC)
        dpq_ops = wquant(lambda cc: dpw_t[:, 4 * cc:4 * (cc + 1), :], 4, 4096,
                         lambda cc: dp_wq[:, 4 * cc:4 * (cc + 1), :],
                         1, 2, float(II * H), routing_alias + [gpq_ops[-1]], "wsd", 4)

        rt_done = []
        rtr_evs = []
        rtr_lev = []
        if milestone >= 4:
            ldf_ops = [
                D("sync", lambda e: e.dma_start(out=wrouter[:], in_=wr_d), dsem="ldf!"),
                D("sync", lambda e: e.dma_start(out=brrep[:], in_=brr_d), dsem="ldf!"),
                D("sync", lambda e: e.dma_start(out=iotac16[:], in_=iotac16_d), dsem="ldf!"),
                D("sync", lambda e: e.dma_start(out=rb128s[:], in_=rb128_d), dsem="ldf!"),
                D("sync", lambda e: e.dma_start(out=rbps[:], in_=rbp_d), dsem="ldf!"),
            ]
            rtr_mms = []
            badd_ops = []
            for t in range(NT):
                lr = D("sync", (lambda tt: lambda e: e.dma_start(
                    out=lnrot[:, 0, :], in_=ag1a_in[tt * 128:(tt + 1) * 128, :]))(t),
                    [wr_ops[2 * t]] + ([rtr_evs[-1]] if rtr_evs else [wsel_ops[-1]]), dsem=f"lr{t % 2}")
                for hc in range(8):
                    k = t * 8 + hc
                    bnk = pbk[0] if k % 2 == 0 else pbk[2]
                    tr = C("tensor", (lambda cc, bb: lambda e: e.transpose(
                        bb[:, 0:128], lnrot[:, 0, cc * 128:(cc + 1) * 128], eyef[:]))(hc, bnk),
                        [lr] + ([rtr_evs[k - 2]] if k >= 2 else []))
                    if hc % 2 == 0:
                        ev = C("vector", (lambda cc, bb: lambda e: e.tensor_copy(lnTb[:, cc % 2, :], bb[:, 0:128]))(hc, bnk),
                               [tr] + ([rtr_mms[k - 2]] if k >= 2 else []))
                    else:
                        ev = C("scalar", (lambda cc, bb: lambda e: e.copy(lnTb[:, cc % 2, :], bb[:, 0:128]))(hc, bnk),
                               [tr] + ([rtr_mms[k - 2]] if k >= 2 else []))
                    mm = C("tensor", (lambda cc: lambda e: e.matmul(
                        pbk[1][:, 0:EF], lnTb[:, cc % 2, :], wrouter[:, cc, :],
                        start=(cc == 0), stop=(cc == 7)))(hc),
                        [ev, ldf_ops[0]] + ([rtr_lev[-1]] if (hc == 0 and rtr_lev) else []))
                    rtr_mms.append(mm)
                    rtr_evs.append(ev)
                lev = C("scalar", (lambda tt: lambda e: e.copy(l2t[:, tt, :], pbk[1][:, 0:EF]))(t), [rtr_mms[-1]])
                rtr_lev.append(lev)
                badd = C("vector", (lambda tt: lambda e: e.tensor_tensor(l2t[:, tt, :], l2t[:, tt, :], brrep[:], OP.add))(t),
                         [lev, ldf_ops[1]])
                badd_ops.append(badd)
            rt_done = badd_ops

        wselrows = ag1b_out_f.rearrange("r b -> (r b)").rearrange("(n k) -> n k", k=E)
        gath_ops = [None] * 16
        wg_ops = [None] * 16
        q1w_ops = []
        q1t_ops = {}
        gpmm_ops = {}
        umul_ops = {}
        q2w_ops = {}
        q2t_ops = {}
        dpmm_ops = {}
        ctev_ops = [None] * 16
        ag2w_ops = []
        quant_last = [None] * 16
        dq1b_ops = [None] * 16
        wt2_ops = [None] * 16
        qc1_ops = [None] * 16
        xin_readers = {}
        csc_ops = [None] * 16
        qc2_ops = [None] * 16
        EC = lambda rb, k: exsc[:, (rb % 4) * 16 + k:(rb % 4) * 16 + k + 1]

        if DELAY_AG:
            dlast = None
            for _i in range(DELAY_AG):
                dlast = C("vector", lambda e: e.memset(getmp[:, 0:16], 0.0), ([dlast] if dlast else []) + ag_dep)
            ag_dep = ag_dep + [dlast]
        txin_ops = [None] * 16
        for rb in range(16):
            ch = rb // 4
            grp = rb // 2
            par = rb % 2
            pace_g = xin_readers[rb - 2] if rb >= 2 else [dpq_ops[-1]]
            if rb >= 2 and txin_ops[rb - 2] is not None:
                pace_g = pace_g + [txin_ops[rb - 2]]
            g1 = P.add("gpsimd", (lambda r: lambda e: e.indirect_dma_start(
                xinr[:, r % 2, :], None, ag1a_out, IndirectOffsetOnAxis(toki[:, r:r + 1], 0)))(rb),
                [toki_op] + ag_dep + pace_g, dma=True, dsem=f"gx{par}")
            gath_ops[rb] = g1
            if "tap_xin" in tapd:
                tx = D("gpsimd", (lambda r: lambda e: e.dma_start(
                    out=tapd["tap_xin"][r * 128:(r + 1) * 128, :], in_=xinr[:, r % 2, :]))(rb), [g1], dsem=f"tx{par}")
                txin_ops[rb] = tx
                finals.append(tx)
            ss1 = C("scalar", (lambda r: lambda e: e.activation(dump[:], xinr[:, r % 2, :], AF.Square, accum_out=EC(r, 0)))(rb), [g1])
            xin_readers[rb] = [ss1]
            sq1_ = C("scalar", (lambda r: lambda e: e.activation(EC(r, 0), EC(r, 0), AF.Sqrt, bias=epsc[:, 2:3], scale=1.0 / float(H)))(rb), [ss1])
            rc1_ = C("vector", (lambda r: lambda e: e.reciprocal(EC(r, 0), EC(r, 0)))(rb), [sq1_])
            pp1 = C("vector", (lambda r: lambda e: e.tensor_tensor(p1r[:, 0, :], xinr[:, r % 2, :], gpn[:], OP.mult))(rb), [g1, ld_gpn] + ([qc1_ops[rb - 1]] if rb >= 1 else []))
            xin_readers[rb].append(pp1)
            mm1 = C("vector", (lambda r: lambda e: e.tensor_reduce(EC(r, 1), p1r[:, 0, :], axis=AX, op=OP.max, apply_absolute_value=True))(rb), [pp1])
            mm1b = C("vector", (lambda r: lambda e: e.tensor_scalar(EC(r, 1), EC(r, 1), 1e-5, None, OP.max, OP.bypass))(rb), [mm1])
            sc1 = C("vector", (lambda r: lambda e: e.reciprocal(EC(r, 2), EC(r, 1)))(rb), [mm1b])
            sc1b = C("vector", (lambda r: lambda e: e.tensor_scalar(EC(r, 2), EC(r, 2), 127.0, None, OP.mult, OP.bypass))(rb), [sc1])
            dq1 = C("vector", (lambda r: lambda e: e.tensor_tensor(EC(r, 3), EC(r, 0), EC(r, 1), OP.mult))(rb), [rc1_, mm1b])
            dq1b = C("vector", (lambda r: lambda e: e.tensor_scalar(EC(r, 3), EC(r, 3), 1.0 / 127.0, wqsc[:, 1:2], OP.mult, OP.mult))(rb), [dq1, gpq_ops[-1]])
            dq1b_ops[rb] = dq1b
            mgq1 = C("scalar", (lambda r: lambda e: e.activation(p1r[:, 0, :], p1r[:, 0, :], AF.Copy, bias=MAGIC, scale=EC(r, 2)))(rb), [pp1, sc1b, mm1])
            qc1 = C("vector", (lambda r: lambda e: e.tensor_scalar(q1r[:, r % 4, :], p1r[:, 0, :], MAGIC, None, OP.subtract, OP.bypass))(rb), [mgq1] + ([q1t_ops[(rb - 4) // 4][0]] if rb >= 4 else []))
            q1w = D("sync", (lambda r: lambda e: e.dma_start(out=q1s_d[r * 128:(r + 1) * 128, :], in_=q1r[:, r % 4, :]))(rb), [qc1], dsem=f"q1w{ch}!")
            q1w_ops.append(q1w)
            quant_last[rb] = qc1
            qc1_ops[rb] = qc1

            if rb % 4 == 3:
                deps_t = [q1w_ops[ch * 4 + k] for k in range(4)]
                war = [gpmm_ops[(ch - 1, 3, 1)]] if ch >= 1 else []
                tts = []
                for kf in range(HC):
                    tt_ = D("sync", (lambda cc, kk: lambda e: e.dma_start(
                        out=q1T[:, kk, :], in_=q1s_d[512 * cc:512 * (cc + 1), 128 * kk:128 * (kk + 1)], transpose=True))(ch, kf),
                        deps_t + war, dsem=f"q1t{ch}!")
                    tts.append(tt_)
                q1t_ops[ch] = tts

                for r2_ in range(ch * 4, ch * 4 + 4):
                    rloc = r2_ % 4
                    for j in range(4):
                        bA = pbk[(j % 2) * 2]
                        bB = pbk[(j % 2) * 2 + 1]
                        warA = [umul_ops[(r2_, j - 2)]] if j >= 2 else ([umul_ops[(r2_ - 1, 2)], umul_ops[(r2_ - 1, 3)]] if r2_ > 0 else [])
                        def mk_gp(rr, jj, bank, ocbase):
                            def f(e):
                                last = None
                                for kf in range(HC):
                                    last = e.matmul(bank[:], q1T[:, kf, (rr % 4) * 128:(rr % 4 + 1) * 128],
                                                    gp_wq[:, kf, ocbase * 512:(ocbase + 1) * 512],
                                                    start=(kf == 0), stop=(kf == HC - 1))
                                return last
                            return f
                        rtr_war = (rtr_evs[-1:] + rtr_lev[-1:]) if (milestone >= 4 and r2_ == 0 and j == 0) else []
                        mA = C("tensor", mk_gp(r2_, j, bA, j), q1t_ops[ch] + [gpq_ops[-1]] + warA + rtr_war)
                        mB = C("tensor", mk_gp(r2_, j, bB, j + 4), q1t_ops[ch] + warA + rtr_war)
                        gpmm_ops[(r2_, j, 0)] = mA
                        gpmm_ops[(r2_, j, 1)] = mB
                        sl_ = C("scalar", (lambda rr, jj, bank: lambda e: e.activation(sgb[:], bank[:], AF.Silu, scale=EC(rr, 3)))(r2_, j, bA),
                                [mA, dq1b_ops[r2_]] + ([umul_ops[(r2_, j - 1)]] if j >= 1 else ([umul_ops[(r2_ - 1, 3)]] if r2_ > 0 else
                                 (rtr_evs[-1:] if (milestone >= 4 and r2_ == 0) else []))))
                        um_ = C("vector", (lambda rr, jj, bank: lambda e: e.scalar_tensor_tensor(
                            ubuf[:, jj * 512:(jj + 1) * 512], bank[:], EC(rr, 3), sgb[:], OP.mult, OP.mult))(r2_, j, bB),
                            [mB, sl_, dq1b_ops[r2_]] + (([qc2_ops[r2_ - 1]] + ag2w_ops[r2_ - 1:r2_]) if (j == 0 and r2_ > 0) else []) +
                            (rtr_evs[-1:] if (milestone >= 4 and r2_ == 0 and j == 2) else []))
                        umul_ops[(r2_, j)] = um_
                    udeps = [umul_ops[(r2_, j)] for j in range(4)]
                    ss2 = C("scalar", (lambda rr: lambda e: e.activation(q2r[:, rr % 2, :], ubuf[:], AF.Square, accum_out=EC(rr, 5)))(r2_),
                            udeps + (rtr_evs[-1:] if (milestone >= 4 and r2_ == 0) else []))
                    sq2_ = C("scalar", (lambda rr: lambda e: e.activation(EC(rr, 5), EC(rr, 5), AF.Sqrt, bias=epsc[:, 2:3], scale=1.0 / float(II)))(r2_), [ss2])
                    rc2_ = C("vector", (lambda rr: lambda e: e.reciprocal(EC(rr, 5), EC(rr, 5)))(r2_), [sq2_])
                    pp2 = C("vector", (lambda rr: lambda e: e.tensor_tensor(ubuf[:], ubuf[:], dpn[:], OP.mult))(r2_), udeps + [ld_dpn, ss2])
                    mm2 = C("vector", (lambda rr: lambda e: e.tensor_reduce(EC(rr, 6), ubuf[:], axis=AX, op=OP.max, apply_absolute_value=True))(r2_), [pp2])
                    mm2b = C("vector", (lambda rr: lambda e: e.tensor_scalar(EC(rr, 6), EC(rr, 6), 1e-5, None, OP.max, OP.bypass))(r2_), [mm2])
                    sc2 = C("vector", (lambda rr: lambda e: e.reciprocal(EC(rr, 7), EC(rr, 6)))(r2_), [mm2b])
                    sc2b = C("vector", (lambda rr: lambda e: e.tensor_scalar(EC(rr, 7), EC(rr, 7), 127.0, None, OP.mult, OP.bypass))(r2_), [sc2])
                    dq2 = C("vector", (lambda rr: lambda e: e.tensor_tensor(EC(rr, 8), EC(rr, 5), EC(rr, 6), OP.mult))(r2_), [rc2_, mm2b])
                    dq2b = C("vector", (lambda rr: lambda e: e.tensor_scalar(EC(rr, 8), EC(rr, 8), 1.0 / 127.0, wqsc[:, 3:4], OP.mult, OP.mult))(r2_), [dq2, dpq_ops[-1]])
                    csc = C("vector", (lambda rr: lambda e: e.tensor_copy(EC(rr, 9), EC(rr, 8)))(r2_), [dq2b])
                    csc_ops[r2_] = csc
                    if "tap_csc" in tapd:
                        C("vector", (lambda rr: lambda e: e.tensor_copy(csc_dbg[:, rr:rr + 1], EC(rr, 9)))(r2_), [csc])
                    mgq2 = C("scalar", (lambda rr: lambda e: e.activation(ubuf[:], ubuf[:], AF.Copy, bias=MAGIC, scale=EC(rr, 7)))(r2_), [pp2, sc2b, mm2])
                    qc2 = C("vector", (lambda rr: lambda e: e.tensor_scalar(q2r[:, rr % 2, :], ubuf[:], MAGIC, None, OP.subtract, OP.bypass))(r2_),
                            [mgq2] + ([q2t_ops[(r2_ - 2) // 2][0]] if r2_ >= 2 else []))
                    qc2_ops[r2_] = qc2
                    q2w = D("sync", (lambda rr: lambda e: e.dma_start(out=q2s_d[rr * 128:(rr + 1) * 128, :], in_=q2r[:, rr % 2, :]))(r2_), [qc2], dsem=f"q2w{r2_ // 2}!")
                    q2w_ops[r2_] = q2w
                    quant_last[r2_] = qc2

                    if r2_ % 2 == 1:
                        gg_ = r2_ // 2
                        depsq = [q2w_ops[gg_ * 2], q2w_ops[gg_ * 2 + 1]]
                        warq = [dpmm_ops[(gg_ - 1, 1, 1)]] if gg_ >= 1 else []
                        tts2 = []
                        for kf in range(IIC):
                            tt_ = D("sync", (lambda gx, kk: lambda e: e.dma_start(
                                out=q2T[:, kk, :], in_=q2s_d[256 * gx:256 * (gx + 1), 128 * kk:128 * (kk + 1)], transpose=True))(gg_, kf),
                                depsq + warq, dsem=f"q2t{gg_}!")
                            tts2.append(tt_)
                        q2t_ops[gg_] = tts2
                        for rr2 in (gg_ * 2, gg_ * 2 + 1):
                            for oc2 in range(2):
                                bC = pbk[4 + oc2]
                                warc = [ctev_ops[rr2 - 1]] if rr2 >= 1 else []
                                def mk_dp(rrx, ocx, bank):
                                    def f(e):
                                        last = None
                                        for kf in range(IIC):
                                            last = e.matmul(bank[:], q2T[:, kf, (rrx % 2) * 128:(rrx % 2 + 1) * 128],
                                                            dp_wq[:, kf, ocx * 512:(ocx + 1) * 512],
                                                            start=(kf == 0), stop=(kf == IIC - 1))
                                        return last
                                    return f
                                mD = C("tensor", mk_dp(rr2, oc2, bC), tts2 + [dpq_ops[-1]] + warc + [gevac_ops[-1]])
                                dpmm_ops[(gg_, rr2 % 2, oc2)] = mD
                            ct_ = C("scalar", (lambda rrx: lambda e: (
                                e.activation(ctb[:, 0:512], pbk[4][:], AF.Copy, scale=EC(rrx, 9)),
                                e.activation(ctb[:, 512:1024], pbk[5][:], AF.Copy, scale=EC(rrx, 9)))[-1])(rr2),
                                [dpmm_ops[(gg_, rr2 % 2, 0)], dpmm_ops[(gg_, rr2 % 2, 1)], csc_ops[rr2], qc2_ops[rr2]] +
                                ([ag2w_ops[rr2 - 1]] if rr2 >= 1 else []))
                            ctev_ops[rr2] = ct_
                            aw = D("sync", (lambda rrx: lambda e: e.dma_start(out=ag2_in[rrx * 128:(rrx + 1) * 128, :], in_=ctb[:, 0:H]))(rr2), [ct_], dsem=f"a2w{rr2 % 2}")
                            ag2w_ops.append(aw)

        ag2_op = P.add("gpsimd", lambda e: e.collective_compute(
            "AllGather", OP.bypass, replica_groups=[list(range(NCORES))],
            ins=[ag2_in.rearrange("r c -> (r c)")], outs=[ag2_out.rearrange("r c -> (r c)")]), ag2w_ops, coll=True)

        fy_sc_ops = []
        so_ops_sd = []
        if milestone >= 4:
            adrain = [ag2w_ops[-1], ag2w_ops[-2]]
            t2_ops = []
            for n in range(NT):
                c1 = C("vector", (lambda nn: lambda e: e.tensor_reduce(mx16[:, 0:1], l2t[:, nn, :], axis=AX, op=OP.max))(n),
                       [rt_done[n]] + t2_ops[-1:])
                c2 = C("vector", (lambda nn: lambda e: e.tensor_scalar(wx16[:], ones8[:, 0:8], mx16[:, 0:1], None, OP.mult, OP.bypass))(n),
                       [c1, ones_op])
                c3 = C("vector", (lambda nn: lambda e: e.max_index(i1u16[:], wx16[:], l2t[:, nn, :]))(n), [c2])
                c4 = C("vector", (lambda nn: lambda e: e.tensor_copy(e1f16[:, nn:nn + 1], i1u16[:, 0:1]))(n), [c3])
                c5 = C("vector", (lambda nn: lambda e: e.tensor_scalar(msk16[:], iota8[:, 0:EF], e1f16[:, nn:nn + 1], None, OP.is_equal, OP.bypass))(n),
                       [c4, ld_io])
                c6 = C("vector", (lambda nn: lambda e: e.scalar_tensor_tensor(msk16[:], msk16[:], -1e30, l2t[:, nn, :], OP.mult, OP.add))(n), [c5])
                c7 = C("vector", (lambda nn: lambda e: e.tensor_reduce(mx16[:, 0:1], msk16[:], axis=AX, op=OP.max))(n), [c6])
                c8 = C("vector", (lambda nn: lambda e: e.tensor_scalar(wx16[:], ones8[:, 0:8], mx16[:, 0:1], None, OP.mult, OP.bypass))(n), [c7])
                c9 = C("vector", (lambda nn: lambda e: e.max_index(i1u16[:], wx16[:], msk16[:]))(n), [c8])
                c10 = C("vector", (lambda nn: lambda e: e.tensor_copy(e2f16[:, nn:nn + 1], i1u16[:, 0:1]))(n), [c9])
                t2_ops = [c10]
            top2_done = t2_ops[0]
            me_ops = []
            for e_ in range(EF):
                m1 = C("vector", (lambda ee: lambda e: e.tensor_scalar(me16[:, ee, :], e1f16[:], float(ee), None, OP.is_equal, OP.bypass))(e_),
                       [top2_done])
                m2 = C("vector", (lambda ee: lambda e: e.tensor_scalar(dumpa16[:], e2f16[:], float(ee), None, OP.is_equal, OP.bypass))(e_),
                       [top2_done] + me_ops[-1:])
                m3 = C("vector", (lambda ee: lambda e: e.tensor_tensor(me16[:, ee, :], me16[:, ee, :], dumpa16[:], OP.add))(e_), [m1, m2])
                me_ops.append(m3)
            r16_ops = []
            for e_ in range(EF):
                r = C("vector", (lambda ee: lambda e: e.tensor_tensor_scan(R16[:, ee, :], me16[:, ee, :], me16[:, ee, :], 0.0, OP.add, OP.bypass))(e_),
                      [me_ops[e_]])
                r16_ops.append(r)
            kt16 = C("vector", lambda e: e.tensor_copy(ktot16[:], R16[:, :, NT - 1]), r16_ops)
            ft1 = C("tensor", lambda e: e.transpose(ptr2[0:EF, 0:128], ktot16[:], eyef[:]), [kt16] + rtr_evs[-1:])
            fe1 = C("scalar", lambda e: e.copy(ktT[:], ptr2[0:EF, 0:128]), [ft1])
            fs1 = C("vector", lambda e: e.tensor_tensor_scan(kscT[:], ktT[:], ktT[:], 0.0, OP.add, OP.bypass), [fe1])
            fx1 = C("vector", lambda e: e.tensor_tensor(koffTx16[:], kscT[:], ktT[:], OP.subtract), [fs1])
            ktw = D("sync", lambda e: e.dma_start(out=kofT_d, in_=koffTx16[:]), [fx1], dsem="ktw!")
            ft2 = C("tensor", lambda e: e.transpose(ptr3[:, 0:EF], koffTx16[:], eyef[0:EF, 0:EF]), [fx1] + rtr_lev[-1:])
            fe2 = C("scalar", lambda e: e.copy(koff16[:], ptr3[:, 0:EF]), [ft2])
            cnt1 = C("vector", lambda e: e.tensor_copy(cntT16[:], kscT[:, 127:128]), [fs1])
            pca = C("vector", lambda e: e.tensor_scalar(pcT16[:], cntT16[:], 1.0 / 128.0, 0.49609375, OP.mult, OP.add), [cnt1])
            pcb = C("vector", lambda e: e.tensor_scalar(pcT16[:], pcT16[:], MAGIC, MAGIC, OP.add, OP.subtract), [pca])
            pcc = C("vector", lambda e: e.tensor_scalar(pcT16[:], pcT16[:], 128.0, None, OP.mult, OP.bypass), [pcb])
            ft3 = C("tensor", lambda e: e.transpose(ptr2[0:1, 0:EF], pcT16[:], eyef[0:EF, 0:EF]), [pcc, fe1])
            fe3 = C("scalar", lambda e: e.copy(pc1r[:], ptr2[0:1, 0:EF]), [ft3])
            fs3 = C("vector", lambda e: e.tensor_tensor_scan(sc1r[:], pc1r[:], pc1r[:], 0.0, OP.add, OP.bypass), [fe3])
            fo3 = C("vector", lambda e: e.tensor_tensor(off1r[:], sc1r[:], pc1r[:], OP.subtract), [fs3])
            ft4 = C("tensor", lambda e: e.transpose(ptr2[0:EF, 0:1], off1r[:], eyef[0:1, 0:1]), [fo3, fe3])
            fe4 = C("scalar", lambda e: e.copy(offT16[:], ptr2[0:EF, 0:1]), [ft4])
            ft5 = C("tensor", lambda e: e.transpose(ptr2[0:EF, 1:2], sc1r[:], eyef[0:1, 0:1]), [fs3, fe4])
            fe5 = C("scalar", lambda e: e.copy(endT16[:], ptr2[0:EF, 1:2]), [ft5])
            eh1 = C("vector", lambda e: e.tensor_scalar(erbhot[:], rb128s[:], offT16[0:EF, 0:1], None, OP.is_ge, OP.bypass), [fe4, ldf_ops[3]])
            eh2_ = C("vector", lambda e: e.tensor_scalar(eh2[:], rb128s[:], endT16[0:EF, 0:1], None, OP.is_lt, OP.bypass), [fe5, ldf_ops[3]])
            eh3 = C("vector", lambda e: e.tensor_tensor(erbhot[:], erbhot[:], eh2[:], OP.mult), [eh1, eh2_])
            mr1 = C("tensor", lambda e: e.matmul(ptr3[0:1, 0:48], iotac16[:], erbhot[:], start=True, stop=True), [eh3, ldf_ops[2], fe2])
            er1 = C("scalar", lambda e: e.copy(erow[:], ptr3[0:1, 0:48]), [mr1])
            mr2 = C("tensor", lambda e: e.matmul(ptr3[0:1, 64:112], cntT16[:], erbhot[:], start=True, stop=True), [eh3, cnt1, er1])
            er2 = C("scalar", lambda e: e.copy(cntrow[:], ptr3[0:1, 64:112]), [mr2])
            mr3 = C("tensor", lambda e: e.matmul(ptr2[0:1, 0:48], offT16[:], erbhot[:], start=True, stop=True), [eh3, fe4, fe5])
            er3 = C("scalar", lambda e: e.copy(offrow[:], ptr2[0:1, 0:48]), [mr3])
            mb1 = C("tensor", lambda e: e.matmul(ptr3[:, 0:48], onesrow[:], erow[:], start=True, stop=True), [er1, onesrow_op, er2])
            eb1 = C("scalar", lambda e: e.copy(erbB[:], ptr3[:, 0:48]), [mb1])
            mb2 = C("tensor", lambda e: e.matmul(ptr3[:, 64:112], onesrow[:], cntrow[:], start=True, stop=True), [er2, eb1])
            eb2 = C("scalar", lambda e: e.copy(cntB[:], ptr3[:, 64:112]), [mb2])
            mb3 = C("tensor", lambda e: e.matmul(ptr2[:, 0:48], onesrow[:], offrow[:], start=True, stop=True), [er3, eb2])
            eb3 = C("scalar", lambda e: e.copy(offB[:], ptr2[:, 0:48]), [mb3])
            rl1 = C("vector", lambda e: e.tensor_tensor(rloc48[:], rbps[:], offB[:], OP.subtract), [eb3, ldf_ops[4]])
            vl1 = C("vector", lambda e: e.tensor_tensor(validf[:], rloc48[:], cntB[:], OP.is_lt), [rl1, eb2])
            ft6 = C("tensor", lambda e: e.transpose(ptr2[0:48, 0:1], erow[:], eyef[0:1, 0:1]), [er1, eb3])
            fe6 = C("scalar", lambda e: e.copy(er48[:], ptr2[0:48, 0:1]), [ft6])
            ei1 = C("vector", lambda e: e.tensor_copy(erb48i[:], er48[:]), [fe6])
            ei2 = C("vector", lambda e: e.tensor_copy(erbi1[:], erow[:]), [er1])
            kg = P.add("gpsimd", lambda e: e.indirect_dma_start(
                koffg[:], None, kofT_d, IndirectOffsetOnAxis(erb48i[:], 0)), [ei1, ktw], dma=True, dsem="kg0")
            ft7 = C("tensor", lambda e: e.transpose(ptr2[0:48, 0:128], rloc48[:], eyef[:]), [rl1, fe6])
            fe7 = C("scalar", lambda e: e.copy(rlT[:], ptr2[0:48, 0:128]), [ft7])
            l1ops = []
            for p in range(128):
                op = C("vector", (lambda pp: lambda e: e.tensor_scalar(
                    dump48[:], koffg[:], rlT[:, pp:pp + 1], None, OP.is_le, OP.add,
                    accum_out=pzT48[:, pp:pp + 1]))(p), [kg, fe7] + l1ops[-1:])
                l1ops.append(op)
            ft8 = C("tensor", lambda e: e.transpose(ptr2[0:128, 0:48], pzT48[:], eyef[0:48, 0:48]), l1ops[-1:] + [fe7])
            fe8 = C("scalar", lambda e: e.copy(pz48[:], ptr2[0:128, 0:48]), [ft8])
            pzm = C("vector", lambda e: e.tensor_scalar(pz48[:], pz48[:], 1.0, None, OP.subtract, OP.bypass), [fe8])
            tb_ops = []
            for e_ in range(EF):
                ta = C("scalar", (lambda ee: lambda e: e.copy(tbs16[:, ee % 2, 0:1], koff16[:, ee:ee + 1]))(e_),
                       [fe2] + ([tb_ops[-2]] if len(tb_ops) >= 2 else []))
                tb = C("vector", (lambda ee: lambda e: e.tensor_copy(tbs16[:, ee % 2, 1:17], R16[:, ee, :]))(e_),
                       [r16_ops[e_], ta])
                tw = D("sync", (lambda ee: lambda e: e.dma_start(
                    out=tb16_d[ee * 128:(ee + 1) * 128, :], in_=tbs16[:, ee % 2, :]))(e_), [ta, tb], dsem=f"tb{e_ % 2}")
                tb_ops.append(tw)
            ix1 = C("vector", lambda e: e.scalar_tensor_tensor(idx48[:], erbB[:], 128.0, pz48[:], OP.mult, OP.add), [pzm, eb1])
            ix2 = C("vector", lambda e: e.tensor_copy(idx48i[:], idx48[:]), [ix1])
            fvops = [None] * 48
            g6ops = [None] * 48
            for rb in range(48):
                g6 = P.add("gpsimd", (lambda r: lambda e: e.indirect_dma_start(
                    g6b[:, r % 2, :], None, tb16_d, IndirectOffsetOnAxis(idx48i[:, r:r + 1], 0)))(rb),
                    [ix2, tb_ops[-1], tb_ops[-2]] + ([fvops[rb - 2]] if rb >= 2 else []), dma=True, dsem=f"g7{rb % 2}")
                g6ops[rb] = g6
                j1 = C("vector", (lambda r: lambda e: e.scalar_tensor_tensor(
                    j48[:, r:r + 1], g6b[:, r % 2, 0:1], -1.0, rloc48[:, r:r + 1], OP.mult, OP.add))(rb), [g6, rl1])
                f1 = C("vector", (lambda r: lambda e: e.tensor_scalar(
                    dump16f[:], g6b[:, r % 2, 1:17], j48[:, r:r + 1], None, OP.is_le, OP.add,
                    accum_out=fv48[:, r:r + 1]))(rb), [j1])
                fvops[rb] = f1
            tk1 = C("vector", lambda e: e.scalar_tensor_tensor(tok48[:], fv48[:], 128.0, pz48[:], OP.mult, OP.add), fvops[-2:] + [pzm] + fvops[:46])
            sc1_ = C("vector", lambda e: e.tensor_scalar(sct48[:], tok48[:], 2048.0, None, OP.subtract, OP.bypass), [tk1])
            sc2_ = C("vector", lambda e: e.tensor_tensor(sct48[:], sct48[:], validf[:], OP.mult), [sc1_, vl1])
            sc3_ = C("vector", lambda e: e.tensor_scalar(sct48[:], sct48[:], 2048.0, None, OP.add, OP.bypass), [sc2_])
            gi1_ = C("vector", lambda e: e.tensor_tensor(gidx48[:], tok48[:], validf[:], OP.mult), [tk1, vl1])
            gii = C("vector", lambda e: e.tensor_copy(gidx48i[:], gidx48[:]), [gi1_])
            sti = C("vector", lambda e: e.tensor_copy(scti[:], sct48[:]), [sc3_])
            wia = C("vector", lambda e: e.tensor_scalar(idx48[:], erbB[:], 128.0, None, OP.mult, OP.bypass), [eb1, ix2])
            wib = C("vector", lambda e: e.tensor_scalar(idx48[:], idx48[:], iota128[:, 0:1], None, OP.add, OP.bypass), [wia, ld_i128])
            wic = C("vector", lambda e: e.tensor_copy(w1i48[:], idx48[:]), [wib])
            w2a = C("vector", lambda e: e.tensor_scalar(idx48[0:65, :], erbB[0:65, :], 65.0, None, OP.mult, OP.bypass), [wic, eb1])
            w2b_ = C("vector", lambda e: e.tensor_scalar(idx48[0:65, :], idx48[0:65, :], iota128[0:65, 0:1], None, OP.add, OP.bypass), [w2a])
            w2c = C("vector", lambda e: e.tensor_copy(w2i48[:], idx48[0:65, :]), [w2b_])
            ebc = C("vector", lambda e: e.tensor_copy(ebi128[:], erbB[:]), [eb1])

            hro = C("vector", lambda e: e.memset(hrT[64:65, :, :], 1.0))
            zy = C("vector", lambda e: e.memset(ybuf[:, 0, :], 0.0), adrain)
            yz_ops = []
            for r in range(16):
                yz_ops.append(D("gpsimd", (lambda rr: lambda e: e.dma_start(
                    out=y_d[rr * 128:(rr + 1) * 128, :], in_=ybuf[:, 0, :]))(r), [zy], dsem="yz!"))
            yz_ops.append(D("gpsimd", lambda e: e.dma_start(out=y_d[2048:2049, :], in_=ybuf[0:1, 0, :]), [zy], dsem="yz!"))
            yzl = yz_ops[-1]

            fgx_ops = [None] * 48
            ftrx_ops = {}
            fevx_ops = {}
            fhmm_ops = {}
            fhmml = [None] * 48
            frl_ops = [None] * 48
            fba_ops = [None] * 48
            ftrh_ops = [None] * 48
            fevh_ops = [None] * 48
            fymm_ops = {}
            fyev_ops = {}
            fsc_ops = [None] * 48
            fwl_ops = [None] * 48
            fvl_ops = [None] * 48
            fbl_ops = [None] * 48
            for rb in range(48):
                par = rb % 2
                gx = P.add("gpsimd", (lambda r: lambda e: e.indirect_dma_start(
                    xg[:, r % 2, :], None, ag1a_in, IndirectOffsetOnAxis(gidx48i[:, r:r + 1], 0)))(rb),
                    [gii] + (adrain if rb < 2 else [ftrx_ops[(rb - 2, 7)]]), dma=True, dsem=f"fx{par}")
                fgx_ops[rb] = gx
                wl = P.add("gpsimd", (lambda r: lambda e: e.indirect_dma_start(
                    w1b[:, r % 2, :], None, w1s_d.rearrange("e p n -> (e p) n"),
                    IndirectOffsetOnAxis(w1i48[:, r:r + 1], 0)))(rb),
                    [wic] + (adrain if rb < 2 else [fhmml[rb - 2]]), dma=True, dsem=f"fw{par}")
                fwl_ops[rb] = wl
                vl = P.add("gpsimd", (lambda r: lambda e: e.indirect_dma_start(
                    w2b[0:65, r % 2, :], None, w2s_d.rearrange("e p n -> (e p) n"),
                    IndirectOffsetOnAxis(w2i48[:, r:r + 1], 0)))(rb),
                    [w2c] + (adrain if rb < 2 else [fymm_ops[(rb - 2, 1)]]), dma=True, dsem=f"fv{par}")
                fvl_ops[rb] = vl
                gb1 = P.add("gpsimd", (lambda r: lambda e: e.indirect_dma_start(
                    b1B[:, r % 2, :], None, b1f_d, IndirectOffsetOnAxis(ebi128[:, r:r + 1], 0)))(rb),
                    [ebc] + ([] if rb < 2 else [fba_ops[rb - 2]]), dma=True, dsem=f"fb{par}")
                for hc in range(8):
                    bnk = pbk[2] if hc % 2 == 0 else pbk[3]
                    tr = C("tensor", (lambda r, cc, bb: lambda e: e.transpose(
                        bb[:, 0:128], xg[:, r % 2, cc * 128:(cc + 1) * 128], eyef[:]))(rb, hc, bnk),
                        [gx] + ([fevx_ops[(rb, hc - 2)]] if hc >= 2 else
                                ([fevx_ops[(rb - 1, hc + 6)]] if rb >= 1 else [fe8, ctev_ops[15]])))
                    eng_ev = "vector" if hc % 2 == 0 else "scalar"
                    if eng_ev == "vector":
                        ev = C("vector", (lambda r, cc, bb: lambda e: e.tensor_copy(xgT[:, r % 2, cc, :], bb[:, 0:128]))(rb, hc, bnk),
                               [tr] + ([fhmm_ops[(rb - 2, hc)]] if rb >= 2 else []))
                    else:
                        ev = C("scalar", (lambda r, cc, bb: lambda e: e.copy(xgT[:, r % 2, cc, :], bb[:, 0:128]))(rb, hc, bnk),
                               [tr] + ([fhmm_ops[(rb - 2, hc)]] if rb >= 2 else []))
                    ftrx_ops[(rb, hc)] = tr
                    fevx_ops[(rb, hc)] = ev
                for hc in range(8):
                    mm = C("tensor", (lambda r, cc: lambda e: e.matmul(
                        pbk[0][:, 0:64], xgT[:, r % 2, cc, :], w1b[:, r % 2, cc * 64:(cc + 1) * 64],
                        start=(cc == 0), stop=(cc == 7)))(rb, hc),
                        [fevx_ops[(rb, hc)], wl] + ([fba_ops[rb - 1]] if (hc == 0 and rb >= 1) else []))
                    fhmm_ops[(rb, hc)] = mm
                fhmml[rb] = fhmm_ops[(rb, 7)]
                ba = C("vector", (lambda r: lambda e: e.tensor_tensor(hr16[:, r % 2, :], pbk[0][:, 0:64], b1B[:, r % 2, :], OP.add))(rb),
                       [fhmml[rb], gb1] + ([ftrh_ops[rb - 2]] if rb >= 2 else []))
                fba_ops[rb] = ba
                rl = C("scalar", (lambda r: lambda e: e.activation(hr16[:, r % 2, :], hr16[:, r % 2, :], AF.Relu))(rb), [ba])
                frl_ops[rb] = rl
                ftrh = C("tensor", (lambda r: lambda e: e.transpose(
                    ptr3[0:64, 0:128], hr16[:, r % 2, :], eyef[:]))(rb),
                    [rl] + ([fevh_ops[rb - 1]] if rb >= 1 else []))
                ftrh_ops[rb] = ftrh
                evh = C("scalar", (lambda r: lambda e: e.copy(hrT[0:64, r % 2, :], ptr3[0:64, 0:128]))(rb),
                       [ftrh] + ([fymm_ops[(rb - 2, 1)]] if rb >= 2 else []))
                fevh_ops[rb] = evh
                y0 = C("tensor", (lambda r: lambda e: e.matmul(
                    pbk[4][:], hrT[0:65, r % 2, :], w2b[0:65, r % 2, 0:512], start=True, stop=True))(rb),
                    [evh, vl, hro] + ([fyev_ops[(rb - 1, 0)]] if rb >= 1 else []))
                y1 = C("tensor", (lambda r: lambda e: e.matmul(
                    pbk[5][:], hrT[0:65, r % 2, :], w2b[0:65, r % 2, 512:1024], start=True, stop=True))(rb),
                    [evh, vl, hro] + ([fyev_ops[(rb - 1, 1)]] if rb >= 1 else []))
                fymm_ops[(rb, 0)] = y0
                fymm_ops[(rb, 1)] = y1
                ye0 = C("scalar", (lambda r: lambda e: e.copy(ybuf[:, r % 2, 0:512], pbk[4][:]))(rb),
                        [y0] + ([fsc_ops[rb - 2]] if rb >= 2 else [yzl]))
                ye1 = C("scalar", (lambda r: lambda e: e.copy(ybuf[:, r % 2, 512:1024], pbk[5][:]))(rb),
                        [y1] + ([fsc_ops[rb - 2]] if rb >= 2 else [yzl]))
                fyev_ops[(rb, 0)] = ye0
                fyev_ops[(rb, 1)] = ye1
                sc = P.add("gpsimd", (lambda r: lambda e: e.indirect_dma_start(
                    y_d, IndirectOffsetOnAxis(scti[:, r:r + 1], 0), ybuf[:, r % 2, :], None,
                    compute_op=OP.add))(rb),
                    [ye0, ye1, sti, yz_ops[-1]], dma=True, dsem=f"fy{par}")
                fsc_ops[rb] = sc
            fy_sc_ops = fsc_ops

        gidx2 = gidx_d.rearrange("j (r x) -> j r x", x=TC)
        gwin_ops = []
        for j in range(4):
            def mk_gw(jj):
                def f(e):
                    rv = nc.partition_id(engines=[SPENG])
                    return e.dma_start(out=gidxw[:, :, jj], in_=gidx2[jj, rv].rearrange("(i p) -> p i", p=128),
                                      allow_slow_non_contiguous=True)
                return f
            gwin_ops.append(D("sync", mk_gw(j), gidx_wr, dsem="gwin!"))
        gi0 = C("vector", lambda e: e.tensor_copy(gidxi[:, 0, :], gidxw[:, :, 0]), gwin_ops)
        gi1 = C("vector", lambda e: e.tensor_copy(gidxi[:, 1, :], gidxw[:, :, 1]), gwin_ops)
        sout_ops = []
        tapr_ops = [[] for _ in range(NT)]
        for i in range(NT):
            pace_c = ([sout_ops[i - 2]] + tapr_ops[i - 2]) if i >= 2 else []
            cg1 = P.add("gpsimd", (lambda ii: lambda e: e.indirect_dma_start(
                r1g[:, ii % 2, :], None, ag2_out, IndirectOffsetOnAxis(gidxi[:, 0, ii:ii + 1], 0)))(i),
                [ag2_op, gi0] + pace_c, dma=True, dsem=f"cg{i % 2}")
            cg2 = P.add("gpsimd", (lambda ii: lambda e: e.indirect_dma_start(
                r2g[:, ii % 2, :], None, ag2_out, IndirectOffsetOnAxis(gidxi[:, 1, ii:ii + 1], 0)))(i),
                [ag2_op, gi1] + pace_c, dma=True, dsem=f"ch{i % 2}")
            if "tap_r1" in tapd:
                tr1 = D("gpsimd", (lambda ii: lambda e: e.dma_start(out=tapd["tap_r1"][ii * 128:(ii + 1) * 128, :], in_=r1g[:, ii % 2, :]))(i), [cg1], dsem=f"tr{i % 2}")
                tr2 = D("gpsimd", (lambda ii: lambda e: e.dma_start(out=tapd["tap_r2"][ii * 128:(ii + 1) * 128, :], in_=r2g[:, ii % 2, :]))(i), [cg2], dsem=f"ts{i % 2}")
                tapr_ops[i] = [tr1, tr2]
                finals += [tr1, tr2]
            s1_ = C("vector", (lambda ii: lambda e: e.tensor_scalar(stile[:, ii % 2, :], r2g[:, ii % 2, :], gidxw[:, ii:ii + 1, 3], None, OP.mult, OP.bypass))(i),
                    [cg2, gwin_ops[3]] + pace_c)
            s2_ = C("vector", (lambda ii: lambda e: e.scalar_tensor_tensor(stile[:, ii % 2, :], r1g[:, ii % 2, :], gidxw[:, ii:ii + 1, 2], stile[:, ii % 2, :], OP.mult, OP.add))(i),
                    [cg1, s1_, gwin_ops[2]])
            s_ops.append(s2_)
            if milestone >= 3:
                s_target = out_d if milestone == 3 else s_d
                so = D("gpsimd", (lambda ii, tgt: lambda e: e.dma_start(out=tgt.rearrange("(n p) h -> p n h", p=128)[:, ii, :], in_=stile[:, ii % 2, :]))(i, s_target),
                       [s2_], dsem=f"so{i % 2}")
                if "tap_s" in tapd:
                    so2 = D("gpsimd", (lambda ii: lambda e: e.dma_start(out=tapd["tap_s"].rearrange("(n p) h -> p n h", p=128)[:, ii, :], in_=stile[:, ii % 2, :]))(i),
                            [s2_, so], dsem=f"so{i % 2}")
                    sout_ops.append(so2)
                    finals.append(so2)
                else:
                    sout_ops.append(so)
                if milestone == 3:
                    finals.append(so)
                else:
                    finals.append(so)

    if milestone >= 4:
        f4_out = []
        f4_chain = [None] * NT
        for i in range(NT):
            par = i % 2
            pace4 = [f4_chain[i - 2]] if i >= 2 else [sout_ops[-1], sout_ops[-2]]
            lx = D("sync", (lambda ii: lambda e: e.dma_start(out=r1g[:, ii % 2, :], in_=x_sl[ii * 128:(ii + 1) * 128, :]))(i),
                   pace4, dsem=f"c4x{par}")
            ll = D("sync", (lambda ii: lambda e: e.dma_start(out=lnr[:, ii % 2, :], in_=ag1a_in[ii * 128:(ii + 1) * 128, :]))(i),
                   pace4, dsem=f"c4l{par}")
            ly = D("sync", (lambda ii: lambda e: e.dma_start(out=r2g[:, ii % 2, :], in_=y_d[ii * 128:(ii + 1) * 128, :]))(i),
                   [fy_sc_ops[-1], fy_sc_ops[-2]] + pace4, dsem=f"c4y{par}")
            ls = D("sync", (lambda ii: lambda e: e.dma_start(out=stile[:, ii % 2, :], in_=s_d[ii * 128:(ii + 1) * 128, :]))(i),
                   [sout_ops[ii_] for ii_ in (len(sout_ops) - 1, len(sout_ops) - 2)] + pace4, dsem=f"c4s{par}")
            v1 = C("vector", (lambda ii: lambda e: e.tensor_tensor(lnr[:, ii % 2, :], lnr[:, ii % 2, :], r2g[:, ii % 2, :], OP.add))(i),
                   [ll, ly])
            v2 = C("scalar", (lambda ii: lambda e: e.activation(lnr[:, ii % 2, :], lnr[:, ii % 2, :], AF.Silu))(i), [v1])
            v3 = C("vector", (lambda ii: lambda e: e.tensor_tensor(lnr[:, ii % 2, :], lnr[:, ii % 2, :], stile[:, ii % 2, :], OP.mult))(i),
                   [v2, ls])
            v4 = C("vector", (lambda ii: lambda e: e.tensor_tensor(lnr[:, ii % 2, :], lnr[:, ii % 2, :], r1g[:, ii % 2, :], OP.add))(i),
                   [v3, lx])
            o4 = D("gpsimd", (lambda ii: lambda e: e.dma_start(out=out_d[ii * 128:(ii + 1) * 128, :], in_=lnr[:, ii % 2, :]))(i),
                   [v4], dsem=f"o4{par}")
            f4_chain[i] = o4
            f4_out.append(o4)
            finals.append(o4)

    if "tap_ln" in tapd:
        for t in range(NT):
            finals.append(D("gpsimd", (lambda tt: lambda e: e.dma_start(out=tapd["tap_ln"].rearrange("(n p) h -> p n h", p=128)[:, tt, :], in_=ln[:, tt, :]))(t),
                            [ln_ops[t]] + ssq1, dsem="st!"))
    if "tap_logits" in tapd:
        finals.append(D("gpsimd", lambda e: e.dma_start(out=tapd["tap_logits"].rearrange("(n p) k -> p n k", p=128), in_=logit[:]), lg_ops + nm_ops + ex_ops, dsem="st!"))
    if "tap_wsel" in tapd:
        finals.append(D("gpsimd", lambda e: e.dma_start(out=tapd["tap_wsel"].rearrange("(n p) k -> p n k", p=128), in_=wsel[:]), wsel_ops, dsem="st!"))
    if "tap_top" in tapd:
        for arridx, arr in enumerate([e1t, e2t, w1t, w2t]):
            finals.append(D("gpsimd", (lambda a, j: lambda e: e.dma_start(
                out=tapd["tap_top"].rearrange("(n p) k -> p n k", p=128)[:, :, j:j + 1], in_=a[:].rearrange("p (n o) -> p n o", o=1), allow_slow_non_contiguous=True))(arr, arridx),
                wsel_ops, dsem="st!"))

    if "tap_ag1" in tapd and milestone >= 3:
        finals.append(D("gpsimd", lambda e: e.dma_start(out=tapd["tap_ag1"], in_=ag1a_out),
                        toki_ops + ag_dep, dsem="ta!"))
    if "tap_ag2" in tapd and milestone >= 3:
        finals.append(D("gpsimd", lambda e: e.dma_start(out=tapd["tap_ag2"], in_=ag2_in), ag2w_ops[-2:], dsem="st!"))
    if "tap_q1" in tapd and milestone >= 3:
        finals.append(D("gpsimd", lambda e: e.dma_start(out=tapd["tap_q1"], in_=q1s_d),
                        [q1w_ops[3], q1w_ops[7], q1w_ops[11], q1w_ops[15]], dsem="st!"))
    if "tap_q2" in tapd and milestone >= 3:
        finals.append(D("gpsimd", lambda e: e.dma_start(out=tapd["tap_q2"], in_=q2s_d),
                        [q2w_ops[g * 2 + 1] for g in range(8)], dsem="st!"))
    if "tap_ag2o" in tapd and milestone >= 3:
        finals.append(D("gpsimd", lambda e: e.dma_start(out=tapd["tap_ag2o"], in_=ag2_out), [ag2_op], dsem="to!"))
    if "tap_csc" in tapd and milestone >= 3:
        finals.append(D("gpsimd", lambda e: e.dma_start(out=tapd["tap_csc"], in_=csc_dbg[:]), [csc_ops[15]], dsem="st!"))
    if "tap_fl2" in tapd and milestone >= 4:
        finals.append(D("gpsimd", lambda e: e.dma_start(out=tapd["tap_fl2"].rearrange("(n p) k -> p n k", p=128), in_=l2t[:]), rt_done, dsem="st!"))
    if "tap_fslot" in tapd and milestone >= 4:
        for nm, buf, dep in [(0, tok48, None), (1, validf, None), (2, sct48, None), (3, pz48, None)]:
            finals.append(D("gpsimd", (lambda nn, bb: lambda e: e.dma_start(out=tapd["tap_fslot"][:, nn * 48:(nn + 1) * 48], in_=bb[:]))(nm, buf),
                            [sti, gii], dsem="st!"))
    if "tap_y" in tapd and milestone >= 4:
        finals.append(D("gpsimd", lambda e: e.dma_start(out=tapd["tap_y"], in_=y_d), [fy_sc_ops[-1], fy_sc_ops[-2]], dsem="st!"))
    if "tap_gw" in tapd and milestone >= 3:
        finals.append(D("gpsimd", lambda e: e.dma_start(out=tapd["tap_gw"], in_=gidxw[:].rearrange("p i j -> p (i j)")), gwin_ops, dsem="st!"))
    if "tap_tok" in tapd and milestone >= 2:
        finals.append(D("gpsimd", lambda e: e.dma_start(out=tapd["tap_tok"], in_=tokf[:]), toki_ops, dsem="st!"))
    if "tap_gidx" in tapd and milestone >= 2:
        finals.append(D("gpsimd", lambda e: e.dma_start(out=tapd["tap_gidx"], in_=gidx_d), gidx_wr, dsem="st!"))
    if milestone in (1, 2):
        for t in range(NT):
            finals.append(D("gpsimd", (lambda tt: lambda e: e.dma_start(out=out_d.rearrange("(n p) h -> p n h", p=128)[:, tt, :], in_=ln[:, tt, :]))(t),
                            ([ln_ops[t]] if len(ln_ops) > t else []) + ssq1 + wsel_ops + routing_done, dsem="st!"))

    P.emit(nc, None, finals)
    es.close()
    return nc


def make_inputs(inputs):
    x = np.asarray(inputs["L0_attn_x"], np.float32).reshape(T, H)
    gnw = np.asarray(inputs["L0_attn_gate_norm_w"], np.float32).reshape(1, H)
    ginw = np.asarray(inputs["L0_attn_gate_in_norm_w"], np.float32).reshape(1, H)
    gwT = np.ascontiguousarray(np.asarray(inputs["L0_attn_gate_w"], np.float32).T)
    eyebf = np.eye(128, dtype=np.float32).astype(ml_dtypes.bfloat16)
    eyef = np.eye(128, dtype=np.float32)
    iota8 = np.broadcast_to(np.arange(EF, dtype=np.float32), (128, EF)).copy()
    W1 = np.asarray(inputs["L0_ffn_W1"], np.float32)
    W2 = np.asarray(inputs["L0_ffn_W2"], np.float32)
    b1 = np.asarray(inputs["L0_ffn_b1"], np.float32)
    b2 = np.asarray(inputs["L0_ffn_b2"], np.float32)
    Wr = np.asarray(inputs["L0_ffn_Wr"], np.float32)
    br = np.asarray(inputs["L0_ffn_br"], np.float32)
    w1s = np.ascontiguousarray(W1.reshape(EF, 8, 128, 64).transpose(0, 2, 1, 3).reshape(EF, 128, 512))
    w2s = np.ascontiguousarray(np.concatenate([W2, b2[:, None, :]], axis=1))
    wrs = np.ascontiguousarray(Wr.reshape(8, 128, EF).transpose(1, 0, 2))
    brr = np.broadcast_to(br.reshape(1, EF), (128, EF)).copy()
    iotac16 = np.arange(EF, dtype=np.float32).reshape(EF, 1)
    rb128 = np.broadcast_to((128.0 * np.arange(48, dtype=np.float32)).reshape(1, 48), (EF, 48)).copy()
    rbp = (np.arange(48, dtype=np.float32) * 128.0)[None, :] + np.arange(128, dtype=np.float32)[:, None]
    rbp = np.ascontiguousarray(rbp)
    maps = []
    for c in range(NCORES):
        maps.append({
            "x_sl": np.ascontiguousarray(x[c * TC:(c + 1) * TC]),
            "gnw": np.broadcast_to(gnw, (128, K_H)).copy() if False else np.broadcast_to(gnw, (128, H)).copy(), "ginw": np.broadcast_to(ginw, (128, H)).copy(), "gwT": gwT,
            "eyebf": eyebf, "eyef": eyef, "iota8": iota8,
            "iota128": np.arange(128, dtype=np.float32).reshape(128, 1),
            "gpn": np.broadcast_to(np.asarray(inputs["L0_attn_gp_norm_w"], np.float32)[c].reshape(1, H), (128, H)).copy(),
            "dpn": np.broadcast_to(np.asarray(inputs["L0_attn_dp_norm_w"], np.float32)[c].reshape(1, II), (128, II)).copy(),
            "gpwT": np.ascontiguousarray(np.asarray(inputs["L0_attn_gp_w"], np.float32)[c].T),
            "dpwT": np.ascontiguousarray(np.asarray(inputs["L0_attn_dp_w"], np.float32)[c].T),
            "myhot": np.eye(NCORES, dtype=np.float32)[c].reshape(8, 1),
            "myhotr": np.eye(NCORES, dtype=np.float32)[c].reshape(1, 8),
            "w1s": w1s, "w2s": w2s, "b1f": np.ascontiguousarray(b1), "wrs": wrs, "brr": brr,
            "iotac16": iotac16, "rb128": rb128, "rbp": rbp,
        })
    return maps


def kernel(**inputs):
    nc = build_program()
    maps = make_inputs(inputs)
    res = run_bass_kernel_spmd(nc, maps, core_ids=list(range(NCORES)))
    out = np.concatenate([res.results[c]["out"] for c in range(NCORES)], axis=0)
    return out.reshape(4, 4096, 1024)
